# version 01
# baseline (speedup 1.0000x reference)
from contextlib import ExitStack
import math
import numpy as np
import ml_dtypes
import concourse.bass as bass
import concourse.mybir as mybir
from concourse.bass_utils import run_bass_kernel_spmd

F32 = mybir.dt.float32
BF16 = mybir.dt.bfloat16
AF = mybir.ActivationFunctionType
ALU = mybir.AluOpType
AX = mybir.AxisListType.X

NCORES = 8
T = 1024
NT = 8
D = 2048
ND = 16
HD = 128
SEQ = 8192
NKC = 64
INTER = 5632
NIC = 44
NE, RK = 8, 64
ISQ = 1.0 / math.sqrt(128.0)


class Ctr:
    def __init__(self, ctx, nc, name):
        self.sem = ctx.enter_context(nc.semaphore(name))
        self.v = 0

    def inc(self, inst, k=1):
        inst.then_inc(self.sem, k)
        self.v += k
        return self.v


def build_program(stop_after=None, tap_x=()):
    nc = bass.Bass("TRN2", target_bir_lowering=False)
    din = {}

    def dI(name, shape, dtype=F32):
        din[name] = nc.dram_tensor(name, shape, dtype, kind="ExternalInput").ap()
        return din[name]

    x_in = dI("x", [T, D])
    wq0 = dI("wq0", [D, 16 * HD])
    wk0 = dI("wk0", [D, 4 * HD])
    wv0 = dI("wv0", [D, 4 * HD])
    wo0 = dI("wo0", [16 * HD, D])
    gwT = dI("gwT", [D, INTER])
    uwT = dI("uwT", [D, INTER])
    dwT = dI("dwT", [INTER, D])
    gs_c = dI("gs_c", [128, NIC])
    us_c = dI("us_c", [128, NIC])
    ds_row = dI("ds_row", [1, D])
    wq1 = dI("wq1", [D, D])
    wk1 = dI("wk1", [D, D])
    wv1 = dI("wv1", [D, D])
    wo1 = dI("wo1", [D, D])
    a_T = dI("a_T", [D, NE * RK])
    b_sT = dI("b_sT", [NE * RK, D])
    wr_n = dI("wr_n", [NE, D])
    br_r = dI("br_r", [1, NE])
    ident_in = dI("ident", [128, 128])
    shift_in = dI("shiftm", [128, 128])
    cosf_in = dI("cosf", [128, T])
    sinf_in = dI("sinf", [128, T])
    gq_in = dI("gqt", [128, T])
    qzb_in = dI("qzb", [128, T])
    gkc_in = dI("gkc", [128, NKC])

    y_out = nc.dram_tensor("y", [T, D], F32, kind="ExternalOutput").ap()
    taps = {}
    for tname in tap_x:
        taps[tname] = nc.dram_tensor("tap_" + tname, [T, D], F32,
                                     kind="ExternalOutput").ap()

    kin = [nc.dram_tensor("kin0", [128, 4 * T], BF16),
           nc.dram_tensor("kin1", [128, 16 * T], BF16)]
    vin = [nc.dram_tensor("vin0", [T, 4 * HD], BF16),
           nc.dram_tensor("vin1", [T, 16 * HD], BF16)]
    kg = [nc.dram_tensor("kg0", [NCORES * 128, 4 * T], BF16),
          nc.dram_tensor("kg1", [NCORES * 128, 16 * T], BF16)]
    vg = [nc.dram_tensor("vg0", [SEQ, 4 * HD], BF16),
          nc.dram_tensor("vg1", [SEQ, 16 * HD], BF16)]
    kml = [nc.dram_tensor("kml0", [1, 4], F32),
           nc.dram_tensor("kml1", [1, 16], F32)]
    kmg = [nc.dram_tensor("kmg0", [1, 4], F32),
           nc.dram_tensor("kmg1", [1, 16], F32)]

    ctx = ExitStack()
    with ctx:
        E = ctx.enter_context

        X = E(nc.sbuf_tensor("X", [128, NT * D], F32))
        xnT = E(nc.sbuf_tensor("xnT", [128, ND * T], BF16))
        ident = E(nc.sbuf_tensor("ident_s", [128, 128], F32))
        ones_col_bf = E(nc.sbuf_tensor("ones_col_bf", [128, 1], BF16))
        ones_row_f = E(nc.sbuf_tensor("ones_row_f", [1, 128], F32))
        eps_col = E(nc.sbuf_tensor("eps_col", [128, 1], F32))
        negc = [E(nc.sbuf_tensor("negc0", [128, 16], F32)),
                E(nc.sbuf_tensor("negc1", [128, 16], F32))]
        nrow = E(nc.sbuf_tensor("nrow", [1, D], F32))
        scr = [E(nc.sbuf_tensor(f"scr{i}", [128, D], F32)) for i in range(3)]
        psA = E(nc.psum_tensor("psA", [128, 1024], F32))
        psB = E(nc.psum_tensor("psB", [128, 1024], F32))
        psC = E(nc.psum_tensor("psC", [128, 1024], F32))
        psD = E(nc.psum_tensor("psD", [128, 1024], F32))
        lay = {}
        _anon = [0]

        import os
        import concourse.bass_primitives as _bp

        def memdbg(tag):
            if os.environ.get("KMEMDBG"):
                print("MEM", tag, nc.bytes_allocated(_bp.MemorySpace.SBUF))

        def sb(shape, dtype):
            _anon[0] += 1
            return nc.sbuf_tensor(f"tt{_anon[0]}", shape, dtype)

        s_ld = Ctr(ctx, nc, "s_ld")
        s_st = Ctr(ctx, nc, "s_st")
        s_ldp = Ctr(ctx, nc, "s_ldp")
        s_cc = [Ctr(ctx, nc, f"s_cc{i}") for i in range(6)]
        cA = Ctr(ctx, nc, "cA")
        cB = Ctr(ctx, nc, "cB")
        cC = Ctr(ctx, nc, "cC")
        cD = Ctr(ctx, nc, "cD")
        cE = Ctr(ctx, nc, "cE")
        cF = Ctr(ctx, nc, "cF")
        cG = Ctr(ctx, nc, "cG")
        cH = Ctr(ctx, nc, "cH")
        cI = Ctr(ctx, nc, "cI")
        cJ = Ctr(ctx, nc, "cJ")
        s_kv = Ctr(ctx, nc, "s_kv")
        s_ss = Ctr(ctx, nc, "s_ss")
        s_se = Ctr(ctx, nc, "s_se")
        s_sm = Ctr(ctx, nc, "s_sm")
        s_sa = Ctr(ctx, nc, "s_sa")
        s_hd1 = Ctr(ctx, nc, "s_hd1")
        s_hd2 = Ctr(ctx, nc, "s_hd2")
        s_hd3 = Ctr(ctx, nc, "s_hd3")
        s_hpe = Ctr(ctx, nc, "s_hpe")

        with nc.Block("init") as blk:
            @blk.sync
            def _(sync):
                s_ld.inc(sync.dma_start(
                    out=X[:, :].rearrange("p (t d) -> p t d", d=D),
                    in_=x_in.rearrange("(t p) d -> p t d", p=128)), 16)
                s_ld.inc(sync.dma_start(out=ident[:, :], in_=ident_in), 16)

            @blk.vector
            def _(vector):
                vector.memset(ones_col_bf[:, :], 1.0)
                vector.memset(eps_col[:, :], 1e-5)
                vector.memset(ones_row_f[:, :], 1.0)

        def phase_ln(tag):
            with ExitStack() as pctx, nc.Block("ln_" + tag) as blk:
                P = pctx.enter_context
                nst = P(sb([128, 6 * NT], F32))
                sqd = P(sb([128, D], BF16))

                def sl(j, k):
                    return nst[:, 6 * j + k:6 * j + k + 1]

                Ba = cA.v
                Ba2 = cB.v
                Bd = cC.v
                Bd2 = cD.v
                Bp = cE.v
                Bev = cF.v

                @blk.scalar
                def _(scalar):
                    scalar.wait_ge(s_ld.sem, 32)
                    for j in range(NT):
                        cA.inc(scalar.activation(
                            sqd[:, :], X[:, j * D:(j + 1) * D], AF.Square,
                            accum_out=sl(j, 1)))
                    for j in range(NT):
                        scalar.wait_ge(cC.sem, Bd + j + 1)
                        cB.inc(scalar.activation(
                            sl(j, 4), sl(j, 3), AF.Sqrt, bias=eps_col[:, :]))

                @blk.vector
                def _(vector):
                    vector.wait_ge(s_ld.sem, 32)
                    for j in range(NT):
                        xj = X[:, j * D:(j + 1) * D]
                        h = cH.inc(vector.reduce_sum(sl(j, 0), xj, axis=AX,
                                                     negate=True))
                        vector.wait_ge(cH.sem, h)
                        h = cH.inc(vector.tensor_scalar(
                            sl(j, 0), sl(j, 0), 1.0 / D, None, op0=ALU.mult))
                        vector.wait_ge(cH.sem, h)
                        h = cH.inc(vector.tensor_tensor(
                            sl(j, 2), sl(j, 0), sl(j, 0), ALU.mult))
                        vector.wait_ge(cH.sem, h)
                        vector.wait_ge(cA.sem, Ba + j + 1)
                        cC.inc(vector.scalar_tensor_tensor(
                            sl(j, 3), sl(j, 1), 1.0 / D, sl(j, 2),
                            op0=ALU.mult, op1=ALU.subtract))
                    xv = xnT[:, :].rearrange("p (d t) -> p d t", t=T)

                    def evac(j):
                        vector.wait_ge(cE.sem, Bp + 2 * j + 1)
                        cF.inc(vector.tensor_copy(
                            xv[:, 0:8, j * 128:(j + 1) * 128],
                            psC[:, :].rearrange("p (a b) -> p a b", b=128)))
                        vector.wait_ge(cE.sem, Bp + 2 * j + 2)
                        cF.inc(vector.tensor_copy(
                            xv[:, 8:16, j * 128:(j + 1) * 128],
                            psD[:, :].rearrange("p (a b) -> p a b", b=128)))

                    for j in range(NT):
                        xj = X[:, j * D:(j + 1) * D]
                        vector.wait_ge(cB.sem, Ba2 + j + 1)
                        h = cH.inc(vector.reciprocal(sl(j, 5), sl(j, 4)))
                        vector.wait_ge(cH.sem, h)
                        if j >= 2:
                            vector.wait_ge(cE.sem, Bp + 2 * (j - 2) + 2)
                        cD.inc(vector.tensor_scalar(
                            scr[j % 2][:, :], xj, sl(j, 0), sl(j, 5),
                            op0=ALU.add, op1=ALU.mult))
                        if j >= 1:
                            evac(j - 1)
                    evac(NT - 1)

                @blk.tensor
                def _(tensor):
                    for j in range(NT):
                        tensor.wait_ge(cD.sem, Bd2 + j + 1)
                        if j >= 1:
                            tensor.wait_ge(cF.sem, Bev + 2 * j)
                        for d in range(8):
                            mm = tensor.transpose(
                                psC[:, d * 128:(d + 1) * 128],
                                scr[j % 2][:, d * 128:(d + 1) * 128],
                                ident[:, :])
                        cE.inc(mm)
                        for d in range(8, 16):
                            mm = tensor.transpose(
                                psD[:, (d - 8) * 128:(d - 7) * 128],
                                scr[j % 2][:, d * 128:(d + 1) * 128],
                                ident[:, :])
                        cE.inc(mm)

        def phase_qk(l, wq, wk, HKV, rope, qz):
            QF = 16
            KF = HKV
            NP = QF + KF
            memdbg(f"qk_l{l}")
            with ExitStack() as pctx, nc.Block(f"qk_l{l}") as blk:
                P = pctx.enter_context
                wstg = [P(sb([128, 16, 128], F32))
                        for _ in range(2)]
                wbf = [P(sb([128, D], BF16)) for _ in range(2)]
                kts = [P(sb([128, T], BF16)) for _ in range(2)]
                sqb = P(sb([128, T], BF16))
                qf32 = [scr[0][:, 0:1024], scr[1][:, 0:1024]]
                ps0 = [psA, psC]
                ps1 = [psB, psD]
                consts = []
                if rope:
                    consts = [(cosf_in, lay["cosf"]), (sinf_in, lay["sinf"]),
                              (shift_in, lay["shiftm"])]
                    cosf, sinf, shiftm = (lay["cosf"], lay["sinf"],
                                          lay["shiftm"])
                if l == 1:
                    consts = [(gq_in, lay["gqt"]), (qzb_in, lay["qzb"]),
                              (gkc_in, lay["gkc"])]
                    qzb = lay["qzb"]
                NCST = len(consts)

                def wsrc(i):
                    if i < QF:
                        return wq, i
                    return wk, i - QF

                Bl = s_ld.v
                Bcast = cA.v
                Bp1 = cB.v
                Bqf = cC.v
                Broll = cD.v
                Bdst = cE.v
                Bsq = cF.v
                Bnm = cG.v
                Bnr = cH.v
                Bkd = s_st.v

                @blk.sync
                def _(sync):
                    for csrc, cdst in consts:
                        s_ld.inc(sync.dma_start(out=cdst[:, :], in_=csrc), 16)
                    for i in range(NP):
                        w, fc = wsrc(i)
                        if i >= 2:
                            sync.wait_ge(cA.sem, Bcast + i - 1)
                        s_ld.inc(sync.dma_start(
                            out=wstg[i % 2][:, :, :],
                            in_=w.rearrange("(dc p) f -> p dc f", p=128)
                            [:, :, fc * 128:(fc + 1) * 128]), 16)
                        if i >= QF:
                            sync.wait_ge(cE.sem, Bdst + i + 1)
                            s_st.inc(sync.dma_start(
                                out=kin[l].ap()[:, (i - QF) * T:(i - QF + 1) * T],
                                in_=kts[i % 2][:, :]), 16)

                @blk.scalar
                def _(scalar):
                    for i in range(NP):
                        scalar.wait_ge(s_ld.sem, Bl + 16 * (NCST + i + 1))
                        if i >= 2:
                            scalar.wait_ge(cB.sem, Bp1 + i - 1)
                        cA.inc(scalar.copy(wbf[i % 2][:, :],
                                           wstg[i % 2][:, :, :].rearrange(
                                               "p a b -> p (a b)")))

                @blk.tensor
                def _(tensor):
                    for i in range(NP):
                        p0, p1 = ps0[i % 2], ps1[i % 2]
                        tensor.wait_ge(cA.sem, Bcast + i + 1)
                        if i >= 2:
                            if rope:
                                tensor.wait_ge(cC.sem, Bqf + i - 1)
                            else:
                                tensor.wait_ge(cE.sem, Bdst + i - 1)
                        for d in range(ND):
                            for hf in range(2):
                                mm = tensor.matmul(
                                    p0[:, hf * 512:(hf + 1) * 512],
                                    wbf[i % 2][:, d * 128:(d + 1) * 128],
                                    xnT[:, d * T + hf * 512:d * T + (hf + 1) * 512],
                                    start=(d == 0), stop=(d == ND - 1),
                                    skip_group_check=True)
                        cB.inc(mm)
                        if rope:
                            tensor.wait_ge(cC.sem, Bqf + i + 1)
                            if i >= 2:
                                tensor.wait_ge(cH.sem, Bnr + i - 1)
                            for hf in range(2):
                                mm = tensor.matmul(
                                    p1[:, hf * 512:(hf + 1) * 512],
                                    shiftm[:, :],
                                    qf32[i % 2][:, hf * 512:(hf + 1) * 512],
                                    start=True, stop=True,
                                    skip_group_check=True)
                            cD.inc(mm)
                        tensor.wait_ge(cF.sem, Bsq + i + 1)
                        if not rope and i >= 2:
                            tensor.wait_ge(cH.sem, Bnr + i - 1)
                        for hf in range(2):
                            mm = tensor.matmul(
                                p1[0:1, hf * 512:(hf + 1) * 512],
                                ones_col_bf[:, :],
                                sqb[:, hf * 512:(hf + 1) * 512],
                                start=True, stop=True, skip_group_check=True)
                        cG.inc(mm)

                @blk.vector
                def _(vector):
                    for i in range(NP):
                        p0, p1 = ps0[i % 2], ps1[i % 2]
                        isq = i < QF
                        dest = (QT[:, i * T:(i + 1) * T] if isq
                                else kts[i % 2][:, :])
                        vector.wait_ge(cB.sem, Bp1 + i + 1)
                        if not isq and i - QF >= 2:
                            vector.wait_ge(s_st.sem, Bkd + 16 * (i - QF - 1))
                        if i >= 1:
                            vector.wait_ge(cG.sem, Bnm + i)
                        if rope:
                            h = cC.inc(vector.tensor_copy(qf32[i % 2][:, :],
                                                          p0[:, :]))
                            vector.wait_ge(cD.sem, Broll + i + 1)
                            vector.wait_ge(cC.sem, h)
                            h = cH.inc(vector.tensor_tensor(
                                scr[2][:, 0:1024], qf32[i % 2][:, :],
                                cosf[:, :], ALU.mult))
                            vector.wait_ge(cH.sem, h)
                            h = cH.inc(vector.tensor_tensor(
                                scr[2][:, 1024:2048], p1[:, :], sinf[:, :],
                                ALU.mult))
                            vector.wait_ge(cH.sem, h)
                            cE.inc(vector.tensor_tensor(
                                dest, scr[2][:, 0:1024], scr[2][:, 1024:2048],
                                ALU.add))
                        else:
                            if isq and qz:
                                cE.inc(vector.tensor_tensor(
                                    dest, p0[:, :], qzb[:, :], ALU.mult))
                            else:
                                cE.inc(vector.tensor_copy(dest, p0[:, :]))
                        vector.wait_ge(cE.sem, Bdst + i + 1)
                        cF.inc(vector.tensor_tensor(sqb[:, :], dest, dest,
                                                    ALU.mult))
                        vector.wait_ge(cG.sem, Bnm + i + 1)
                        slot = nrow[0:1, i:i + 1] if isq else \
                            nrow[0:1, 32 + (i - QF):32 + (i - QF) + 1]
                        cH.inc(vector.reduce_max(slot, p1[0:1, :], axis=AX))

        def phase_v_cached(l, wv, HKV):
            FW = HKV * HD
            memdbg(f"v_l{l}")
            with ExitStack() as pctx, nc.Block(f"v_l{l}") as blk:
                P = pctx.enter_context
                wvbf = P(sb([128, ND * FW], BF16))
                vstg = [P(sb([128, FW], BF16))
                        for _ in range(2)]
                Bl = s_ld.v
                Bc = cA.v
                Bmm = cB.v
                Bev = cC.v
                Bst = s_st.v

                @blk.sync
                def _(sync):
                    for d in range(ND):
                        if d >= 2:
                            sync.wait_ge(cA.sem, Bc + d - 1)
                        s_ld.inc(sync.dma_start(
                            out=scr[d % 2][:, 0:FW],
                            in_=wv[d * 128:(d + 1) * 128, :]), 16)
                    for t in range(NT):
                        sync.wait_ge(cC.sem, Bev + t + 1)
                        s_st.inc(sync.dma_start(
                            out=vin[l].ap()[t * 128:(t + 1) * 128, :],
                            in_=vstg[t % 2][:, :]), 16)

                @blk.scalar
                def _(scalar):
                    for d in range(ND):
                        scalar.wait_ge(s_ld.sem, Bl + 16 * (d + 1))
                        cA.inc(scalar.copy(wvbf[:, d * FW:(d + 1) * FW],
                                           scr[d % 2][:, 0:FW]))

                @blk.tensor
                def _(tensor):
                    tensor.wait_ge(cA.sem, Bc + ND)
                    for t in range(NT):
                        p0 = [psA, psB][t % 2]
                        if t >= 2:
                            tensor.wait_ge(cC.sem, Bev + t - 1)
                        for d in range(ND):
                            mm = tensor.matmul(
                                p0[:, 0:FW],
                                xnT[:, d * T + t * 128:d * T + (t + 1) * 128],
                                wvbf[:, d * FW:(d + 1) * FW],
                                start=(d == 0), stop=(d == ND - 1),
                                skip_group_check=True)
                        cB.inc(mm)

                @blk.vector
                def _(vector):
                    for t in range(NT):
                        p0 = [psA, psB][t % 2]
                        vector.wait_ge(cB.sem, Bmm + t + 1)
                        if t >= 2:
                            vector.wait_ge(s_st.sem, Bst + 16 * (t - 1))
                        cC.inc(vector.tensor_copy(vstg[t % 2][:, :],
                                                  p0[:, 0:FW]))

        def phase_v_fmajor(l, wv):
            memdbg(f"v_l{l}")
            with ExitStack() as pctx, nc.Block(f"v_l{l}") as blk:
                P = pctx.enter_context
                wstg = [P(sb([128, 16, 128], F32))
                        for _ in range(2)]
                wbf = [P(sb([128, D], BF16)) for _ in range(2)]
                vstg = [P(sb([128, NT * 128], BF16))
                        for _ in range(2)]
                Bl = s_ld.v
                Bc = cA.v
                Bp1 = cB.v
                Bf = cC.v
                Btr = cD.v
                Bbe = cE.v
                Bst = s_st.v

                @blk.sync
                def _(sync):
                    for fc in range(16):
                        if fc >= 2:
                            sync.wait_ge(cA.sem, Bc + fc - 1)
                        s_ld.inc(sync.dma_start(
                            out=wstg[fc % 2][:, :, :],
                            in_=wv.rearrange("(dc p) f -> p dc f", p=128)
                            [:, :, fc * 128:(fc + 1) * 128]), 16)
                        sync.wait_ge(cE.sem, Bbe + fc + 1)
                        s_st.inc(sync.dma_start(
                            out=vin[l].ap().rearrange("(t p) f -> p t f", p=128)
                            [:, :, fc * 128:(fc + 1) * 128],
                            in_=vstg[fc % 2][:, :].rearrange(
                                "p (t f) -> p t f", f=128)), 16)

                @blk.scalar
                def _(scalar):
                    for fc in range(16):
                        scalar.wait_ge(s_ld.sem, Bl + 16 * (fc + 1))
                        if fc >= 2:
                            scalar.wait_ge(cB.sem, Bp1 + fc - 1)
                        cA.inc(scalar.copy(
                            wbf[fc % 2][:, :],
                            wstg[fc % 2][:, :, :].rearrange("p a b -> p (a b)")))

                @blk.tensor
                def _(tensor):
                    for fc in range(16):
                        p0 = [psA, psC][fc % 2]
                        p1 = [psB, psD][fc % 2]
                        tensor.wait_ge(cA.sem, Bc + fc + 1)
                        if fc >= 2:
                            tensor.wait_ge(cC.sem, Bf + fc - 1)
                        for d in range(ND):
                            for hf in range(2):
                                mm = tensor.matmul(
                                    p0[:, hf * 512:(hf + 1) * 512],
                                    wbf[fc % 2][:, d * 128:(d + 1) * 128],
                                    xnT[:, d * T + hf * 512:d * T + (hf + 1) * 512],
                                    start=(d == 0), stop=(d == ND - 1),
                                    skip_group_check=True)
                        cB.inc(mm)
                        tensor.wait_ge(cC.sem, Bf + fc + 1)
                        if fc >= 2:
                            tensor.wait_ge(cE.sem, Bbe + fc - 1)
                        for t in range(NT):
                            mm = tensor.transpose(
                                p1[:, t * 128:(t + 1) * 128],
                                scr[2][:, t * 128:(t + 1) * 128],
                                ident[:, :])
                        cD.inc(mm)

                @blk.vector
                def _(vector):
                    for fc in range(16):
                        p0 = [psA, psC][fc % 2]
                        p1 = [psB, psD][fc % 2]
                        vector.wait_ge(cB.sem, Bp1 + fc + 1)
                        if fc >= 1:
                            vector.wait_ge(cD.sem, Btr + fc)
                        h = cC.inc(vector.tensor_copy(scr[2][:, 0:1024],
                                                      p0[:, :]))
                        vector.wait_ge(cC.sem, h)
                        vector.wait_ge(cD.sem, Btr + fc + 1)
                        if fc >= 2:
                            vector.wait_ge(s_st.sem, Bst + 16 * (fc - 1))
                        cE.inc(vector.tensor_copy(vstg[fc % 2][:, :], p1[:, :]))

        def phase_gather(l, HKV, kv_st_total):
            memdbg(f"gather_l{l}")
            with ExitStack() as pctx, nc.Block(f"gather_l{l}") as blk:
                P = pctx.enter_context
                krow = P(sb([1, 16], F32))

                @blk.gpsimd
                def _(gpsimd):
                    gpsimd.wait_ge(cH.sem, cH.v)
                    s_ldp.inc(gpsimd.dma_start(
                        out=kml[l].ap(), in_=nrow[0:1, 32:32 + HKV]), 16)
                    gpsimd.wait_ge(s_ldp.sem, s_ldp.v)
                    gpsimd.wait_ge(s_st.sem, kv_st_total)
                    s_cc[3 * l + 0].inc(gpsimd.collective_compute(
                        "AllGather", ALU.bypass,
                        replica_groups=[list(range(NCORES))],
                        ins=[kin[l].ap().opt()], outs=[kg[l].ap().opt()]))
                    s_cc[3 * l + 1].inc(gpsimd.collective_compute(
                        "AllGather", ALU.bypass,
                        replica_groups=[list(range(NCORES))],
                        ins=[vin[l].ap().opt()], outs=[vg[l].ap().opt()]))
                    s_cc[3 * l + 2].inc(gpsimd.collective_compute(
                        "AllReduce", ALU.max,
                        replica_groups=[list(range(NCORES))],
                        ins=[kml[l].ap().opt()], outs=[kmg[l].ap().opt()]))
                    gpsimd.wait_ge(s_cc[3 * l + 2].sem, 1)
                    s_ldp.inc(gpsimd.dma_start(out=krow[0:1, 0:HKV],
                                               in_=kmg[l].ap()), 16)

                @blk.vector
                def _(vector):
                    vector.wait_ge(s_ldp.sem, s_ldp.v)
                    if HKV == 16:
                        h = cH.inc(vector.tensor_tensor(
                            nrow[0:1, 64:80], nrow[0:1, 0:16],
                            krow[0:1, 0:16], ALU.mult))
                    else:
                        h = cH.inc(vector.tensor_tensor(
                            nrow[0:1, 64:80].rearrange("p (a b) -> p a b", b=4),
                            nrow[0:1, 0:16].rearrange("p (a b) -> p a b", b=4),
                            krow[0:1, 0:4].rearrange("p (a b) -> p a b", b=1)
                            .broadcast_to([1, 4, 4]), ALU.mult))
                    vector.wait_ge(cH.sem, h)
                    vector.wait_ge(cA.sem, cA.v + 1)
                    cC.inc(vector.tensor_scalar(
                        nrow[0:1, 96:112], nrow[0:1, 80:96], -1.0, None,
                        op0=ALU.mult))
                    vector.wait_ge(cB.sem, cB.v + 1)
                    cD.inc(vector.tensor_copy(negc[l][:, :], psA[:, 0:16]))

                @blk.scalar
                def _(scalar):
                    scalar.wait_ge(cH.sem, cH.v)
                    cA.inc(scalar.activation(nrow[0:1, 80:96],
                                             nrow[0:1, 64:80], AF.Sqrt,
                                             scale=1.0 / 128.0))

                @blk.tensor
                def _(tensor):
                    tensor.wait_ge(cC.sem, cC.v)
                    cB.inc(tensor.matmul(psA[:, 0:16], ones_row_f[:, :],
                                         nrow[0:1, 96:112], start=True,
                                         stop=True))

        def phase_flash(l, H, HKV, mask):
            C = NKC
            memdbg(f"flash_l{l}")
            with ExitStack() as pctx, nc.Block(f"flash_l{l}") as blk:
                P = pctx.enter_context
                kbuf = [P(sb([128, 128], BF16))
                        for _ in range(2)]
                vbuf = [P(sb([128, 128], BF16))
                        for _ in range(2)]
                pbuf = [P(sb([128, T], BF16))
                        for _ in range(2)]
                recipd = scr[1][0:1, 0:1024]
                recipb = scr[0][:, 0:1024]
                Bkv = s_kv.v
                Bss = s_ss.v
                Bse = s_se.v
                Bsm = s_sm.v
                Bsa = s_sa.v
                B1 = s_hd1.v
                B2 = s_hd2.v
                B3 = s_hd3.v
                Bpe = s_hpe.v
                kgl = kg[l].ap()
                vgl = vg[l].ap()

                def kv_idx(qh, c):
                    kvh = qh // (H // HKV)
                    return (kgl[(c // 8) * 128:(c // 8 + 1) * 128,
                                kvh * T + (c % 8) * 128:
                                kvh * T + (c % 8 + 1) * 128],
                            vgl[c * 128:(c + 1) * 128,
                                kvh * 128:(kvh + 1) * 128])

                @blk.gpsimd
                def _(gpsimd):
                    for g in range(H * C):
                        qh, c = divmod(g, C)
                        if g >= 2:
                            gpsimd.wait_ge(s_sa.sem, Bsa + g - 1)
                        ksrc, vsrc = kv_idx(qh, c)
                        s_kv.inc(gpsimd.dma_start(out=kbuf[g % 2][:, :],
                                                  in_=ksrc), 16)
                        s_kv.inc(gpsimd.dma_start(out=vbuf[g % 2][:, :],
                                                  in_=vsrc), 16)

                @blk.tensor
                def _(tensor):
                    psS = [psA, psB]

                    def emit_S(g):
                        qh, c = divmod(g, C)
                        tensor.wait_ge(s_kv.sem, Bkv + 32 * (g + 1))
                        if g >= 2:
                            tensor.wait_ge(s_se.sem, Bse + g - 1)
                        for hf in range(2):
                            mm = tensor.matmul(
                                psS[g % 2][:, hf * 512:(hf + 1) * 512],
                                kbuf[g % 2][:, :],
                                QT[:, qh * T + hf * 512:qh * T + (hf + 1) * 512],
                                start=True, stop=True, skip_group_check=True)
                        s_ss.inc(mm)

                    def emit_AV(g):
                        qh, c = divmod(g, C)
                        if mask:
                            tensor.wait_ge(s_sm.sem, Bsm + g + 1)
                        else:
                            tensor.wait_ge(s_se.sem, Bse + g + 1)
                        pb = pbuf[g % 2]
                        if c == 0 and qh >= 1:
                            tensor.wait_ge(s_hd3.sem, B3 + qh)
                        for hf in range(2):
                            mm = tensor.matmul(
                                psC[:, hf * 512:(hf + 1) * 512],
                                vbuf[g % 2][:, :],
                                pb[:, hf * 512:(hf + 1) * 512],
                                start=(c == 0), stop=(c == C - 1),
                                skip_group_check=True)
                        if c == 0 and qh >= 1:
                            tensor.wait_ge(s_hd2.sem, B2 + qh)
                        for hf in range(2):
                            mm2 = tensor.matmul(
                                psD[0:1, hf * 512:(hf + 1) * 512],
                                ones_col_bf[:, :],
                                pb[:, hf * 512:(hf + 1) * 512],
                                start=(c == 0), stop=(c == C - 1),
                                skip_group_check=True)
                        s_sa.inc(mm2)

                    def emit_headend(p):
                        tensor.wait_ge(s_hd1.sem, B1 + p + 1)
                        for hf in range(2):
                            mm = tensor.matmul(
                                psD[:, hf * 512:(hf + 1) * 512],
                                ones_row_f[:, :],
                                recipd[0:1, hf * 512:(hf + 1) * 512],
                                start=True, stop=True, skip_group_check=True)
                        s_hpe.inc(mm)

                    for g in range(H * C):
                        emit_S(g)
                        if g >= 1:
                            emit_AV(g - 1)
                            if (g - 1) % C == C - 1:
                                emit_headend((g - 1) // C)
                    emit_AV(H * C - 1)
                    emit_headend(H - 1)

                @blk.scalar
                def _(scalar):
                    psS = [psA, psB]
                    for g in range(H * C):
                        qh, c = divmod(g, C)
                        scalar.wait_ge(s_ss.sem, Bss + g + 1)
                        if g >= 2:
                            scalar.wait_ge(s_sa.sem, Bsa + g - 1)
                        s_se.inc(scalar.activation(
                            pbuf[g % 2][:, :], psS[g % 2][:, :], AF.Exp,
                            bias=negc[l][:, qh:qh + 1], scale=ISQ))

                @blk.vector
                def _(vector):
                    for g in range(H * C):
                        qh, c = divmod(g, C)
                        if mask:
                            vector.wait_ge(s_se.sem, Bse + g + 1)
                            if g >= 2:
                                vector.wait_ge(s_sa.sem, Bsa + g - 1)
                            s_sm.inc(vector.scalar_tensor_tensor(
                                pbuf[g % 2][:, :], lay["gqt"][:, :],
                                lay["gkc"][:, c:c + 1], pbuf[g % 2][:, :],
                                op0=ALU.is_lt, op1=ALU.mult))
                        if c == C - 1:
                            vector.wait_ge(s_sa.sem, Bsa + (qh + 1) * C)
                            h = s_hd1.inc(vector.reciprocal(recipd[0:1, :],
                                                            psD[0:1, :]))
                            vector.wait_ge(s_hd1.sem, h)
                            vector.wait_ge(s_hpe.sem, Bpe + qh + 1)
                            h = s_hd2.inc(vector.tensor_copy(recipb[:, :],
                                                             psD[:, :]))
                            vector.wait_ge(s_hd2.sem, h)
                            s_hd3.inc(vector.tensor_tensor(
                                OTn[:, qh * T:(qh + 1) * T], psC[:, :],
                                recipb[:, :], ALU.mult))

        def phase_wo(l, wo):
            memdbg(f"wo_l{l}")
            with ExitStack() as pctx, nc.Block(f"wo_l{l}") as blk:
                P = pctx.enter_context
                wobf = QT
                Bl = s_ld.v
                Bc = cA.v
                Bmm = cB.v
                Bx = cC.v

                @blk.sync
                def _(sync):
                    for grp in range(2):
                        for h8 in range(8):
                            h = grp * 8 + h8
                            if grp == 1:
                                sync.wait_ge(cB.sem, Bmm + 8)
                            if h8 >= 2:
                                sync.wait_ge(cA.sem, Bc + grp * 8 + h8 - 1)
                            s_ld.inc(sync.dma_start(
                                out=scr[h8 % 2][:, :],
                                in_=wo[h * 128:(h + 1) * 128, :]), 16)

                @blk.scalar
                def _(scalar):
                    for grp in range(2):
                        if grp == 1:
                            scalar.wait_ge(cB.sem, Bmm + 8)
                        for h8 in range(8):
                            scalar.wait_ge(s_ld.sem, Bl + 16 * (grp * 8 + h8 + 1))
                            cA.inc(scalar.copy(wobf[:, h8 * D:(h8 + 1) * D],
                                               scr[h8 % 2][:, :]))

                @blk.tensor
                def _(tensor):
                    for grp in range(2):
                        tensor.wait_ge(cA.sem, Bc + 8 * (grp + 1))
                        for t in range(NT):
                            pl, ph = [(psA, psB), (psC, psD)][t % 2]
                            if grp * NT + t >= 2:
                                tensor.wait_ge(cC.sem, Bx + grp * NT + t - 1)
                            for h8 in range(8):
                                h = grp * 8 + h8
                                for hf in range(4):
                                    p = pl if hf < 2 else ph
                                    mm = tensor.matmul(
                                        p[:, (hf % 2) * 512:(hf % 2 + 1) * 512],
                                        OTn[:, h * T + t * 128:h * T + (t + 1) * 128],
                                        wobf[:, h8 * D + hf * 512:
                                             h8 * D + (hf + 1) * 512],
                                        start=(h8 == 0), stop=(h8 == 7),
                                        skip_group_check=True)
                            cB.inc(mm)

                @blk.vector
                def _(vector):
                    for grp in range(2):
                        for t in range(NT):
                            pl, ph = [(psA, psB), (psC, psD)][t % 2]
                            vector.wait_ge(cB.sem, Bmm + grp * NT + t + 1)
                            h = cH.inc(vector.tensor_tensor(
                                X[:, t * D:t * D + 1024],
                                X[:, t * D:t * D + 1024], pl[:, :], ALU.add))
                            vector.wait_ge(cH.sem, h)
                            cC.inc(vector.tensor_tensor(
                                X[:, t * D + 1024:(t + 1) * D],
                                X[:, t * D + 1024:(t + 1) * D], ph[:, :],
                                ALU.add))

        def phase_ffn0():
            memdbg("ffn0")
            with ExitStack() as pctx, nc.Block("ffn0") as blk:
                P = pctx.enter_context
                dsb = P(sb([128, D], F32))
                dsrow_sb = nrow[0:1, 0:D]
                gs_sb = P(sb([128, NIC], F32))
                us_sb = P(sb([128, NIC], F32))
                xadd = scr[2][:, 1024:2048]
                h_sb = P(sb([128, NIC * 512], BF16))
                gbf = [P(sb([128, D], BF16)) for _ in range(2)]
                ubf = [P(sb([128, D], BF16)) for _ in range(2)]
                dbf = [P(sb([128, 1024], BF16))
                       for _ in range(2)]
                silu = [P(sb([128, 512], BF16))
                        for _ in range(2)]
                usb = [P(sb([128, 512], BF16))
                       for _ in range(2)]
                Bl = s_ld.v
                Bc = cA.v
                Bgu = cB.v
                Bsil = cC.v
                Buev = cD.v
                Bh = cE.v
                Bxa = cF.v
                Bdn = cG.v
                Bdc = cI.v
                Bdl = s_ldp.v
                BJ = cJ.v

                @blk.sync
                def _(sync):
                    cJ.inc(sync.dma_start(out=dsrow_sb[:, :], in_=ds_row), 16)
                    cJ.inc(sync.dma_start(out=gs_sb[:, :], in_=gs_c), 16)
                    cJ.inc(sync.dma_start(out=us_sb[:, :], in_=us_c), 16)
                    for hh in range(2):
                        for ic in range(NIC):
                            i = hh * NIC + ic
                            if i >= 1:
                                sync.wait_ge(cA.sem, Bc + i)
                            s_ld.inc(sync.dma_start(
                                out=scr[0][:, :].rearrange(
                                    "p (a b) -> p a b", b=128),
                                in_=gwT.rearrange("(dc p) f -> p dc f", p=128)
                                [:, :, ic * 128:(ic + 1) * 128]), 16)
                            s_ld.inc(sync.dma_start(
                                out=scr[1][:, :].rearrange(
                                    "p (a b) -> p a b", b=128),
                                in_=uwT.rearrange("(dc p) f -> p dc f", p=128)
                                [:, :, ic * 128:(ic + 1) * 128]), 16)

                @blk.gpsimd
                def _(gpsimd):
                    for hh in range(2):
                        for dmh in range(2):
                            for ic in range(NIC):
                                j = (hh * 2 + dmh) * NIC + ic
                                if j >= 1:
                                    gpsimd.wait_ge(cI.sem, Bdc + j)
                                s_ldp.inc(gpsimd.dma_start(
                                    out=scr[2][:, 0:1024],
                                    in_=dwT[ic * 128:(ic + 1) * 128,
                                            dmh * 1024:(dmh + 1) * 1024]), 16)

                @blk.scalar
                def _(scalar):
                    for hh in range(2):
                        for ic in range(NIC):
                            i = hh * NIC + ic
                            scalar.wait_ge(s_ld.sem, Bl + 32 * (i + 1))
                            if i >= 2:
                                scalar.wait_ge(cB.sem, Bgu + i - 1)
                            scalar.copy(gbf[i % 2][:, :], scr[0][:, :])
                            cA.inc(scalar.copy(ubf[i % 2][:, :], scr[1][:, :]))
                            scalar.wait_ge(cB.sem, Bgu + i + 1)
                            if i == 0:
                                scalar.wait_ge(cJ.sem, BJ + 48)
                            psGU = [psA, psB][i % 2]
                            cC.inc(scalar.activation(
                                silu[i % 2][:, :], psGU[:, 0:512], AF.Silu,
                                scale=gs_sb[:, ic:ic + 1]))
                        for dmh in range(2):
                            for ic in range(NIC):
                                j = (hh * 2 + dmh) * NIC + ic
                                scalar.wait_ge(s_ldp.sem, Bdl + 16 * (j + 1))
                                if j >= 2:
                                    scalar.wait_ge(cG.sem, Bdn + j - 1)
                                cI.inc(scalar.copy(dbf[j % 2][:, :],
                                                   scr[2][:, 0:1024]))

                @blk.tensor
                def _(tensor):
                    tensor.wait_ge(cJ.sem, BJ + 48)
                    for j4 in range(4):
                        p = psC if j4 < 2 else psD
                        mm = tensor.matmul(
                            p[:, (j4 % 2) * 512:(j4 % 2 + 1) * 512],
                            ones_row_f[:, :],
                            dsrow_sb[:, j4 * 512:(j4 + 1) * 512],
                            start=True, stop=True, skip_group_check=True)
                    cJ.inc(mm, 1)
                    for hh in range(2):
                        for ic in range(NIC):
                            i = hh * NIC + ic
                            psGU = [psA, psB][i % 2]
                            tensor.wait_ge(cA.sem, Bc + i + 1)
                            if i >= 2:
                                tensor.wait_ge(cC.sem, Bsil + i - 1)
                                tensor.wait_ge(cD.sem, Buev + i - 1)
                            if ic == 0 and hh == 1:
                                tensor.wait_ge(cF.sem, Bxa + 8)
                            for d in range(ND):
                                rhs = xnT[:, d * T + hh * 512:
                                          d * T + (hh + 1) * 512]
                                tensor.matmul(
                                    psGU[:, 0:512],
                                    gbf[i % 2][:, d * 128:(d + 1) * 128], rhs,
                                    start=(d == 0), stop=(d == ND - 1),
                                    skip_group_check=True)
                                mm = tensor.matmul(
                                    psGU[:, 512:1024],
                                    ubf[i % 2][:, d * 128:(d + 1) * 128], rhs,
                                    start=(d == 0), stop=(d == ND - 1),
                                    skip_group_check=True)
                            cB.inc(mm)
                        tensor.wait_ge(cE.sem, Bh + (hh + 1) * NIC)
                        pstch = [psA, psB, psC, psD]
                        for dmh in range(2):
                            for ic in range(NIC):
                                j = (hh * 2 + dmh) * NIC + ic
                                tensor.wait_ge(cI.sem, Bdc + j + 1)
                                if dmh == 0 and ic == 0:
                                    tensor.wait_ge(cC.sem, Bsil + (hh + 1) * NIC)
                                    tensor.wait_ge(cD.sem, Buev + (hh + 1) * NIC)
                                    tensor.wait_ge(cJ.sem, BJ + 50)
                                if ic == 0 and (hh * 2 + dmh) >= 1:
                                    tensor.wait_ge(cF.sem, Bxa + (hh * 2 + dmh) * 4)
                                for tch in range(4):
                                    for hf in range(2):
                                        mm = tensor.matmul(
                                            pstch[tch][:, hf * 512:(hf + 1) * 512],
                                            h_sb[:, ic * 512 + tch * 128:
                                                 ic * 512 + (tch + 1) * 128],
                                            dbf[j % 2][:, hf * 512:(hf + 1) * 512],
                                            start=(ic == 0), stop=(ic == NIC - 1),
                                            skip_group_check=True)
                                cG.inc(mm)

                @blk.vector
                def _(vector):
                    vector.wait_ge(cJ.sem, BJ + 49)
                    vector.tensor_copy(dsb[:, 0:1024], psC[:, :])
                    h0 = cJ.inc(vector.tensor_copy(dsb[:, 1024:2048],
                                                   psD[:, :]), 1)
                    for hh in range(2):
                        for ic in range(NIC):
                            i = hh * NIC + ic
                            psGU = [psA, psB][i % 2]
                            vector.wait_ge(cB.sem, Bgu + i + 1)
                            h = cD.inc(vector.tensor_scalar(
                                usb[i % 2][:, :], psGU[:, 512:1024],
                                us_sb[:, ic:ic + 1], None, op0=ALU.mult))
                            vector.wait_ge(cD.sem, h)
                            vector.wait_ge(cC.sem, Bsil + i + 1)
                            cE.inc(vector.tensor_tensor(
                                h_sb[:, ic * 512:(ic + 1) * 512],
                                silu[i % 2][:, :], usb[i % 2][:, :], ALU.mult))
                        pstch = [psA, psB, psC, psD]
                        for dmh in range(2):
                            for tch in range(4):
                                vector.wait_ge(cG.sem,
                                               Bdn + (hh * 2 + dmh + 1) * NIC)
                                tg = hh * 4 + tch
                                xsl = X[:, tg * D + dmh * 1024:
                                        tg * D + (dmh + 1) * 1024]
                                h = cH.inc(vector.tensor_tensor(
                                    xadd, pstch[tch][:, :],
                                    dsb[:, dmh * 1024:(dmh + 1) * 1024],
                                    ALU.mult))
                                vector.wait_ge(cH.sem, h)
                                cF.inc(vector.tensor_tensor(
                                    xsl, xsl, xadd, ALU.add))

        def phase_ffn1():
            memdbg("ffn1")
            with ExitStack() as pctx, nc.Block("ffn1") as blk:
                P = pctx.enter_context
                abf = P(sb([128, ND * 512], BF16))
                bbf = P(sb([128, 4 * D], BF16))
                wrT = P(sb([128, ND * 8], BF16))
                wrst = P(sb([8, D], F32))
                brsb = P(sb([1, 8], F32))
                p8 = P(sb([128, 8], F32))
                ow = P(sb([128, 9], F32))
                sst = P(sb([128, 4], F32))
                hmf = P(sb([128, 512], F32))
                hTb = [P(sb([128, 512], BF16))
                       for _ in range(2)]
                Bl = s_ld.v
                Bc = cA.v
                Bex = cI.v
                Bpe = cB.v
                Btr = cG.v
                BwT = cD.v
                Bsx = cC.v
                Bmk = cF.v
                Bhev = cE.v
                BxH = cH.v

                @blk.sync
                def _(sync):
                    for d in range(ND):
                        if d >= 2:
                            sync.wait_ge(cA.sem, Bc + d - 1)
                        s_ld.inc(sync.dma_start(
                            out=scr[d % 2][:, 0:512],
                            in_=a_T[d * 128:(d + 1) * 128, :]), 16)
                    for b in range(4):
                        sync.wait_ge(cA.sem, Bc + ND + b)
                        s_ld.inc(sync.dma_start(
                            out=scr[b % 2][:, :],
                            in_=b_sT[b * 128:(b + 1) * 128, :]), 16)
                    s_ld.inc(sync.dma_start(out=wrst[:, :], in_=wr_n), 16)
                    s_ld.inc(sync.dma_start(out=brsb[:, :], in_=br_r), 16)

                @blk.scalar
                def _(scalar):
                    for d in range(ND):
                        scalar.wait_ge(s_ld.sem, Bl + 16 * (d + 1))
                        cA.inc(scalar.copy(abf[:, d * 512:(d + 1) * 512],
                                           scr[d % 2][:, 0:512]))
                    for b in range(4):
                        scalar.wait_ge(s_ld.sem, Bl + 16 * (ND + b + 1))
                        cA.inc(scalar.mul(bbf[:, b * D:(b + 1) * D],
                                          scr[b % 2][:, :], 1.0 / RK))
                    for t in range(NT):
                        scalar.wait_ge(cC.sem, Bsx + 5 * t + 1)
                        cI.inc(scalar.activation(
                            p8[:, :], psB[:, 0:8], AF.Exp,
                            bias=sst[:, 0:1]))

                @blk.tensor
                def _(tensor):
                    tensor.wait_ge(s_ld.sem, Bl + 16 * (ND + 4 + 2))
                    for d in range(ND):
                        mm = tensor.transpose(
                            psC[:, d * 8:(d + 1) * 8],
                            wrst[:, d * 128:(d + 1) * 128], ident[0:8, 0:8])
                    cB.inc(mm)
                    tensor.wait_ge(cA.sem, Bc + ND + 4)
                    tensor.wait_ge(cD.sem, BwT + 1)
                    for t in range(NT):
                        if t >= 1:
                            tensor.wait_ge(cI.sem, Bex + t)
                            tensor.wait_ge(cC.sem, Bsx + 5 * t)
                        for d in range(ND):
                            tensor.matmul(
                                psB[:, 0:8],
                                xnT[:, d * T + t * 128:d * T + (t + 1) * 128],
                                wrT[:, d * 8:(d + 1) * 8],
                                start=(d == 0), stop=False,
                                skip_group_check=True)
                        mm = tensor.matmul(
                            psB[:, 0:8], ones_row_f[:, :], brsb[:, :],
                            start=False, stop=True, skip_group_check=True)
                        cB.inc(mm)
                        if t >= 1:
                            tensor.wait_ge(cF.sem, Bmk + t)
                        for d in range(ND):
                            mm = tensor.matmul(
                                psA[:, 0:512],
                                xnT[:, d * T + t * 128:d * T + (t + 1) * 128],
                                abf[:, d * 512:(d + 1) * 512],
                                start=(d == 0), stop=(d == ND - 1),
                                skip_group_check=True)
                        cB.inc(mm)
                        tensor.wait_ge(cF.sem, Bmk + t + 1)
                        if t >= 1:
                            tensor.wait_ge(cE.sem, Bhev + t)
                        for b in range(4):
                            mm = tensor.transpose(
                                psB[:, 512 + b * 128:512 + (b + 1) * 128],
                                hmf[:, b * 128:(b + 1) * 128], ident[:, :])
                        cG.inc(mm)
                        tensor.wait_ge(cE.sem, Bhev + t + 1)
                        if t >= 1:
                            tensor.wait_ge(cH.sem, BxH + 2 * t)
                        for b in range(4):
                            for hf in range(4):
                                p = psC if hf < 2 else psD
                                mm = tensor.matmul(
                                    p[:, (hf % 2) * 512:(hf % 2 + 1) * 512],
                                    hTb[t % 2][:, b * 128:(b + 1) * 128],
                                    bbf[:, b * D + hf * 512:b * D + (hf + 1) * 512],
                                    start=(b == 0), stop=(b == 3),
                                    skip_group_check=True)
                        cB.inc(mm)

                @blk.vector
                def _(vector):
                    vector.wait_ge(cB.sem, Bpe + 1)
                    cD.inc(vector.tensor_copy(wrT[:, :], psC[:, 0:ND * 8]))
                    for t in range(NT):
                        vector.wait_ge(cB.sem, Bpe + 1 + 3 * t + 1)
                        h = cC.inc(vector.reduce_max(
                            sst[:, 0:1], psB[:, 0:8], axis=AX, negate=True))
                        vector.wait_ge(cC.sem, h)
                        h = cC.inc(vector.reduce_max(
                            sst[:, 1:2], psB[:, 0:8], axis=AX))
                        vector.wait_ge(cC.sem, h)
                        vector.wait_ge(cI.sem, Bex + t + 1)
                        h = cC.inc(vector.reduce_sum(
                            sst[:, 2:3], p8[:, :], axis=AX))
                        vector.wait_ge(cC.sem, h)
                        h = cC.inc(vector.reciprocal(sst[:, 3:4], sst[:, 2:3]))
                        vector.wait_ge(cC.sem, h)
                        h = cC.inc(vector.tensor_scalar(
                            ow[:, 0:8], psB[:, 0:8], sst[:, 1:2], sst[:, 3:4],
                            op0=ALU.is_ge, op1=ALU.mult))
                        vector.wait_ge(cC.sem, h)
                        vector.wait_ge(cB.sem, Bpe + 1 + 3 * t + 2)
                        if t >= 1:
                            vector.wait_ge(cG.sem, Btr + t)
                        for e in range(NE):
                            inst = vector.tensor_scalar(
                                hmf[:, e * 64:(e + 1) * 64],
                                psA[:, e * 64:(e + 1) * 64],
                                ow[:, e:e + 1], None, op0=ALU.mult)
                        cF.inc(inst)
                        vector.wait_ge(cG.sem, Btr + t + 1)
                        if t >= 2:
                            vector.wait_ge(cB.sem, Bpe + 1 + 3 * (t - 1))
                        cE.inc(vector.tensor_copy(hTb[t % 2][:, :],
                                                  psB[:, 512:1024]))
                        vector.wait_ge(cB.sem, Bpe + 1 + 3 * t + 3)
                        cH.inc(vector.tensor_tensor(
                            X[:, t * D:t * D + 1024],
                            X[:, t * D:t * D + 1024], psC[:, :], ALU.add))
                        cH.inc(vector.tensor_tensor(
                            X[:, t * D + 1024:(t + 1) * D],
                            X[:, t * D + 1024:(t + 1) * D], psD[:, :],
                            ALU.add))

        def dump_x(dest_ap):
            with nc.Block("dump") as blk:
                @blk.sync
                def _(sync):
                    s_st.inc(sync.dma_start(
                        out=dest_ap.rearrange("(t p) d -> p t d", p=128),
                        in_=X[:, :].rearrange("p (t d) -> p t d", d=D)), 16)
                    sync.wait_ge(s_st.sem, s_st.v)

        def attn_layer(l):
            global QT, OTn
            with ExitStack() as actx:
                A = actx.enter_context
                lay.clear()
                if l == 0:
                    lay["cosf"] = A(nc.sbuf_tensor("cosf_s", [128, T], F32))
                    lay["sinf"] = A(nc.sbuf_tensor("sinf_s", [128, T], F32))
                    lay["shiftm"] = A(nc.sbuf_tensor("shiftm_s", [128, 128], F32))
                else:
                    lay["gqt"] = A(nc.sbuf_tensor("gqt_s", [128, T], F32))
                    lay["qzb"] = A(nc.sbuf_tensor("qzb_s", [128, T], F32))
                    lay["gkc"] = A(nc.sbuf_tensor("gkc_s", [128, NKC], F32))
                QT = A(nc.sbuf_tensor(f"QT{l}", [128, 16 * T], BF16))
                if l == 0:
                    phase_qk(0, wq0, wk0, 4, True, False)
                    phase_v_cached(0, wv0, 4)
                    phase_gather(0, 4, s_st.v)
                    OTn = A(nc.sbuf_tensor(f"OTn{l}", [128, 16 * T], BF16))
                    phase_flash(0, 16, 4, False)
                    phase_wo(0, wo0)
                else:
                    phase_qk(1, wq1, wk1, 16, False, True)
                    phase_v_fmajor(1, wv1)
                    phase_gather(1, 16, s_st.v)
                    OTn = A(nc.sbuf_tensor(f"OTn{l}", [128, 16 * T], BF16))
                    phase_flash(1, 16, 16, True)
                    phase_wo(1, wo1)

        phase_ln("l0a")
        attn_layer(0)
        if "l0_attn" in taps:
            dump_x(taps["l0_attn"])
        if stop_after == "l0_attn":
            dump_x(y_out)
            return nc, din
        phase_ln("l0f")
        phase_ffn0()
        if "l0_ffn" in taps:
            dump_x(taps["l0_ffn"])
        if stop_after == "l0_ffn":
            dump_x(y_out)
            return nc, din
        phase_ln("l1a")
        attn_layer(1)
        if "l1_attn" in taps:
            dump_x(taps["l1_attn"])
        if stop_after == "l1_attn":
            dump_x(y_out)
            return nc, din
        phase_ln("l1f")
        phase_ffn1()
        dump_x(y_out)
    return nc, din



def _rope_tables(pos):
    inv = 1.0 / (10000.0 ** (np.arange(0, HD, 2, dtype=np.float32) / HD))
    fr = pos[:, None].astype(np.float32) * inv[None, :]
    emb = np.concatenate([fr, fr], axis=-1)
    return np.cos(emb).astype(np.float32), np.sin(emb).astype(np.float32)


def _shift_mat():
    m = np.zeros((128, 128), dtype=np.float32)
    for k in range(128):
        m[k, (k + 1) % 128] = 1.0
    return m


def _core_inputs(inputs, r):
    x = np.ascontiguousarray(
        inputs["L0_attn_x"][0, r * T:(r + 1) * T, :]).astype(np.float32, copy=False)
    pos = np.arange(r * T, (r + 1) * T)
    cos, sin = _rope_tables(pos)
    cosf = np.ascontiguousarray(cos.T)
    sinf = np.ascontiguousarray(sin.T)
    gq = np.tile(pos.astype(np.float32)[None, :], (128, 1))
    qz = np.ones((128, T), dtype=np.float32)
    if r == NCORES - 1:
        gq[:, T - 1] = -1.0
        qz[:, T - 1] = 0.0
    gkcols = (np.arange(NKC, dtype=np.float32)[None, :] * 128
              + np.arange(128, dtype=np.float32)[:, None])
    gs = inputs["L0_ffn_gate_s"].astype(np.float32)
    us = inputs["L0_ffn_up_s"].astype(np.float32)
    ds = inputs["L0_ffn_down_s"].astype(np.float32)
    A = np.asarray(inputs["L1_ffn_A"], dtype=np.float32)
    B = np.asarray(inputs["L1_ffn_B"], dtype=np.float32)
    d = dict(
        x=x,
        wq0=np.asarray(inputs["L0_attn_Wq"], dtype=np.float32),
        wk0=np.asarray(inputs["L0_attn_Wk"], dtype=np.float32),
        wv0=np.asarray(inputs["L0_attn_Wv"], dtype=np.float32),
        wo0=np.asarray(inputs["L0_attn_Wo"], dtype=np.float32),
        gwT=np.ascontiguousarray(np.asarray(inputs["L0_ffn_gate_w"],
                                            dtype=np.float32).T),
        uwT=np.ascontiguousarray(np.asarray(inputs["L0_ffn_up_w"],
                                            dtype=np.float32).T),
        dwT=np.ascontiguousarray(np.asarray(inputs["L0_ffn_down_w"],
                                            dtype=np.float32).T),
        gs_c=np.ascontiguousarray(gs.reshape(NIC, 128).T),
        us_c=np.ascontiguousarray(us.reshape(NIC, 128).T),
        ds_row=np.ascontiguousarray(ds[None, :]),
        wq1=np.asarray(inputs["L1_attn_Wq"], dtype=np.float32),
        wk1=np.asarray(inputs["L1_attn_Wk"], dtype=np.float32),
        wv1=np.asarray(inputs["L1_attn_Wv"], dtype=np.float32),
        wo1=np.asarray(inputs["L1_attn_Wo"], dtype=np.float32),
        a_T=np.ascontiguousarray(A.reshape(NE * RK, D).T),
        b_sT=np.ascontiguousarray(B.transpose(0, 2, 1).reshape(NE * RK, D)),
        wr_n=np.asarray(inputs["L1_ffn_Wr"], dtype=np.float32),
        br_r=np.ascontiguousarray(
            np.asarray(inputs["L1_ffn_br"], dtype=np.float32)[None, :]),
        ident=np.eye(128, dtype=np.float32),
        shiftm=_shift_mat(),
        cosf=cosf, sinf=sinf, gqt=gq, qzb=qz,
        gkc=np.ascontiguousarray(gkcols),
    )
    return d


_CACHE = {}


def kernel(**inputs):
    key = "full"
    if key not in _CACHE:
        _CACHE[key] = build_program()
    nc, _ = _CACHE[key]
    in_maps = [_core_inputs(inputs, r) for r in range(NCORES)]
    res = run_bass_kernel_spmd(nc, in_maps, core_ids=list(range(NCORES)))
    out = np.concatenate([res.results[r]["y"] for r in range(NCORES)], axis=0)
    return out.reshape(1, SEQ, D).astype(np.float32)


# version 05
# speedup vs baseline: 1.5691x; 1.5691x over previous
from contextlib import ExitStack
import math
import numpy as np
import ml_dtypes
import concourse.bass as bass
import concourse.mybir as mybir
from concourse.bass_utils import run_bass_kernel_spmd

F32 = mybir.dt.float32
BF16 = mybir.dt.bfloat16
AF = mybir.ActivationFunctionType
ALU = mybir.AluOpType
AX = mybir.AxisListType.X

NCORES = 8
T = 1024
NT = 8
D = 2048
ND = 16
HD = 128
SEQ = 8192
NKC = 64
INTER = 5632
NIC = 44
NE, RK = 8, 64
ISQ = 1.0 / math.sqrt(128.0)


class Ctr:
    def __init__(self, ctx, nc, name):
        self.sem = ctx.enter_context(nc.semaphore(name))
        self.v = 0

    def inc(self, inst, k=1):
        inst.then_inc(self.sem, k)
        self.v += k
        return self.v


DEV_DUMP_KV = False


def build_program(stop_after=None, tap_x=()):
    nc = bass.Bass("TRN2", target_bir_lowering=False)
    din = {}

    def dI(name, shape, dtype=F32):
        din[name] = nc.dram_tensor(name, shape, dtype, kind="ExternalInput").ap()
        return din[name]

    x_in = dI("x", [T, D])
    wq0 = dI("wq0", [D, 16 * HD])
    wk0 = dI("wk0", [4 * HD, D])
    wv0 = dI("wv0", [D, 4 * HD])
    wo0 = dI("wo0", [16 * HD, D])
    gwT = dI("gwT", [INTER, D])
    uwT = dI("uwT", [INTER, D])
    dwT = dI("dwT", [INTER, D])
    gs_c = dI("gs_c", [128, NIC])
    us_c = dI("us_c", [128, NIC])
    ds_b = dI("ds_b", [128, D])
    wq1 = dI("wq1", [D, D])
    wk1 = dI("wk1", [D, D])
    wv1 = dI("wv1", [D, D])
    wo1 = dI("wo1", [D, D])
    a_T = dI("a_T", [D, NE * RK])
    b_sT = dI("b_sT", [NE * RK, D])
    wr_n = dI("wr_n", [NE, D])
    br_r = dI("br_r", [1, NE])
    ident_in = dI("ident", [128, 128])
    shift_in = dI("shiftm", [128, 128])
    cosf_in = dI("cosf", [128, T])
    sinf_in = dI("sinf", [128, T])
    gq_in = dI("gqt", [128, T])
    qzb_in = dI("qzb", [128, T])
    gkc_in = dI("gkc", [128, NKC])

    y_out = nc.dram_tensor("y", [T, D], F32, kind="ExternalOutput").ap()
    taps = {}
    for tname in tap_x:
        taps[tname] = nc.dram_tensor("tap_" + tname, [T, D], F32,
                                     kind="ExternalOutput").ap()

    kin = [nc.dram_tensor("kin0", [128, 4 * T], BF16),
           nc.dram_tensor("kin1", [128, 16 * T], BF16)]
    vin = [nc.dram_tensor("vin0", [T, 4 * HD], BF16),
           nc.dram_tensor("vin1", [T, 16 * HD], BF16)]
    kg = [nc.dram_tensor("kg0", [NCORES * 128, 4 * T], BF16),
          nc.dram_tensor("kg1", [NCORES * 128, 16 * T], BF16)]
    vg = [nc.dram_tensor("vg0", [SEQ, 4 * HD], BF16),
          nc.dram_tensor("vg1", [SEQ, 16 * HD], BF16)]
    kml = [nc.dram_tensor("kml0", [1, 4], F32),
           nc.dram_tensor("kml1", [1, 16], F32)]
    kmg = [nc.dram_tensor("kmg0", [1, 4], F32),
           nc.dram_tensor("kmg1", [1, 16], F32)]

    ctx = ExitStack()
    with ctx:
        E = ctx.enter_context

        X = E(nc.sbuf_tensor("X", [128, NT * D], F32))
        xnT = E(nc.sbuf_tensor("xnT", [128, ND * T], BF16))
        ident = E(nc.sbuf_tensor("ident_s", [128, 128], F32))
        ones_col_bf = E(nc.sbuf_tensor("ones_col_bf", [128, 1], BF16))
        ones_row_f = E(nc.sbuf_tensor("ones_row_f", [1, 128], F32))
        eps_col = E(nc.sbuf_tensor("eps_col", [128, 1], F32))
        negc = [E(nc.sbuf_tensor("negc0", [128, 16], F32)),
                E(nc.sbuf_tensor("negc1", [128, 16], F32))]
        nrow = E(nc.sbuf_tensor("nrow", [1, 1024], F32))
        scr = [E(nc.sbuf_tensor(f"scr{i}", [128, D], F32)) for i in range(3)]
        psA = E(nc.psum_tensor("psA", [128, 1024], F32))
        psB = E(nc.psum_tensor("psB", [128, 1024], F32))
        psC = E(nc.psum_tensor("psC", [128, 1024], F32))
        psD = E(nc.psum_tensor("psD", [128, 1024], F32))
        lay = {}
        _anon = [0]

        import os
        import concourse.bass_primitives as _bp

        def memdbg(tag):
            if os.environ.get("KMEMDBG"):
                print("MEM", tag, nc.bytes_allocated(_bp.MemorySpace.SBUF))

        def sb(shape, dtype):
            _anon[0] += 1
            return nc.sbuf_tensor(f"tt{_anon[0]}", shape, dtype)

        s_ld = Ctr(ctx, nc, "s_ld")
        s_st = Ctr(ctx, nc, "s_st")
        s_ldp = Ctr(ctx, nc, "s_ldp")
        s_cc = [Ctr(ctx, nc, f"s_cc{i}") for i in range(6)]
        cA = Ctr(ctx, nc, "cA")
        cB = Ctr(ctx, nc, "cB")
        cC = Ctr(ctx, nc, "cC")
        cD = Ctr(ctx, nc, "cD")
        cE = Ctr(ctx, nc, "cE")
        cF = Ctr(ctx, nc, "cF")
        cG = Ctr(ctx, nc, "cG")
        cH = Ctr(ctx, nc, "cH")
        cI = Ctr(ctx, nc, "cI")
        cJ = Ctr(ctx, nc, "cJ")
        s_kv = Ctr(ctx, nc, "s_kv")
        s_ss = Ctr(ctx, nc, "s_ss")
        s_se = Ctr(ctx, nc, "s_se")
        s_sm = Ctr(ctx, nc, "s_sm")
        s_sa = Ctr(ctx, nc, "s_sa")
        s_hd1 = Ctr(ctx, nc, "s_hd1")
        s_hd2 = Ctr(ctx, nc, "s_hd2")
        s_hd3 = Ctr(ctx, nc, "s_hd3")
        s_hpe = Ctr(ctx, nc, "s_hpe")

        with nc.Block("init") as blk:
            @blk.sync
            def _(sync):
                s_ld.inc(sync.dma_start(
                    out=X[:, :].rearrange("p (t d) -> p t d", d=D),
                    in_=x_in.rearrange("(t p) d -> p t d", p=128)), 16)
                s_ld.inc(sync.dma_start(out=ident[:, :], in_=ident_in), 16)

            @blk.vector
            def _(vector):
                vector.memset(ones_col_bf[:, :], 1.0)
                vector.memset(eps_col[:, :], 1e-5)
                vector.memset(ones_row_f[:, :], 1.0)

        def phase_ln(tag):
            with ExitStack() as pctx, nc.Block("ln_" + tag) as blk:
                P = pctx.enter_context
                nst = P(sb([128, 6 * NT], F32))
                sqd = P(sb([128, D], BF16))

                def sl(j, k):
                    return nst[:, 6 * j + k:6 * j + k + 1]

                Ba = cA.v
                Ba2 = cB.v
                Bd = cC.v
                Bd2 = cD.v
                Bp = cE.v
                Bev = cF.v

                @blk.scalar
                def _(scalar):
                    scalar.wait_ge(s_ld.sem, 32)
                    for j in range(NT):
                        cA.inc(scalar.activation(
                            sqd[:, :], X[:, j * D:(j + 1) * D], AF.Square,
                            accum_out=sl(j, 1)))
                    for j in range(NT):
                        scalar.wait_ge(cC.sem, Bd + j + 1)
                        cB.inc(scalar.activation(
                            sl(j, 4), sl(j, 3), AF.Sqrt, bias=eps_col[:, :]))

                @blk.vector
                def _(vector):
                    vector.wait_ge(s_ld.sem, 32)
                    for j in range(NT):
                        xj = X[:, j * D:(j + 1) * D]
                        h = cH.inc(vector.reduce_sum(sl(j, 0), xj, axis=AX,
                                                     negate=True))
                        vector.wait_ge(cH.sem, h)
                        h = cH.inc(vector.tensor_scalar(
                            sl(j, 0), sl(j, 0), 1.0 / D, None, op0=ALU.mult))
                        vector.wait_ge(cH.sem, h)
                        h = cH.inc(vector.tensor_tensor(
                            sl(j, 2), sl(j, 0), sl(j, 0), ALU.mult))
                        vector.wait_ge(cH.sem, h)
                        vector.wait_ge(cA.sem, Ba + j + 1)
                        cC.inc(vector.scalar_tensor_tensor(
                            sl(j, 3), sl(j, 1), 1.0 / D, sl(j, 2),
                            op0=ALU.mult, op1=ALU.subtract))
                    xv = xnT[:, :].rearrange("p (d t) -> p d t", t=T)

                    def evac(j):
                        vector.wait_ge(cE.sem, Bp + 2 * j + 1)
                        cF.inc(vector.tensor_copy(
                            xv[:, 0:8, j * 128:(j + 1) * 128],
                            psC[:, :].rearrange("p (a b) -> p a b", b=128)))
                        vector.wait_ge(cE.sem, Bp + 2 * j + 2)
                        cF.inc(vector.tensor_copy(
                            xv[:, 8:16, j * 128:(j + 1) * 128],
                            psD[:, :].rearrange("p (a b) -> p a b", b=128)))

                    for j in range(NT):
                        xj = X[:, j * D:(j + 1) * D]
                        vector.wait_ge(cB.sem, Ba2 + j + 1)
                        h = cH.inc(vector.reciprocal(sl(j, 5), sl(j, 4)))
                        vector.wait_ge(cH.sem, h)
                        if j >= 2:
                            vector.wait_ge(cE.sem, Bp + 2 * (j - 2) + 2)
                        cD.inc(vector.tensor_scalar(
                            scr[j % 2][:, :], xj, sl(j, 0), sl(j, 5),
                            op0=ALU.add, op1=ALU.mult))
                        if j >= 1:
                            evac(j - 1)
                    evac(NT - 1)

                @blk.tensor
                def _(tensor):
                    for j in range(NT):
                        tensor.wait_ge(cD.sem, Bd2 + j + 1)
                        if j >= 1:
                            tensor.wait_ge(cF.sem, Bev + 2 * j)
                        for d in range(8):
                            mm = tensor.transpose(
                                psC[:, d * 128:(d + 1) * 128],
                                scr[j % 2][:, d * 128:(d + 1) * 128],
                                ident[:, :])
                        cE.inc(mm)
                        for d in range(8, 16):
                            mm = tensor.transpose(
                                psD[:, (d - 8) * 128:(d - 7) * 128],
                                scr[j % 2][:, d * 128:(d + 1) * 128],
                                ident[:, :])
                        cE.inc(mm)

        def phase_proj(l, w, NP, kind):
            rope = (l == 0)
            qz = (l == 1 and kind == "q")
            memdbg(f"proj_{kind}_l{l}")
            with ExitStack() as pctx, nc.Block(f"proj_{kind}_l{l}") as blk:
                P = pctx.enter_context
                wbf = [P(sb([128, D], BF16)) for _ in range(3)]
                kts = [P(sb([128, T], BF16)) for _ in range(2)]
                sqb = P(sb([128, T], BF16))
                qf32 = [scr[0][:, 0:1024], scr[1][:, 0:1024]]
                ps0 = [psA, psC]
                ps1 = [psB, psD]
                consts = []
                if kind == "k":
                    if rope:
                        consts = [(cosf_in, lay["cosf"]), (sinf_in, lay["sinf"]),
                                  (shift_in, lay["shiftm"])]
                    if l == 1:
                        consts = [(gq_in, lay["gqt"]), (qzb_in, lay["qzb"]),
                                  (gkc_in, lay["gkc"])]
                if rope:
                    cosf, sinf, shiftm = (lay["cosf"], lay["sinf"],
                                          lay["shiftm"])
                if qz:
                    qzb = lay["qzb"]
                Bl = s_ld.v
                Bldp = s_ldp.v
                Bp1 = cB.v
                Bqf = cC.v
                Broll = cD.v
                Bdst = cE.v
                Bsq = cF.v
                Bnm = cG.v
                Bnr = cI.v
                Bkd = s_st.v
                NCST = len(consts)
                if kind == "k":
                    lay["c_ld"] = s_ld.v + 16 * NCST

                @blk.sync
                def _(sync):
                    for csrc, cdst in consts:
                        s_ld.inc(sync.dma_start(out=cdst[:, :], in_=csrc), 16)
                    if kind == "k":
                        for i in range(NP):
                            sync.wait_ge(cE.sem, Bdst + i + 1)
                            s_st.inc(sync.dma_start(
                                out=kin[l].ap()[:, i * T:(i + 1) * T],
                                in_=kts[i % 2][:, :]), 16)

                @blk.gpsimd
                def _(gpsimd):
                    for i in range(NP):
                        if i >= 3:
                            gpsimd.wait_ge(cB.sem, Bp1 + i - 2)
                        s_ldp.inc(gpsimd.dma_start(
                            out=wbf[i % 3][:, :],
                            in_=w.rearrange("(fc p) k -> fc p k", p=128)[i]), 16)
                    if kind == "q":
                        gpsimd.wait_ge(s_cc[3 * l + 2].sem, 1)
                        s_ldp.inc(gpsimd.dma_start(
                            out=nrow[0:1, 48:48 + (16 if l else 4)],
                            in_=kmg[l].ap()), 16)

                @blk.tensor
                def _(tensor):
                    if rope or qz:
                        tensor.wait_ge(s_ld.sem, lay.get("c_ld", 0))
                    for i in range(NP):
                        p0, p1 = ps0[i % 2], ps1[i % 2]
                        tensor.wait_ge(s_ldp.sem, Bldp + 16 * (i + 1))
                        if i >= 2:
                            if rope:
                                tensor.wait_ge(cC.sem, Bqf + i - 1)
                            else:
                                tensor.wait_ge(cE.sem, Bdst + i - 1)
                        for d in range(ND):
                            for hf in range(2):
                                mm = tensor.matmul(
                                    p0[:, hf * 512:(hf + 1) * 512],
                                    wbf[i % 3][:, d * 128:(d + 1) * 128],
                                    xnT[:, d * T + hf * 512:d * T + (hf + 1) * 512],
                                    start=(d == 0), stop=(d == ND - 1),
                                    skip_group_check=True)
                        cB.inc(mm)
                        if rope:
                            tensor.wait_ge(cC.sem, Bqf + i + 1)
                            if i >= 2:
                                tensor.wait_ge(cI.sem, Bnr + i - 1)
                            for hf in range(2):
                                mm = tensor.matmul(
                                    p1[:, hf * 512:(hf + 1) * 512],
                                    shiftm[:, :],
                                    qf32[i % 2][:, hf * 512:(hf + 1) * 512],
                                    start=True, stop=True,
                                    skip_group_check=True)
                            cD.inc(mm)
                        tensor.wait_ge(cF.sem, Bsq + i + 1)
                        if not rope and i >= 2:
                            tensor.wait_ge(cI.sem, Bnr + i - 1)
                        for hf in range(2):
                            mm = tensor.matmul(
                                p1[0:1, hf * 512:(hf + 1) * 512],
                                ones_col_bf[:, :],
                                sqb[:, hf * 512:(hf + 1) * 512],
                                start=True, stop=True, skip_group_check=True)
                        cG.inc(mm)
                    if kind == "q":
                        tensor.wait_ge(cC.sem,
                                       Bqf + (NP if rope else 0) + 1)
                        cB.inc(tensor.matmul(psA[:, 0:16], ones_row_f[:, :],
                                             nrow[0:1, 96:112], start=True,
                                             stop=True, skip_group_check=True))

                @blk.vector
                def _(vector):
                    if rope or qz:
                        vector.wait_ge(s_ld.sem, lay.get("c_ld", 0))
                    for i in range(NP):
                        p0, p1 = ps0[i % 2], ps1[i % 2]
                        dest = (QT[:, i * T:(i + 1) * T] if kind == "q"
                                else kts[i % 2][:, :])
                        vector.wait_ge(cB.sem, Bp1 + i + 1)
                        if kind == "k" and i >= 2:
                            vector.wait_ge(s_st.sem, Bkd + 16 * (i - 1))
                        if i >= 1:
                            vector.wait_ge(cG.sem, Bnm + i)
                        if rope:
                            h = cC.inc(vector.tensor_copy(qf32[i % 2][:, :],
                                                          p0[:, :]))
                            vector.wait_ge(cD.sem, Broll + i + 1)
                            vector.wait_ge(cC.sem, h)
                            h = cH.inc(vector.tensor_tensor(
                                scr[2][:, 0:1024], qf32[i % 2][:, :],
                                cosf[:, :], ALU.mult))
                            vector.wait_ge(cH.sem, h)
                            h = cH.inc(vector.tensor_tensor(
                                scr[2][:, 1024:2048], p1[:, :], sinf[:, :],
                                ALU.mult))
                            vector.wait_ge(cH.sem, h)
                            cE.inc(vector.tensor_tensor(
                                dest, scr[2][:, 0:1024], scr[2][:, 1024:2048],
                                ALU.add))
                        else:
                            if qz:
                                cE.inc(vector.tensor_tensor(
                                    dest, p0[:, :], qzb[:, :], ALU.mult))
                            else:
                                cE.inc(vector.tensor_copy(dest, p0[:, :]))
                        vector.wait_ge(cE.sem, Bdst + i + 1)
                        cF.inc(vector.tensor_tensor(sqb[:, :], dest, dest,
                                                    ALU.mult))
                        vector.wait_ge(cG.sem, Bnm + i + 1)
                        slot = (nrow[0:1, i:i + 1] if kind == "q"
                                else nrow[0:1, 32 + i:32 + i + 1])
                        cI.inc(vector.reduce_max(slot, p1[0:1, :], axis=AX))
                    if kind == "q":
                        HKV = 16 if l == 1 else 4
                        vector.wait_ge(s_ldp.sem, s_ldp.v)
                        vector.wait_ge(cI.sem, Bnr + NP)
                        if HKV == 16:
                            h = cH.inc(vector.tensor_tensor(
                                nrow[0:1, 64:80], nrow[0:1, 0:16],
                                nrow[0:1, 48:64], ALU.mult))
                        else:
                            h = cH.inc(vector.tensor_tensor(
                                nrow[0:1, 64:80].rearrange(
                                    "p (a b) -> p a b", b=4),
                                nrow[0:1, 0:16].rearrange(
                                    "p (a b) -> p a b", b=4),
                                nrow[0:1, 48:52].rearrange(
                                    "p (a b) -> p a b", b=1)
                                .broadcast_to([1, 4, 4]), ALU.mult))
                        vector.wait_ge(cH.sem, h)
                        vector.wait_ge(cA.sem, cA.v + 1)
                        cC.inc(vector.tensor_scalar(
                            nrow[0:1, 96:112], nrow[0:1, 80:96], -1.0, None,
                            op0=ALU.mult))
                        vector.wait_ge(cB.sem, cB.v)
                        cD.inc(vector.tensor_copy(negc[l][:, :], psA[:, 0:16]))

                @blk.scalar
                def _(scalar):
                    if kind == "q":
                        scalar.wait_ge(cH.sem, cH.v)
                        cA.inc(scalar.activation(nrow[0:1, 80:96],
                                                 nrow[0:1, 64:80], AF.Sqrt,
                                                 scale=1.0 / 128.0))

        def phase_v_cached(l, wv, HKV, knrow_hkv):
            FW = HKV * HD
            memdbg(f"v_l{l}")
            with ExitStack() as pctx, nc.Block(f"v_l{l}") as blk:
                P = pctx.enter_context
                wvbf = P(sb([128, ND * FW], BF16))
                vstg = [P(sb([128, FW], BF16)) for _ in range(2)]
                Bld = s_ldp.v
                Bmm = cB.v
                Bev = cC.v
                Bst = s_st.v

                @blk.gpsimd
                def _(gpsimd):
                    for d in range(ND):
                        s_ldp.inc(gpsimd.dma_start(
                            out=wvbf[:, d * FW:(d + 1) * FW],
                            in_=wv[d * 128:(d + 1) * 128, :]), 16)
                    gpsimd.wait_ge(s_st.sem, Bst + 16 * NT)
                    s_cc[3 * l + 0].inc(gpsimd.collective_compute(
                        "AllGather", ALU.bypass,
                        replica_groups=[list(range(NCORES))],
                        ins=[kin[l].ap().opt()], outs=[kg[l].ap().opt()]))
                    s_cc[3 * l + 1].inc(gpsimd.collective_compute(
                        "AllGather", ALU.bypass,
                        replica_groups=[list(range(NCORES))],
                        ins=[vin[l].ap().opt()], outs=[vg[l].ap().opt()]))
                    s_ldp.inc(gpsimd.dma_start(
                        out=kml[l].ap(), in_=nrow[0:1, 32:32 + knrow_hkv]), 16)
                    gpsimd.wait_ge(s_ldp.sem, s_ldp.v)
                    s_cc[3 * l + 2].inc(gpsimd.collective_compute(
                        "AllReduce", ALU.max,
                        replica_groups=[list(range(NCORES))],
                        ins=[kml[l].ap().opt()], outs=[kmg[l].ap().opt()]))

                @blk.tensor
                def _(tensor):
                    tensor.wait_ge(s_ldp.sem, Bld + 16 * ND)
                    for t in range(NT):
                        p0 = [psA, psB][t % 2]
                        if t >= 2:
                            tensor.wait_ge(cC.sem, Bev + t - 1)
                        for d in range(ND):
                            mm = tensor.matmul(
                                p0[:, 0:FW],
                                xnT[:, d * T + t * 128:d * T + (t + 1) * 128],
                                wvbf[:, d * FW:(d + 1) * FW],
                                start=(d == 0), stop=(d == ND - 1),
                                skip_group_check=True)
                        cB.inc(mm)

                @blk.vector
                def _(vector):
                    for t in range(NT):
                        p0 = [psA, psB][t % 2]
                        vector.wait_ge(cB.sem, Bmm + t + 1)
                        if t >= 2:
                            vector.wait_ge(s_st.sem, Bst + 16 * (t - 1))
                        cC.inc(vector.tensor_copy(vstg[t % 2][:, :],
                                                  p0[:, 0:FW]))

                @blk.sync
                def _(sync):
                    for t in range(NT):
                        sync.wait_ge(cC.sem, Bev + t + 1)
                        s_st.inc(sync.dma_start(
                            out=vin[l].ap()[t * 128:(t + 1) * 128, :],
                            in_=vstg[t % 2][:, :]), 16)

        def phase_v_fmajor(l, wv, knrow_hkv):
            memdbg(f"v_l{l}")
            with ExitStack() as pctx, nc.Block(f"v_l{l}") as blk:
                P = pctx.enter_context
                wbf = [P(sb([128, D], BF16)) for _ in range(3)]
                vstg = [P(sb([128, NT * 128], BF16)) for _ in range(2)]
                Bld = s_ldp.v
                Bp1 = cB.v
                Bf = cC.v
                Btr = cD.v
                Bbe = cE.v
                Bst = s_st.v

                @blk.gpsimd
                def _(gpsimd):
                    for fc in range(16):
                        if fc >= 3:
                            gpsimd.wait_ge(cB.sem, Bp1 + fc - 2)
                        s_ldp.inc(gpsimd.dma_start(
                            out=wbf[fc % 3][:, :],
                            in_=wv.rearrange("(fc p) k -> fc p k", p=128)[fc]),
                            16)
                    gpsimd.wait_ge(s_st.sem, Bst + 16 * 16)
                    s_cc[3 * l + 0].inc(gpsimd.collective_compute(
                        "AllGather", ALU.bypass,
                        replica_groups=[list(range(NCORES))],
                        ins=[kin[l].ap().opt()], outs=[kg[l].ap().opt()]))
                    s_cc[3 * l + 1].inc(gpsimd.collective_compute(
                        "AllGather", ALU.bypass,
                        replica_groups=[list(range(NCORES))],
                        ins=[vin[l].ap().opt()], outs=[vg[l].ap().opt()]))
                    s_ldp.inc(gpsimd.dma_start(
                        out=kml[l].ap(), in_=nrow[0:1, 32:32 + knrow_hkv]), 16)
                    gpsimd.wait_ge(s_ldp.sem, s_ldp.v)
                    s_cc[3 * l + 2].inc(gpsimd.collective_compute(
                        "AllReduce", ALU.max,
                        replica_groups=[list(range(NCORES))],
                        ins=[kml[l].ap().opt()], outs=[kmg[l].ap().opt()]))

                @blk.tensor
                def _(tensor):
                    for fc in range(16):
                        p0 = [psA, psC][fc % 2]
                        p1 = [psB, psD][fc % 2]
                        tensor.wait_ge(s_ldp.sem, Bld + 16 * (fc + 1))
                        if fc >= 2:
                            tensor.wait_ge(cC.sem, Bf + fc - 1)
                        for d in range(ND):
                            for hf in range(2):
                                mm = tensor.matmul(
                                    p0[:, hf * 512:(hf + 1) * 512],
                                    wbf[fc % 3][:, d * 128:(d + 1) * 128],
                                    xnT[:, d * T + hf * 512:d * T + (hf + 1) * 512],
                                    start=(d == 0), stop=(d == ND - 1),
                                    skip_group_check=True)
                        cB.inc(mm)
                        tensor.wait_ge(cC.sem, Bf + fc + 1)
                        if fc >= 2:
                            tensor.wait_ge(cE.sem, Bbe + fc - 1)
                        for t in range(NT):
                            mm = tensor.transpose(
                                p1[:, t * 128:(t + 1) * 128],
                                scr[2][:, t * 128:(t + 1) * 128],
                                ident[:, :])
                        cD.inc(mm)

                @blk.vector
                def _(vector):
                    for fc in range(16):
                        p0 = [psA, psC][fc % 2]
                        p1 = [psB, psD][fc % 2]
                        vector.wait_ge(cB.sem, Bp1 + fc + 1)
                        if fc >= 1:
                            vector.wait_ge(cD.sem, Btr + fc)
                        h = cC.inc(vector.tensor_copy(scr[2][:, 0:1024],
                                                      p0[:, :]))
                        vector.wait_ge(cC.sem, h)
                        vector.wait_ge(cD.sem, Btr + fc + 1)
                        if fc >= 2:
                            vector.wait_ge(s_st.sem, Bst + 16 * (fc - 1))
                        cE.inc(vector.tensor_copy(vstg[fc % 2][:, :], p1[:, :]))

                @blk.sync
                def _(sync):
                    for fc in range(16):
                        sync.wait_ge(cE.sem, Bbe + fc + 1)
                        s_st.inc(sync.dma_start(
                            out=vin[l].ap().rearrange("(t p) f -> p t f", p=128)
                            [:, :, fc * 128:(fc + 1) * 128],
                            in_=vstg[fc % 2][:, :].rearrange(
                                "p (t f) -> p t f", f=128)), 16)

        def phase_flash(l, H, HKV, mask):
            C = NKC
            memdbg(f"flash_l{l}")
            with ExitStack() as pctx, nc.Block(f"flash_l{l}") as blk:
                P = pctx.enter_context
                NKV = 4
                kbuf = [P(sb([128, 128], BF16)) for _ in range(NKV)]
                vbuf = [P(sb([128, 129], BF16)) for _ in range(NKV)]
                NPB = 5 if mask else 4
                pbuf = [P(sb([128, T], BF16)) for _ in range(NPB)]
                rec8 = P(sb([128, 8], F32))
                dmy = nrow[0:1, 256:264]
                Bkv = s_kv.v
                Bss = s_ss.v
                Bse = s_se.v
                Bsm = s_sm.v
                Bsa = s_sa.v
                B1 = s_hd1.v
                B3 = s_hd3.v
                Bpe = s_hpe.v
                kgl = kg[l].ap()
                vgl = vg[l].ap()
                psS = [psA, psB]

                def kv_idx(qh, c):
                    kvh = qh // (H // HKV)
                    return (kgl[(c // 8) * 128:(c // 8 + 1) * 128,
                                kvh * T + (c % 8) * 128:
                                kvh * T + (c % 8 + 1) * 128],
                            vgl[c * 128:(c + 1) * 128,
                                kvh * 128:(kvh + 1) * 128])

                def psO(qc):
                    p = psC if qc < 4 else psD
                    return p[:, (qc % 4) * 256:(qc % 4) * 256 + 129]

                @blk.sync
                def _(sync):
                    sync.wait_ge(s_cc[3 * l + 0].sem, 1)
                    sync.wait_ge(s_cc[3 * l + 1].sem, 1)
                    for g in range(H * C):
                        qh, c = divmod(g, C)
                        if g >= NKV:
                            sync.wait_ge(s_sa.sem, Bsa + g - NKV + 1)
                        ksrc, vsrc = kv_idx(qh, c)
                        s_kv.inc(sync.dma_start(out=kbuf[g % NKV][:, :],
                                                in_=ksrc), 16)
                        s_kv.inc(sync.dma_start(out=vbuf[g % NKV][:, 0:128],
                                                in_=vsrc), 16)

                @blk.vector
                def _(vector):
                    if mask:
                        vector.wait_ge(s_ld.sem, lay.get("c_ld", 0))
                    for b in range(NKV):
                        inst = vector.memset(vbuf[b][:, 128:129], 1.0)
                    cJ.inc(inst)

                    def dve_headend(qh):
                        vector.wait_ge(s_sa.sem, Bsa + (qh + 1) * C)
                        h = cH.inc(vector.reciprocal(
                            rec8[:, 0:4].rearrange("p (a b) -> p a b", b=1),
                            psC[:, :].rearrange("p (a b) -> p a b", b=256)
                            [:, :, 128:129]))
                        h = cH.inc(vector.reciprocal(
                            rec8[:, 4:8].rearrange("p (a b) -> p a b", b=1),
                            psD[:, :].rearrange("p (a b) -> p a b", b=256)
                            [:, :, 128:129]))
                        vector.wait_ge(cH.sem, h)
                        for qc in range(8):
                            p = psC if qc < 4 else psD
                            inst2 = vector.tensor_scalar(
                                scr[0][:, qc * 128:(qc + 1) * 128],
                                p[:, (qc % 4) * 256:(qc % 4) * 256 + 128],
                                rec8[:, qc:qc + 1], None, op0=ALU.mult)
                        s_hd1.inc(inst2)
                        vector.wait_ge(s_hpe.sem, Bpe + qh + 1)
                        s_hd3.inc(vector.tensor_copy(
                            OTn[:, qh * T:(qh + 1) * T], psC[:, :]))

                    pend = None
                    for g in range(H * C):
                        qh, c = divmod(g, C)
                        if mask:
                            vector.wait_ge(s_se.sem, Bse + g + 2)
                            s_sm.inc(vector.scalar_tensor_tensor(
                                pbuf[g % NPB][:, :], lay["gqt"][:, :],
                                lay["gkc"][:, c:c + 1], pbuf[g % NPB][:, :],
                                op0=ALU.is_lt, op1=ALU.mult))
                            if g == H * C - 1:
                                s_sm.inc(vector.scalar_tensor_tensor(
                                    dmy[0:1, 1:2], lay["gqt"][0:1, 0:1],
                                    lay["gkc"][0:1, 0:1], dmy[0:1, 1:2],
                                    op0=ALU.is_lt, op1=ALU.mult))
                        if pend is not None:
                            dve_headend(pend)
                            pend = None
                        if c == C - 1:
                            pend = qh
                    dve_headend(pend)

                @blk.tensor
                def _(tensor):
                    def emit_S(g):
                        qh, c = divmod(g, C)
                        tensor.wait_ge(s_kv.sem, Bkv + 32 * (g + 1))
                        if g == 0:
                            tensor.wait_ge(cJ.sem, cJ.v)
                        if g >= 2:
                            tensor.wait_ge(s_se.sem, Bse + g - 1)
                        for hf in range(2):
                            mm = tensor.matmul(
                                psS[g % 2][:, hf * 512:(hf + 1) * 512],
                                kbuf[g % NKV][:, :],
                                QT[:, qh * T + hf * 512:qh * T + (hf + 1) * 512],
                                start=True, stop=True, skip_group_check=True)
                        s_ss.inc(mm)

                    def emit_AV(g):
                        qh, c = divmod(g, C)
                        if mask:
                            tensor.wait_ge(s_sm.sem, Bsm + g + 2)
                        else:
                            tensor.wait_ge(s_se.sem, Bse + g + 2)
                        if c == 0 and qh >= 1:
                            tensor.wait_ge(s_hd3.sem, B3 + qh)
                        for qc in range(8):
                            mm = tensor.matmul(
                                psO(qc),
                                pbuf[g % NPB][:, qc * 128:(qc + 1) * 128],
                                vbuf[g % NKV][:, :],
                                start=(c == 0 and qc % 2 == 0),
                                stop=(c == C - 1),
                                skip_group_check=True)
                        s_sa.inc(mm)

                    def emit_headend(p):
                        tensor.wait_ge(s_hd1.sem, B1 + p + 1)
                        for qc in range(8):
                            mm = tensor.transpose(
                                psC[:, qc * 128:(qc + 1) * 128],
                                scr[0][:, qc * 128:(qc + 1) * 128],
                                ident[:, :])
                        s_hpe.inc(mm)

                    pend_he = None
                    for g in range(H * C):
                        emit_S(g)
                        if pend_he is not None:
                            emit_headend(pend_he)
                            pend_he = None
                        if g >= 2:
                            emit_AV(g - 2)
                            if (g - 2) % C == C - 1:
                                pend_he = (g - 2) // C
                    emit_AV(H * C - 2)
                    emit_AV(H * C - 1)
                    emit_headend(H - 1)

                @blk.scalar
                def _(scalar):
                    for g in range(H * C):
                        qh, c = divmod(g, C)
                        scalar.wait_ge(s_ss.sem, Bss + g + 1)
                        if g >= NPB:
                            scalar.wait_ge(s_sa.sem, Bsa + g - NPB + 1)
                        s_se.inc(scalar.activation(
                            pbuf[g % NPB][:, :], psS[g % 2][:, :], AF.Exp,
                            bias=negc[l][:, qh:qh + 1], scale=ISQ))
                    s_se.inc(scalar.activation(
                        dmy[0:1, 0:1], psS[0][0:1, 0:1], AF.Exp,
                        bias=negc[l][0:1, 0:1], scale=ISQ))

        def phase_wo(l, wo):
            memdbg(f"wo_l{l}")
            with ExitStack() as pctx, nc.Block(f"wo_l{l}") as blk:
                P = pctx.enter_context
                wobf = QT
                Bld = s_ldp.v
                Bmm = cB.v
                Bx = cC.v

                @blk.gpsimd
                def _(gpsimd):
                    for grp in range(2):
                        if grp == 1:
                            gpsimd.wait_ge(cB.sem, Bmm + 8)
                        for h8 in range(8):
                            h = grp * 8 + h8
                            s_ldp.inc(gpsimd.dma_start(
                                out=wobf[:, h8 * D:(h8 + 1) * D],
                                in_=wo[h * 128:(h + 1) * 128, :]), 16)

                @blk.tensor
                def _(tensor):
                    for grp in range(2):
                        tensor.wait_ge(s_ldp.sem, Bld + 16 * 8 * (grp + 1))
                        for t in range(NT):
                            pl, ph = [(psA, psB), (psC, psD)][t % 2]
                            if grp * NT + t >= 2:
                                tensor.wait_ge(cC.sem, Bx + grp * NT + t - 1)
                            for h8 in range(8):
                                h = grp * 8 + h8
                                for hf in range(4):
                                    p = pl if hf < 2 else ph
                                    mm = tensor.matmul(
                                        p[:, (hf % 2) * 512:(hf % 2 + 1) * 512],
                                        OTn[:, h * T + t * 128:h * T + (t + 1) * 128],
                                        wobf[:, h8 * D + hf * 512:
                                             h8 * D + (hf + 1) * 512],
                                        start=(h8 == 0), stop=(h8 == 7),
                                        skip_group_check=True)
                            cB.inc(mm)

                @blk.vector
                def _(vector):
                    for grp in range(2):
                        for t in range(NT):
                            pl, ph = [(psA, psB), (psC, psD)][t % 2]
                            vector.wait_ge(cB.sem, Bmm + grp * NT + t + 1)
                            h = cH.inc(vector.tensor_tensor(
                                X[:, t * D:t * D + 1024],
                                X[:, t * D:t * D + 1024], pl[:, :], ALU.add))
                            vector.wait_ge(cH.sem, h)
                            cC.inc(vector.tensor_tensor(
                                X[:, t * D + 1024:(t + 1) * D],
                                X[:, t * D + 1024:(t + 1) * D], ph[:, :],
                                ALU.add))

        def phase_ffn0():
            memdbg("ffn0")
            with ExitStack() as pctx, nc.Block("ffn0") as blk:
                P = pctx.enter_context
                dsb = P(sb([128, D], F32))
                gs_sb = P(sb([128, NIC], F32))
                us_sb = P(sb([128, NIC], F32))
                xadd = scr[2][:, 1024:2048]
                h_sb = P(sb([128, NIC * 512], BF16))
                gbf = [P(sb([128, D], BF16)) for _ in range(2)]
                ubf = [P(sb([128, D], BF16)) for _ in range(2)]
                dbf = [P(sb([128, 1024], BF16))
                       for _ in range(2)]
                silu = [P(sb([128, 512], BF16))
                        for _ in range(2)]
                usb = [P(sb([128, 512], BF16))
                       for _ in range(2)]
                Bl = s_ld.v
                Bc = cA.v
                Bgu = cB.v
                Bsil = cC.v
                Buev = cD.v
                Bh = cE.v
                Bxa = cF.v
                Bdn = cG.v
                Bdc = cI.v
                Bdl = s_ldp.v
                BJ = cJ.v

                @blk.sync
                def _(sync):
                    cJ.inc(sync.dma_start(out=dsb[:, :], in_=ds_b), 16)
                    cJ.inc(sync.dma_start(out=gs_sb[:, :], in_=gs_c), 16)
                    cJ.inc(sync.dma_start(out=us_sb[:, :], in_=us_c), 16)
                    for hh in range(2):
                        for ic in range(NIC):
                            i = hh * NIC + ic
                            if i >= 1:
                                sync.wait_ge(cA.sem, Bc + i)
                            s_ld.inc(sync.dma_start(
                                out=scr[0][:, :],
                                in_=gwT.rearrange("(ic p) k -> ic p k", p=128)
                                [ic]), 16)
                            s_ld.inc(sync.dma_start(
                                out=scr[1][:, :],
                                in_=uwT.rearrange("(ic p) k -> ic p k", p=128)
                                [ic]), 16)

                @blk.gpsimd
                def _(gpsimd):
                    for hh in range(2):
                        for dmh in range(2):
                            for ic in range(NIC):
                                j = (hh * 2 + dmh) * NIC + ic
                                if j >= 1:
                                    gpsimd.wait_ge(cI.sem, Bdc + j)
                                s_ldp.inc(gpsimd.dma_start(
                                    out=scr[2][:, 0:1024],
                                    in_=dwT[ic * 128:(ic + 1) * 128,
                                            dmh * 1024:(dmh + 1) * 1024]), 16)

                @blk.scalar
                def _(scalar):
                    for hh in range(2):
                        for ic in range(NIC):
                            i = hh * NIC + ic
                            scalar.wait_ge(s_ld.sem, Bl + 32 * (i + 1))
                            if i >= 2:
                                scalar.wait_ge(cB.sem, Bgu + i - 1)
                            scalar.copy(gbf[i % 2][:, :], scr[0][:, :])
                            cA.inc(scalar.copy(ubf[i % 2][:, :], scr[1][:, :]))
                            scalar.wait_ge(cB.sem, Bgu + i + 1)
                            if i == 0:
                                scalar.wait_ge(cJ.sem, BJ + 48)
                            psGU = [psA, psB][i % 2]
                            cC.inc(scalar.activation(
                                silu[i % 2][:, :], psGU[:, 0:512], AF.Silu,
                                scale=gs_sb[:, ic:ic + 1]))
                        for dmh in range(2):
                            for ic in range(NIC):
                                j = (hh * 2 + dmh) * NIC + ic
                                scalar.wait_ge(s_ldp.sem, Bdl + 16 * (j + 1))
                                if j >= 2:
                                    scalar.wait_ge(cG.sem, Bdn + j - 1)
                                cI.inc(scalar.copy(dbf[j % 2][:, :],
                                                   scr[2][:, 0:1024]))

                @blk.tensor
                def _(tensor):
                    for hh in range(2):
                        for ic in range(NIC):
                            i = hh * NIC + ic
                            psGU = [psA, psB][i % 2]
                            tensor.wait_ge(cA.sem, Bc + i + 1)
                            if i >= 2:
                                tensor.wait_ge(cC.sem, Bsil + i - 1)
                                tensor.wait_ge(cD.sem, Buev + i - 1)
                            if ic == 0 and hh == 1:
                                tensor.wait_ge(cF.sem, Bxa + 8)
                            for d in range(ND):
                                rhs = xnT[:, d * T + hh * 512:
                                          d * T + (hh + 1) * 512]
                                tensor.matmul(
                                    psGU[:, 0:512],
                                    gbf[i % 2][:, d * 128:(d + 1) * 128], rhs,
                                    start=(d == 0), stop=(d == ND - 1),
                                    skip_group_check=True)
                                mm = tensor.matmul(
                                    psGU[:, 512:1024],
                                    ubf[i % 2][:, d * 128:(d + 1) * 128], rhs,
                                    start=(d == 0), stop=(d == ND - 1),
                                    skip_group_check=True)
                            cB.inc(mm)
                        tensor.wait_ge(cE.sem, Bh + (hh + 1) * NIC)
                        pstch = [psA, psB, psC, psD]
                        for dmh in range(2):
                            for ic in range(NIC):
                                j = (hh * 2 + dmh) * NIC + ic
                                tensor.wait_ge(cI.sem, Bdc + j + 1)
                                if dmh == 0 and ic == 0:
                                    tensor.wait_ge(cC.sem, Bsil + (hh + 1) * NIC)
                                    tensor.wait_ge(cD.sem, Buev + (hh + 1) * NIC)
                                if ic == 0 and (hh * 2 + dmh) >= 1:
                                    tensor.wait_ge(cF.sem, Bxa + (hh * 2 + dmh) * 4)
                                for tch in range(4):
                                    for hf in range(2):
                                        mm = tensor.matmul(
                                            pstch[tch][:, hf * 512:(hf + 1) * 512],
                                            h_sb[:, ic * 512 + tch * 128:
                                                 ic * 512 + (tch + 1) * 128],
                                            dbf[j % 2][:, hf * 512:(hf + 1) * 512],
                                            start=(ic == 0), stop=(ic == NIC - 1),
                                            skip_group_check=True)
                                cG.inc(mm)

                @blk.vector
                def _(vector):
                    vector.wait_ge(cJ.sem, BJ + 48)
                    for hh in range(2):
                        for ic in range(NIC):
                            i = hh * NIC + ic
                            psGU = [psA, psB][i % 2]
                            vector.wait_ge(cB.sem, Bgu + i + 1)
                            h = cD.inc(vector.tensor_scalar(
                                usb[i % 2][:, :], psGU[:, 512:1024],
                                us_sb[:, ic:ic + 1], None, op0=ALU.mult))
                            vector.wait_ge(cD.sem, h)
                            vector.wait_ge(cC.sem, Bsil + i + 1)
                            cE.inc(vector.tensor_tensor(
                                h_sb[:, ic * 512:(ic + 1) * 512],
                                silu[i % 2][:, :], usb[i % 2][:, :], ALU.mult))
                        pstch = [psA, psB, psC, psD]
                        for dmh in range(2):
                            for tch in range(4):
                                vector.wait_ge(cG.sem,
                                               Bdn + (hh * 2 + dmh + 1) * NIC)
                                tg = hh * 4 + tch
                                xsl = X[:, tg * D + dmh * 1024:
                                        tg * D + (dmh + 1) * 1024]
                                h = cH.inc(vector.tensor_tensor(
                                    xadd, pstch[tch][:, :],
                                    dsb[:, dmh * 1024:(dmh + 1) * 1024],
                                    ALU.mult))
                                vector.wait_ge(cH.sem, h)
                                cF.inc(vector.tensor_tensor(
                                    xsl, xsl, xadd, ALU.add))

        def phase_ffn1():
            memdbg("ffn1")
            with ExitStack() as pctx, nc.Block("ffn1") as blk:
                P = pctx.enter_context
                abf = P(sb([128, ND * 512], BF16))
                bbf = P(sb([128, 4 * D], BF16))
                wrT = P(sb([128, ND * 8], BF16))
                wrst = P(sb([8, D], F32))
                brsb = P(sb([1, 8], F32))
                p8 = P(sb([128, 8], F32))
                ow = P(sb([128, 9], F32))
                sst = P(sb([128, 4], F32))
                hmf = P(sb([128, 512], F32))
                hTb = [P(sb([128, 512], BF16))
                       for _ in range(2)]
                Bl = s_ld.v
                Bc = cA.v
                Bex = cI.v
                Bpe = cB.v
                Btr = cG.v
                BwT = cD.v
                Bsx = cC.v
                Bmk = cF.v
                Bhev = cE.v
                BxH = cH.v

                @blk.sync
                def _(sync):
                    for d in range(ND):
                        if d >= 2:
                            sync.wait_ge(cA.sem, Bc + d - 1)
                        s_ld.inc(sync.dma_start(
                            out=scr[d % 2][:, 0:512],
                            in_=a_T[d * 128:(d + 1) * 128, :]), 16)
                    for b in range(4):
                        sync.wait_ge(cA.sem, Bc + ND + b)
                        s_ld.inc(sync.dma_start(
                            out=scr[b % 2][:, :],
                            in_=b_sT[b * 128:(b + 1) * 128, :]), 16)
                    s_ld.inc(sync.dma_start(out=wrst[:, :], in_=wr_n), 16)
                    s_ld.inc(sync.dma_start(out=brsb[:, :], in_=br_r), 16)

                @blk.scalar
                def _(scalar):
                    for d in range(ND):
                        scalar.wait_ge(s_ld.sem, Bl + 16 * (d + 1))
                        cA.inc(scalar.copy(abf[:, d * 512:(d + 1) * 512],
                                           scr[d % 2][:, 0:512]))
                    for b in range(4):
                        scalar.wait_ge(s_ld.sem, Bl + 16 * (ND + b + 1))
                        cA.inc(scalar.mul(bbf[:, b * D:(b + 1) * D],
                                          scr[b % 2][:, :], 1.0 / RK))
                    for t in range(NT):
                        scalar.wait_ge(cC.sem, Bsx + 5 * t + 1)
                        cI.inc(scalar.activation(
                            p8[:, :], psB[:, 0:8], AF.Exp,
                            bias=sst[:, 0:1]))

                @blk.tensor
                def _(tensor):
                    tensor.wait_ge(s_ld.sem, Bl + 16 * (ND + 4 + 2))
                    for d in range(ND):
                        mm = tensor.transpose(
                            psC[:, d * 8:(d + 1) * 8],
                            wrst[:, d * 128:(d + 1) * 128], ident[0:8, 0:8])
                    cB.inc(mm)
                    tensor.wait_ge(cA.sem, Bc + ND + 4)
                    tensor.wait_ge(cD.sem, BwT + 1)
                    for t in range(NT):
                        if t >= 1:
                            tensor.wait_ge(cI.sem, Bex + t)
                            tensor.wait_ge(cC.sem, Bsx + 5 * t)
                        for d in range(ND):
                            tensor.matmul(
                                psB[:, 0:8],
                                xnT[:, d * T + t * 128:d * T + (t + 1) * 128],
                                wrT[:, d * 8:(d + 1) * 8],
                                start=(d == 0), stop=False,
                                skip_group_check=True)
                        mm = tensor.matmul(
                            psB[:, 0:8], ones_row_f[:, :], brsb[:, :],
                            start=False, stop=True, skip_group_check=True)
                        cB.inc(mm)
                        if t >= 1:
                            tensor.wait_ge(cF.sem, Bmk + t)
                        for d in range(ND):
                            mm = tensor.matmul(
                                psA[:, 0:512],
                                xnT[:, d * T + t * 128:d * T + (t + 1) * 128],
                                abf[:, d * 512:(d + 1) * 512],
                                start=(d == 0), stop=(d == ND - 1),
                                skip_group_check=True)
                        cB.inc(mm)
                        tensor.wait_ge(cF.sem, Bmk + t + 1)
                        if t >= 1:
                            tensor.wait_ge(cE.sem, Bhev + t)
                        for b in range(4):
                            mm = tensor.transpose(
                                psB[:, 512 + b * 128:512 + (b + 1) * 128],
                                hmf[:, b * 128:(b + 1) * 128], ident[:, :])
                        cG.inc(mm)
                        tensor.wait_ge(cE.sem, Bhev + t + 1)
                        if t >= 1:
                            tensor.wait_ge(cH.sem, BxH + 2 * t)
                        for b in range(4):
                            for hf in range(4):
                                p = psC if hf < 2 else psD
                                mm = tensor.matmul(
                                    p[:, (hf % 2) * 512:(hf % 2 + 1) * 512],
                                    hTb[t % 2][:, b * 128:(b + 1) * 128],
                                    bbf[:, b * D + hf * 512:b * D + (hf + 1) * 512],
                                    start=(b == 0), stop=(b == 3),
                                    skip_group_check=True)
                        cB.inc(mm)

                @blk.vector
                def _(vector):
                    vector.wait_ge(cB.sem, Bpe + 1)
                    cD.inc(vector.tensor_copy(wrT[:, :], psC[:, 0:ND * 8]))
                    for t in range(NT):
                        vector.wait_ge(cB.sem, Bpe + 1 + 3 * t + 1)
                        h = cC.inc(vector.reduce_max(
                            sst[:, 0:1], psB[:, 0:8], axis=AX, negate=True))
                        vector.wait_ge(cC.sem, h)
                        h = cC.inc(vector.reduce_max(
                            sst[:, 1:2], psB[:, 0:8], axis=AX))
                        vector.wait_ge(cC.sem, h)
                        vector.wait_ge(cI.sem, Bex + t + 1)
                        h = cC.inc(vector.reduce_sum(
                            sst[:, 2:3], p8[:, :], axis=AX))
                        vector.wait_ge(cC.sem, h)
                        h = cC.inc(vector.reciprocal(sst[:, 3:4], sst[:, 2:3]))
                        vector.wait_ge(cC.sem, h)
                        h = cC.inc(vector.tensor_scalar(
                            ow[:, 0:8], psB[:, 0:8], sst[:, 1:2], sst[:, 3:4],
                            op0=ALU.is_ge, op1=ALU.mult))
                        vector.wait_ge(cC.sem, h)
                        vector.wait_ge(cB.sem, Bpe + 1 + 3 * t + 2)
                        if t >= 1:
                            vector.wait_ge(cG.sem, Btr + t)
                        for e in range(NE):
                            inst = vector.tensor_scalar(
                                hmf[:, e * 64:(e + 1) * 64],
                                psA[:, e * 64:(e + 1) * 64],
                                ow[:, e:e + 1], None, op0=ALU.mult)
                        cF.inc(inst)
                        vector.wait_ge(cG.sem, Btr + t + 1)
                        if t >= 2:
                            vector.wait_ge(cB.sem, Bpe + 1 + 3 * (t - 1))
                        cE.inc(vector.tensor_copy(hTb[t % 2][:, :],
                                                  psB[:, 512:1024]))
                        vector.wait_ge(cB.sem, Bpe + 1 + 3 * t + 3)
                        cH.inc(vector.tensor_tensor(
                            X[:, t * D:t * D + 1024],
                            X[:, t * D:t * D + 1024], psC[:, :], ALU.add))
                        cH.inc(vector.tensor_tensor(
                            X[:, t * D + 1024:(t + 1) * D],
                            X[:, t * D + 1024:(t + 1) * D], psD[:, :],
                            ALU.add))

        def dump_x(dest_ap):
            with nc.Block("dump") as blk:
                @blk.sync
                def _(sync):
                    s_st.inc(sync.dma_start(
                        out=dest_ap.rearrange("(t p) d -> p t d", p=128),
                        in_=X[:, :].rearrange("p (t d) -> p t d", d=D)), 16)
                    if DEV_DUMP_KV:
                        kgo = nc.dram_tensor("kg0_tap", [NCORES * 128, 4 * T],
                                             BF16, kind="ExternalOutput").ap()
                        vgo = nc.dram_tensor("vg0_tap", [SEQ, 4 * HD], BF16,
                                             kind="ExternalOutput").ap()
                        nco = nc.dram_tensor("negc_tap", [128, 16], F32,
                                             kind="ExternalOutput").ap()
                        s_st.inc(sync.dma_start(out=kgo, in_=kg[0].ap()), 16)
                        s_st.inc(sync.dma_start(out=vgo, in_=vg[0].ap()), 16)
                        s_st.inc(sync.dma_start(out=nco, in_=negc[0][:, :]), 16)
                    sync.wait_ge(s_st.sem, s_st.v)

        def attn_layer(l):
            global QT, OTn
            with ExitStack() as actx:
                A = actx.enter_context
                lay.clear()
                if l == 1:
                    lay["gqt"] = A(nc.sbuf_tensor("gqt_s", [128, T], F32))
                    lay["gkc"] = A(nc.sbuf_tensor("gkc_s", [128, NKC], F32))
                QT = A(nc.sbuf_tensor(f"QT{l}", [128, 16 * T], BF16))
                ictx = ExitStack()
                IA = ictx.enter_context
                if l == 0:
                    lay["cosf"] = IA(nc.sbuf_tensor("cosf_s", [128, T], F32))
                    lay["sinf"] = IA(nc.sbuf_tensor("sinf_s", [128, T], F32))
                    lay["shiftm"] = IA(nc.sbuf_tensor("shiftm_s",
                                                      [128, 128], F32))
                else:
                    lay["qzb"] = IA(nc.sbuf_tensor("qzb_s", [128, T], F32))
                if l == 0:
                    phase_proj(0, wk0, 4, "k")
                    phase_v_cached(0, wv0, 4, 4)
                    phase_proj(0, wq0, 16, "q")
                    ictx.close()
                    if DEV_DUMP_KV:
                        qtap = nc.dram_tensor("qt_tap", [128, 16 * T], BF16,
                                              kind="ExternalOutput").ap()
                        with nc.Block("qtdump") as qblk:
                            @qblk.sync
                            def _(sync):
                                s_st.inc(sync.dma_start(out=qtap,
                                                        in_=QT[:, :]), 16)
                                sync.wait_ge(s_st.sem, s_st.v)
                    OTn = A(nc.sbuf_tensor(f"OTn{l}", [128, 16 * T], BF16))
                    phase_flash(0, 16, 4, False)
                    if DEV_DUMP_KV:
                        otap = nc.dram_tensor("otn_tap", [128, 16 * T], BF16,
                                              kind="ExternalOutput").ap()
                        with nc.Block("otdump") as oblk:
                            @oblk.sync
                            def _(sync):
                                s_st.inc(sync.dma_start(out=otap,
                                                        in_=OTn[:, :]), 16)
                                sync.wait_ge(s_st.sem, s_st.v)
                    phase_wo(0, wo0)
                else:
                    phase_proj(1, wk1, 16, "k")
                    phase_v_fmajor(1, wv1, 16)
                    phase_proj(1, wq1, 16, "q")
                    ictx.close()
                    OTn = A(nc.sbuf_tensor(f"OTn{l}", [128, 16 * T], BF16))
                    phase_flash(1, 16, 16, True)
                    phase_wo(1, wo1)

        phase_ln("l0a")
        attn_layer(0)
        if "l0_attn" in taps:
            dump_x(taps["l0_attn"])
        if stop_after == "l0_attn":
            dump_x(y_out)
            return nc, din
        phase_ln("l0f")
        phase_ffn0()
        if "l0_ffn" in taps:
            dump_x(taps["l0_ffn"])
        if stop_after == "l0_ffn":
            dump_x(y_out)
            return nc, din
        phase_ln("l1a")
        attn_layer(1)
        if "l1_attn" in taps:
            dump_x(taps["l1_attn"])
        if stop_after == "l1_attn":
            dump_x(y_out)
            return nc, din
        phase_ln("l1f")
        phase_ffn1()
        dump_x(y_out)
    return nc, din



def _rope_tables(pos):
    inv = 1.0 / (10000.0 ** (np.arange(0, HD, 2, dtype=np.float32) / HD))
    fr = pos[:, None].astype(np.float32) * inv[None, :]
    emb = np.concatenate([fr, fr], axis=-1)
    return np.cos(emb).astype(np.float32), np.sin(emb).astype(np.float32)


def _shift_mat():
    m = np.zeros((128, 128), dtype=np.float32)
    for k in range(128):
        m[k, (k + 1) % 128] = 1.0
    return m


def _tile_w(w, nf):
    K, F = w.shape
    a = w.reshape(K // 128, 128, nf, F // nf)
    a = a.transpose(2, 1, 0, 3)
    return np.ascontiguousarray(a.reshape(F, K))


def _core_inputs(inputs, r):
    x = np.ascontiguousarray(
        inputs["L0_attn_x"][0, r * T:(r + 1) * T, :]).astype(np.float32, copy=False)
    pos = np.arange(r * T, (r + 1) * T)
    cos, sin = _rope_tables(pos)
    cosf = np.ascontiguousarray(cos.T)
    sinf = np.ascontiguousarray(sin.T)
    gq = np.tile(pos.astype(np.float32)[None, :], (128, 1))
    qz = np.ones((128, T), dtype=np.float32)
    if r == NCORES - 1:
        gq[:, T - 1] = -1.0
        qz[:, T - 1] = 0.0
    gkcols = (np.arange(NKC, dtype=np.float32)[None, :] * 128
              + np.arange(128, dtype=np.float32)[:, None])
    gs = inputs["L0_ffn_gate_s"].astype(np.float32)
    us = inputs["L0_ffn_up_s"].astype(np.float32)
    ds = inputs["L0_ffn_down_s"].astype(np.float32)
    A = np.asarray(inputs["L1_ffn_A"], dtype=np.float32)
    B = np.asarray(inputs["L1_ffn_B"], dtype=np.float32)
    d = dict(
        x=x,
        wq0=_tile_w(np.asarray(inputs["L0_attn_Wq"], dtype=np.float32), 16),
        wk0=_tile_w(np.asarray(inputs["L0_attn_Wk"], dtype=np.float32), 4),
        wv0=np.asarray(inputs["L0_attn_Wv"], dtype=np.float32),
        wo0=np.asarray(inputs["L0_attn_Wo"], dtype=np.float32),
        gwT=_tile_w(np.asarray(inputs["L0_ffn_gate_w"], dtype=np.float32).T, NIC),
        uwT=_tile_w(np.asarray(inputs["L0_ffn_up_w"], dtype=np.float32).T, NIC),
        dwT=np.ascontiguousarray(np.asarray(inputs["L0_ffn_down_w"],
                                            dtype=np.float32).T),
        gs_c=np.ascontiguousarray(gs.reshape(NIC, 128).T),
        us_c=np.ascontiguousarray(us.reshape(NIC, 128).T),
        ds_b=np.ascontiguousarray(np.tile(ds[None, :], (128, 1))),
        wq1=_tile_w(np.asarray(inputs["L1_attn_Wq"], dtype=np.float32), 16),
        wk1=_tile_w(np.asarray(inputs["L1_attn_Wk"], dtype=np.float32), 16),
        wv1=_tile_w(np.asarray(inputs["L1_attn_Wv"], dtype=np.float32), 16),
        wo1=np.asarray(inputs["L1_attn_Wo"], dtype=np.float32),
        a_T=np.ascontiguousarray(A.reshape(NE * RK, D).T),
        b_sT=np.ascontiguousarray(B.transpose(0, 2, 1).reshape(NE * RK, D)),
        wr_n=np.asarray(inputs["L1_ffn_Wr"], dtype=np.float32),
        br_r=np.ascontiguousarray(
            np.asarray(inputs["L1_ffn_br"], dtype=np.float32)[None, :]),
        ident=np.eye(128, dtype=np.float32),
        shiftm=_shift_mat(),
        cosf=cosf, sinf=sinf, gqt=gq, qzb=qz,
        gkc=np.ascontiguousarray(gkcols),
    )
    return d


_CACHE = {}


def kernel(**inputs):
    key = "full"
    if key not in _CACHE:
        _CACHE[key] = build_program()
    nc, _ = _CACHE[key]
    in_maps = [_core_inputs(inputs, r) for r in range(NCORES)]
    res = run_bass_kernel_spmd(nc, in_maps, core_ids=list(range(NCORES)))
    out = np.concatenate([res.results[r]["y"] for r in range(NCORES)], axis=0)
    return out.reshape(1, SEQ, D).astype(np.float32)


# version 06
# speedup vs baseline: 1.8546x; 1.1819x over previous
from contextlib import ExitStack
import math
import numpy as np
import ml_dtypes
import concourse.bass as bass
import concourse.mybir as mybir
from concourse.bass_utils import run_bass_kernel_spmd

F32 = mybir.dt.float32
BF16 = mybir.dt.bfloat16
AF = mybir.ActivationFunctionType
ALU = mybir.AluOpType
AX = mybir.AxisListType.X

NCORES = 8
T = 1024
NT = 8
D = 2048
ND = 16
HD = 128
SEQ = 8192
NKC = 64
INTER = 5632
NIC = 44
NE, RK = 8, 64
ISQ = 1.0 / math.sqrt(128.0)


class Ctr:
    def __init__(self, ctx, nc, name):
        self.sem = ctx.enter_context(nc.semaphore(name))
        self.v = 0

    def inc(self, inst, k=1):
        inst.then_inc(self.sem, k)
        self.v += k
        return self.v


DEV_DUMP_KV = False


def build_program(stop_after=None, tap_x=()):
    nc = bass.Bass("TRN2", target_bir_lowering=False)
    din = {}

    def dI(name, shape, dtype=F32):
        din[name] = nc.dram_tensor(name, shape, dtype, kind="ExternalInput").ap()
        return din[name]

    x_in = dI("x", [T, D])
    wq0 = dI("wq0", [D, 16 * HD])
    wk0 = dI("wk0", [4 * HD, D])
    wv0 = dI("wv0", [D, 4 * HD])
    wo0 = dI("wo0", [16 * HD, D])
    gwT = dI("gwT", [INTER, D])
    uwT = dI("uwT", [INTER, D])
    dwT = dI("dwT", [INTER, D])
    gs_c = dI("gs_c", [128, NIC])
    us_c = dI("us_c", [128, NIC])
    ds_b = dI("ds_b", [128, D])
    wq1 = dI("wq1", [D, D])
    wk1 = dI("wk1", [D, D])
    wv1 = dI("wv1", [D, D])
    wo1 = dI("wo1", [D, D])
    a_T = dI("a_T", [D, NE * RK])
    b_sT = dI("b_sT", [NE * RK, D])
    wr_n = dI("wr_n", [NE, D])
    br_r = dI("br_r", [1, NE])
    ident_in = dI("ident", [128, 128])
    shift_in = dI("shiftm", [128, 128])
    cosf_in = dI("cosf", [128, T])
    sinf_in = dI("sinf", [128, T])
    gq_in = dI("gqt", [128, T])
    qzb_in = dI("qzb", [128, T])
    gkc_in = dI("gkc", [128, NKC])

    y_out = nc.dram_tensor("y", [T, D], F32, kind="ExternalOutput").ap()
    taps = {}
    for tname in tap_x:
        taps[tname] = nc.dram_tensor("tap_" + tname, [T, D], F32,
                                     kind="ExternalOutput").ap()

    kin = [nc.dram_tensor("kin0", [128, 4 * T], BF16),
           nc.dram_tensor("kin1", [128, 16 * T], BF16)]
    vin = [nc.dram_tensor("vin0", [T, 4 * HD], BF16),
           nc.dram_tensor("vin1", [T, 16 * HD], BF16)]
    kg = [nc.dram_tensor("kg0", [NCORES * 128, 4 * T], BF16),
          nc.dram_tensor("kg1", [NCORES * 128, 16 * T], BF16)]
    vg = [nc.dram_tensor("vg0", [SEQ, 4 * HD], BF16),
          nc.dram_tensor("vg1", [SEQ, 16 * HD], BF16)]
    kml = [nc.dram_tensor("kml0", [1, 4], F32),
           nc.dram_tensor("kml1", [1, 16], F32)]
    kmg = [nc.dram_tensor("kmg0", [1, 4], F32),
           nc.dram_tensor("kmg1", [1, 16], F32)]

    ctx = ExitStack()
    with ctx:
        E = ctx.enter_context

        X = E(nc.sbuf_tensor("X", [128, NT * D], F32))
        xnT = E(nc.sbuf_tensor("xnT", [128, ND * T], BF16))
        ident = E(nc.sbuf_tensor("ident_s", [128, 128], F32))
        ones_col_bf = E(nc.sbuf_tensor("ones_col_bf", [128, 1], BF16))
        ones_row_f = E(nc.sbuf_tensor("ones_row_f", [1, 128], F32))
        eps_col = E(nc.sbuf_tensor("eps_col", [128, 1], F32))
        negc = [E(nc.sbuf_tensor("negc0", [128, 16], F32)),
                E(nc.sbuf_tensor("negc1", [128, 16], F32))]
        nrow = E(nc.sbuf_tensor("nrow", [1, 1024], F32))
        scr = [E(nc.sbuf_tensor(f"scr{i}", [128, D], F32)) for i in range(3)]
        psA = E(nc.psum_tensor("psA", [128, 1024], F32))
        psB = E(nc.psum_tensor("psB", [128, 1024], F32))
        psC = E(nc.psum_tensor("psC", [128, 1024], F32))
        psD = E(nc.psum_tensor("psD", [128, 1024], F32))
        lay = {}
        _anon = [0]

        import os
        import concourse.bass_primitives as _bp

        def memdbg(tag):
            if os.environ.get("KMEMDBG"):
                print("MEM", tag, nc.bytes_allocated(_bp.MemorySpace.SBUF))

        def sb(shape, dtype):
            _anon[0] += 1
            return nc.sbuf_tensor(f"tt{_anon[0]}", shape, dtype)

        s_ld = Ctr(ctx, nc, "s_ld")
        s_st = Ctr(ctx, nc, "s_st")
        s_ldp = Ctr(ctx, nc, "s_ldp")
        s_cc = [Ctr(ctx, nc, f"s_cc{i}") for i in range(6)]
        cA = Ctr(ctx, nc, "cA")
        cB = Ctr(ctx, nc, "cB")
        cC = Ctr(ctx, nc, "cC")
        cD = Ctr(ctx, nc, "cD")
        cE = Ctr(ctx, nc, "cE")
        cF = Ctr(ctx, nc, "cF")
        cG = Ctr(ctx, nc, "cG")
        cH = Ctr(ctx, nc, "cH")
        cI = Ctr(ctx, nc, "cI")
        cJ = Ctr(ctx, nc, "cJ")
        s_kv = Ctr(ctx, nc, "s_kv")
        s_ss = Ctr(ctx, nc, "s_ss")
        s_se = Ctr(ctx, nc, "s_se")
        s_sm = Ctr(ctx, nc, "s_sm")
        s_sa = Ctr(ctx, nc, "s_sa")
        s_hd1 = Ctr(ctx, nc, "s_hd1")
        s_hd2 = Ctr(ctx, nc, "s_hd2")
        s_hd3 = Ctr(ctx, nc, "s_hd3")
        s_hpe = Ctr(ctx, nc, "s_hpe")

        with nc.Block("init") as blk:
            @blk.sync
            def _(sync):
                s_ld.inc(sync.dma_start(
                    out=X[:, :].rearrange("p (t d) -> p t d", d=D),
                    in_=x_in.rearrange("(t p) d -> p t d", p=128)), 16)
                s_ld.inc(sync.dma_start(out=ident[:, :], in_=ident_in), 16)

            @blk.vector
            def _(vector):
                vector.memset(ones_col_bf[:, :], 1.0)
                vector.memset(eps_col[:, :], 1e-5)
                vector.memset(ones_row_f[:, :], 1.0)

        def phase_ln(tag):
            with ExitStack() as pctx, nc.Block("ln_" + tag) as blk:
                P = pctx.enter_context
                nst = P(sb([128, 6 * NT], F32))
                sqd = P(sb([128, D], BF16))

                def sl(j, k):
                    return nst[:, 6 * j + k:6 * j + k + 1]

                Ba = cA.v
                Ba2 = cB.v
                Bd = cC.v
                Bd2 = cD.v
                Bp = cE.v
                Bev = cF.v

                @blk.scalar
                def _(scalar):
                    scalar.wait_ge(s_ld.sem, 32)
                    for j in range(NT):
                        cA.inc(scalar.activation(
                            sqd[:, :], X[:, j * D:(j + 1) * D], AF.Square,
                            accum_out=sl(j, 1)))
                    for j in range(NT):
                        scalar.wait_ge(cC.sem, Bd + j + 1)
                        cB.inc(scalar.activation(
                            sl(j, 4), sl(j, 3), AF.Sqrt, bias=eps_col[:, :]))

                @blk.vector
                def _(vector):
                    vector.wait_ge(s_ld.sem, 32)
                    for j in range(NT):
                        xj = X[:, j * D:(j + 1) * D]
                        h = cH.inc(vector.reduce_sum(sl(j, 0), xj, axis=AX,
                                                     negate=True))
                        vector.wait_ge(cH.sem, h)
                        h = cH.inc(vector.tensor_scalar(
                            sl(j, 0), sl(j, 0), 1.0 / D, None, op0=ALU.mult))
                        vector.wait_ge(cH.sem, h)
                        h = cH.inc(vector.tensor_tensor(
                            sl(j, 2), sl(j, 0), sl(j, 0), ALU.mult))
                        vector.wait_ge(cH.sem, h)
                        vector.wait_ge(cA.sem, Ba + j + 1)
                        cC.inc(vector.scalar_tensor_tensor(
                            sl(j, 3), sl(j, 1), 1.0 / D, sl(j, 2),
                            op0=ALU.mult, op1=ALU.subtract))
                    xv = xnT[:, :].rearrange("p (d t) -> p d t", t=T)

                    def evac(j):
                        vector.wait_ge(cE.sem, Bp + 2 * j + 1)
                        cF.inc(vector.tensor_copy(
                            xv[:, 0:8, j * 128:(j + 1) * 128],
                            psC[:, :].rearrange("p (a b) -> p a b", b=128)))
                        vector.wait_ge(cE.sem, Bp + 2 * j + 2)
                        cF.inc(vector.tensor_copy(
                            xv[:, 8:16, j * 128:(j + 1) * 128],
                            psD[:, :].rearrange("p (a b) -> p a b", b=128)))

                    for j in range(NT):
                        xj = X[:, j * D:(j + 1) * D]
                        vector.wait_ge(cB.sem, Ba2 + j + 1)
                        h = cH.inc(vector.reciprocal(sl(j, 5), sl(j, 4)))
                        vector.wait_ge(cH.sem, h)
                        if j >= 2:
                            vector.wait_ge(cE.sem, Bp + 2 * (j - 2) + 2)
                        cD.inc(vector.tensor_scalar(
                            scr[j % 2][:, :], xj, sl(j, 0), sl(j, 5),
                            op0=ALU.add, op1=ALU.mult))
                        if j >= 1:
                            evac(j - 1)
                    evac(NT - 1)

                @blk.tensor
                def _(tensor):
                    for j in range(NT):
                        tensor.wait_ge(cD.sem, Bd2 + j + 1)
                        if j >= 1:
                            tensor.wait_ge(cF.sem, Bev + 2 * j)
                        for d in range(8):
                            mm = tensor.transpose(
                                psC[:, d * 128:(d + 1) * 128],
                                scr[j % 2][:, d * 128:(d + 1) * 128],
                                ident[:, :])
                        cE.inc(mm)
                        for d in range(8, 16):
                            mm = tensor.transpose(
                                psD[:, (d - 8) * 128:(d - 7) * 128],
                                scr[j % 2][:, d * 128:(d + 1) * 128],
                                ident[:, :])
                        cE.inc(mm)

        def phase_proj(l, w, NP, kind):
            rope = (l == 0)
            qz = (l == 1 and kind == "q")
            memdbg(f"proj_{kind}_l{l}")
            with ExitStack() as pctx, nc.Block(f"proj_{kind}_l{l}") as blk:
                P = pctx.enter_context
                wbf = [P(sb([128, D], BF16)) for _ in range(3)]
                kts = [P(sb([128, T], BF16)) for _ in range(2)]
                sqb = P(sb([128, T], BF16))
                qf32 = [scr[0][:, 0:1024], scr[1][:, 0:1024]]
                ps0 = [psA, psC]
                ps1 = [psB, psD]
                consts = []
                if kind == "k":
                    if rope:
                        consts = [(cosf_in, lay["cosf"]), (sinf_in, lay["sinf"]),
                                  (shift_in, lay["shiftm"])]
                    if l == 1:
                        consts = [(gq_in, lay["gqt"]), (qzb_in, lay["qzb"]),
                                  (gkc_in, lay["gkc"])]
                if rope:
                    cosf, sinf, shiftm = (lay["cosf"], lay["sinf"],
                                          lay["shiftm"])
                if qz:
                    qzb = lay["qzb"]
                Bl = s_ld.v
                Bldp = s_ldp.v
                Bp1 = cB.v
                Bqf = cC.v
                Broll = cD.v
                Bdst = cE.v
                Bsq = cF.v
                Bnm = cG.v
                Bnr = cI.v
                Bkd = s_st.v
                NCST = len(consts)
                if kind == "k":
                    lay["c_ld"] = s_ld.v + 16 * NCST

                @blk.sync
                def _(sync):
                    for csrc, cdst in consts:
                        s_ld.inc(sync.dma_start(out=cdst[:, :], in_=csrc), 16)
                    if kind == "k":
                        for i in range(NP):
                            sync.wait_ge(cE.sem, Bdst + i + 1)
                            s_st.inc(sync.dma_start(
                                out=kin[l].ap()[:, i * T:(i + 1) * T],
                                in_=kts[i % 2][:, :]), 16)

                @blk.gpsimd
                def _(gpsimd):
                    for i in range(NP):
                        if i >= 3:
                            gpsimd.wait_ge(cB.sem, Bp1 + i - 2)
                        s_ldp.inc(gpsimd.dma_start(
                            out=wbf[i % 3][:, :],
                            in_=w.rearrange("(fc p) k -> fc p k", p=128)[i]), 16)
                    if kind == "q":
                        gpsimd.wait_ge(s_cc[3 * l + 2].sem, 1)
                        s_ldp.inc(gpsimd.dma_start(
                            out=nrow[0:1, 48:48 + (16 if l else 4)],
                            in_=kmg[l].ap()), 16)

                @blk.tensor
                def _(tensor):
                    if rope or qz:
                        tensor.wait_ge(s_ld.sem, lay.get("c_ld", 0))
                    for i in range(NP):
                        p0, p1 = ps0[i % 2], ps1[i % 2]
                        tensor.wait_ge(s_ldp.sem, Bldp + 16 * (i + 1))
                        if i >= 2:
                            if rope:
                                tensor.wait_ge(cC.sem, Bqf + i - 1)
                            else:
                                tensor.wait_ge(cE.sem, Bdst + i - 1)
                        for d in range(ND):
                            for hf in range(2):
                                mm = tensor.matmul(
                                    p0[:, hf * 512:(hf + 1) * 512],
                                    wbf[i % 3][:, d * 128:(d + 1) * 128],
                                    xnT[:, d * T + hf * 512:d * T + (hf + 1) * 512],
                                    start=(d == 0), stop=(d == ND - 1),
                                    skip_group_check=True)
                        cB.inc(mm)
                        if rope:
                            tensor.wait_ge(cC.sem, Bqf + i + 1)
                            if i >= 2:
                                tensor.wait_ge(cI.sem, Bnr + i - 1)
                            for hf in range(2):
                                mm = tensor.matmul(
                                    p1[:, hf * 512:(hf + 1) * 512],
                                    shiftm[:, :],
                                    qf32[i % 2][:, hf * 512:(hf + 1) * 512],
                                    start=True, stop=True,
                                    skip_group_check=True)
                            cD.inc(mm)
                        tensor.wait_ge(cF.sem, Bsq + i + 1)
                        if not rope and i >= 2:
                            tensor.wait_ge(cI.sem, Bnr + i - 1)
                        for hf in range(2):
                            mm = tensor.matmul(
                                p1[0:1, hf * 512:(hf + 1) * 512],
                                ones_col_bf[:, :],
                                sqb[:, hf * 512:(hf + 1) * 512],
                                start=True, stop=True, skip_group_check=True)
                        cG.inc(mm)
                    if kind == "q":
                        tensor.wait_ge(cC.sem,
                                       Bqf + (NP if rope else 0) + 1)
                        cB.inc(tensor.matmul(psA[:, 0:16], ones_row_f[:, :],
                                             nrow[0:1, 96:112], start=True,
                                             stop=True, skip_group_check=True))

                @blk.vector
                def _(vector):
                    if rope or qz:
                        vector.wait_ge(s_ld.sem, lay.get("c_ld", 0))
                    for i in range(NP):
                        p0, p1 = ps0[i % 2], ps1[i % 2]
                        dest = (QT[:, i * T:(i + 1) * T] if kind == "q"
                                else kts[i % 2][:, :])
                        vector.wait_ge(cB.sem, Bp1 + i + 1)
                        if kind == "k" and i >= 2:
                            vector.wait_ge(s_st.sem, Bkd + 16 * (i - 1))
                        if i >= 1:
                            vector.wait_ge(cG.sem, Bnm + i)
                        if rope:
                            h = cC.inc(vector.tensor_copy(qf32[i % 2][:, :],
                                                          p0[:, :]))
                            vector.wait_ge(cD.sem, Broll + i + 1)
                            vector.wait_ge(cC.sem, h)
                            h = cH.inc(vector.tensor_tensor(
                                scr[2][:, 0:1024], qf32[i % 2][:, :],
                                cosf[:, :], ALU.mult))
                            vector.wait_ge(cH.sem, h)
                            h = cH.inc(vector.tensor_tensor(
                                scr[2][:, 1024:2048], p1[:, :], sinf[:, :],
                                ALU.mult))
                            vector.wait_ge(cH.sem, h)
                            cE.inc(vector.tensor_tensor(
                                dest, scr[2][:, 0:1024], scr[2][:, 1024:2048],
                                ALU.add))
                        else:
                            if qz:
                                cE.inc(vector.tensor_tensor(
                                    dest, p0[:, :], qzb[:, :], ALU.mult))
                            else:
                                cE.inc(vector.tensor_copy(dest, p0[:, :]))
                        vector.wait_ge(cE.sem, Bdst + i + 1)
                        cF.inc(vector.tensor_tensor(sqb[:, :], dest, dest,
                                                    ALU.mult))
                        vector.wait_ge(cG.sem, Bnm + i + 1)
                        slot = (nrow[0:1, i:i + 1] if kind == "q"
                                else nrow[0:1, 32 + i:32 + i + 1])
                        cI.inc(vector.reduce_max(slot, p1[0:1, :], axis=AX))
                    if kind == "q":
                        HKV = 16 if l == 1 else 4
                        vector.wait_ge(s_ldp.sem, s_ldp.v)
                        vector.wait_ge(cI.sem, Bnr + NP)
                        if HKV == 16:
                            h = cH.inc(vector.tensor_tensor(
                                nrow[0:1, 64:80], nrow[0:1, 0:16],
                                nrow[0:1, 48:64], ALU.mult))
                        else:
                            h = cH.inc(vector.tensor_tensor(
                                nrow[0:1, 64:80].rearrange(
                                    "p (a b) -> p a b", b=4),
                                nrow[0:1, 0:16].rearrange(
                                    "p (a b) -> p a b", b=4),
                                nrow[0:1, 48:52].rearrange(
                                    "p (a b) -> p a b", b=1)
                                .broadcast_to([1, 4, 4]), ALU.mult))
                        vector.wait_ge(cH.sem, h)
                        vector.wait_ge(cA.sem, cA.v + 1)
                        cC.inc(vector.tensor_scalar(
                            nrow[0:1, 96:112], nrow[0:1, 80:96], -1.0, None,
                            op0=ALU.mult))
                        vector.wait_ge(cB.sem, cB.v)
                        cD.inc(vector.tensor_copy(negc[l][:, :], psA[:, 0:16]))

                @blk.scalar
                def _(scalar):
                    if kind == "q":
                        scalar.wait_ge(cH.sem, cH.v)
                        cA.inc(scalar.activation(nrow[0:1, 80:96],
                                                 nrow[0:1, 64:80], AF.Sqrt,
                                                 scale=1.0 / 128.0))

        def phase_v_cached(l, wv, HKV, knrow_hkv):
            FW = HKV * HD
            memdbg(f"v_l{l}")
            with ExitStack() as pctx, nc.Block(f"v_l{l}") as blk:
                P = pctx.enter_context
                wvbf = P(sb([128, ND * FW], BF16))
                vstg = [P(sb([128, FW], BF16)) for _ in range(2)]
                Bld = s_ldp.v
                Bmm = cB.v
                Bev = cC.v
                Bst = s_st.v

                @blk.gpsimd
                def _(gpsimd):
                    for d in range(ND):
                        s_ldp.inc(gpsimd.dma_start(
                            out=wvbf[:, d * FW:(d + 1) * FW],
                            in_=wv[d * 128:(d + 1) * 128, :]), 16)
                    gpsimd.wait_ge(s_st.sem, Bst + 16 * NT)
                    s_cc[3 * l + 0].inc(gpsimd.collective_compute(
                        "AllGather", ALU.bypass,
                        replica_groups=[list(range(NCORES))],
                        ins=[kin[l].ap().opt()], outs=[kg[l].ap().opt()]))
                    s_cc[3 * l + 1].inc(gpsimd.collective_compute(
                        "AllGather", ALU.bypass,
                        replica_groups=[list(range(NCORES))],
                        ins=[vin[l].ap().opt()], outs=[vg[l].ap().opt()]))
                    s_ldp.inc(gpsimd.dma_start(
                        out=kml[l].ap(), in_=nrow[0:1, 32:32 + knrow_hkv]), 16)
                    gpsimd.wait_ge(s_ldp.sem, s_ldp.v)
                    s_cc[3 * l + 2].inc(gpsimd.collective_compute(
                        "AllReduce", ALU.max,
                        replica_groups=[list(range(NCORES))],
                        ins=[kml[l].ap().opt()], outs=[kmg[l].ap().opt()]))

                @blk.tensor
                def _(tensor):
                    tensor.wait_ge(s_ldp.sem, Bld + 16 * ND)
                    for t in range(NT):
                        p0 = [psA, psB][t % 2]
                        if t >= 2:
                            tensor.wait_ge(cC.sem, Bev + t - 1)
                        for d in range(ND):
                            mm = tensor.matmul(
                                p0[:, 0:FW],
                                xnT[:, d * T + t * 128:d * T + (t + 1) * 128],
                                wvbf[:, d * FW:(d + 1) * FW],
                                start=(d == 0), stop=(d == ND - 1),
                                skip_group_check=True)
                        cB.inc(mm)

                @blk.vector
                def _(vector):
                    for t in range(NT):
                        p0 = [psA, psB][t % 2]
                        vector.wait_ge(cB.sem, Bmm + t + 1)
                        if t >= 2:
                            vector.wait_ge(s_st.sem, Bst + 16 * (t - 1))
                        cC.inc(vector.tensor_copy(vstg[t % 2][:, :],
                                                  p0[:, 0:FW]))

                @blk.sync
                def _(sync):
                    for t in range(NT):
                        sync.wait_ge(cC.sem, Bev + t + 1)
                        s_st.inc(sync.dma_start(
                            out=vin[l].ap()[t * 128:(t + 1) * 128, :],
                            in_=vstg[t % 2][:, :]), 16)

        def phase_v_fmajor(l, wv, knrow_hkv):
            memdbg(f"v_l{l}")
            with ExitStack() as pctx, nc.Block(f"v_l{l}") as blk:
                P = pctx.enter_context
                wbf = [P(sb([128, D], BF16)) for _ in range(3)]
                vstg = [P(sb([128, NT * 128], BF16)) for _ in range(2)]
                Bld = s_ldp.v
                Bp1 = cB.v
                Bf = cC.v
                Btr = cD.v
                Bbe = cE.v
                Bst = s_st.v

                @blk.gpsimd
                def _(gpsimd):
                    for fc in range(16):
                        if fc >= 3:
                            gpsimd.wait_ge(cB.sem, Bp1 + fc - 2)
                        s_ldp.inc(gpsimd.dma_start(
                            out=wbf[fc % 3][:, :],
                            in_=wv.rearrange("(fc p) k -> fc p k", p=128)[fc]),
                            16)
                    gpsimd.wait_ge(s_st.sem, Bst + 16 * 16)
                    s_cc[3 * l + 0].inc(gpsimd.collective_compute(
                        "AllGather", ALU.bypass,
                        replica_groups=[list(range(NCORES))],
                        ins=[kin[l].ap().opt()], outs=[kg[l].ap().opt()]))
                    s_cc[3 * l + 1].inc(gpsimd.collective_compute(
                        "AllGather", ALU.bypass,
                        replica_groups=[list(range(NCORES))],
                        ins=[vin[l].ap().opt()], outs=[vg[l].ap().opt()]))
                    s_ldp.inc(gpsimd.dma_start(
                        out=kml[l].ap(), in_=nrow[0:1, 32:32 + knrow_hkv]), 16)
                    gpsimd.wait_ge(s_ldp.sem, s_ldp.v)
                    s_cc[3 * l + 2].inc(gpsimd.collective_compute(
                        "AllReduce", ALU.max,
                        replica_groups=[list(range(NCORES))],
                        ins=[kml[l].ap().opt()], outs=[kmg[l].ap().opt()]))

                @blk.tensor
                def _(tensor):
                    for fc in range(16):
                        p0 = [psA, psC][fc % 2]
                        p1 = [psB, psD][fc % 2]
                        tensor.wait_ge(s_ldp.sem, Bld + 16 * (fc + 1))
                        if fc >= 2:
                            tensor.wait_ge(cC.sem, Bf + fc - 1)
                        for d in range(ND):
                            for hf in range(2):
                                mm = tensor.matmul(
                                    p0[:, hf * 512:(hf + 1) * 512],
                                    wbf[fc % 3][:, d * 128:(d + 1) * 128],
                                    xnT[:, d * T + hf * 512:d * T + (hf + 1) * 512],
                                    start=(d == 0), stop=(d == ND - 1),
                                    skip_group_check=True)
                        cB.inc(mm)
                        tensor.wait_ge(cC.sem, Bf + fc + 1)
                        if fc >= 2:
                            tensor.wait_ge(cE.sem, Bbe + fc - 1)
                        for t in range(NT):
                            mm = tensor.transpose(
                                p1[:, t * 128:(t + 1) * 128],
                                scr[2][:, t * 128:(t + 1) * 128],
                                ident[:, :])
                        cD.inc(mm)

                @blk.vector
                def _(vector):
                    for fc in range(16):
                        p0 = [psA, psC][fc % 2]
                        p1 = [psB, psD][fc % 2]
                        vector.wait_ge(cB.sem, Bp1 + fc + 1)
                        if fc >= 1:
                            vector.wait_ge(cD.sem, Btr + fc)
                        h = cC.inc(vector.tensor_copy(scr[2][:, 0:1024],
                                                      p0[:, :]))
                        vector.wait_ge(cC.sem, h)
                        vector.wait_ge(cD.sem, Btr + fc + 1)
                        if fc >= 2:
                            vector.wait_ge(s_st.sem, Bst + 16 * (fc - 1))
                        cE.inc(vector.tensor_copy(vstg[fc % 2][:, :], p1[:, :]))

                @blk.sync
                def _(sync):
                    for fc in range(16):
                        sync.wait_ge(cE.sem, Bbe + fc + 1)
                        s_st.inc(sync.dma_start(
                            out=vin[l].ap().rearrange("(t p) f -> p t f", p=128)
                            [:, :, fc * 128:(fc + 1) * 128],
                            in_=vstg[fc % 2][:, :].rearrange(
                                "p (t f) -> p t f", f=128)), 16)

        def phase_flash(l, H, HKV, mask):
            C = NKC
            memdbg(f"flash_l{l}")
            with ExitStack() as pctx, nc.Block(f"flash_l{l}") as blk:
                P = pctx.enter_context
                NKV = 6
                kbuf = [P(sb([128, 128], BF16)) for _ in range(NKV)]
                vbuf = [P(sb([128, 129], BF16)) for _ in range(NKV)]
                NPB = 5 if mask else 4
                pbuf = [P(sb([128, T], BF16)) for _ in range(NPB)]
                rec8 = P(sb([128, 8], F32))
                dmy = nrow[0:1, 256:264]
                Bkv = s_kv.v
                Bss = s_ss.v
                Bse = s_se.v
                Bsm = s_sm.v
                Bsa = s_sa.v
                B1 = s_hd1.v
                B3 = s_hd3.v
                Bpe = s_hpe.v
                kgl = kg[l].ap()
                vgl = vg[l].ap()
                psS = [psA, psB]

                def kv_idx(qh, c):
                    kvh = qh // (H // HKV)
                    return (kgl[(c // 8) * 128:(c // 8 + 1) * 128,
                                kvh * T + (c % 8) * 128:
                                kvh * T + (c % 8 + 1) * 128],
                            vgl[c * 128:(c + 1) * 128,
                                kvh * 128:(kvh + 1) * 128])

                def psO(qc):
                    p = psC if qc < 4 else psD
                    return p[:, (qc % 4) * 256:(qc % 4) * 256 + 129]

                @blk.sync
                def _(sync):
                    sync.wait_ge(s_cc[3 * l + 0].sem, 1)
                    sync.wait_ge(s_cc[3 * l + 1].sem, 1)
                    for g in range(H * C):
                        qh, c = divmod(g, C)
                        if g >= NKV:
                            sync.wait_ge(s_sa.sem, Bsa + g - NKV + 1)
                        ksrc, vsrc = kv_idx(qh, c)
                        s_kv.inc(sync.dma_start(out=kbuf[g % NKV][:, :],
                                                in_=ksrc), 16)
                        s_kv.inc(sync.dma_start(out=vbuf[g % NKV][:, 0:128],
                                                in_=vsrc), 16)

                @blk.vector
                def _(vector):
                    if mask:
                        vector.wait_ge(s_ld.sem, lay.get("c_ld", 0))
                    for b in range(NKV):
                        inst = vector.memset(vbuf[b][:, 128:129], 1.0)
                    cJ.inc(inst)

                    def dve_headend(qh):
                        vector.wait_ge(s_sa.sem, Bsa + (qh + 1) * C)
                        h = cH.inc(vector.reciprocal(
                            rec8[:, 0:4].rearrange("p (a b) -> p a b", b=1),
                            psC[:, :].rearrange("p (a b) -> p a b", b=256)
                            [:, :, 128:129]))
                        h = cH.inc(vector.reciprocal(
                            rec8[:, 4:8].rearrange("p (a b) -> p a b", b=1),
                            psD[:, :].rearrange("p (a b) -> p a b", b=256)
                            [:, :, 128:129]))
                        vector.wait_ge(cH.sem, h)
                        for qc in range(8):
                            p = psC if qc < 4 else psD
                            inst2 = vector.tensor_scalar(
                                scr[0][:, qc * 128:(qc + 1) * 128],
                                p[:, (qc % 4) * 256:(qc % 4) * 256 + 128],
                                rec8[:, qc:qc + 1], None, op0=ALU.mult)
                        s_hd1.inc(inst2)
                        vector.wait_ge(s_hpe.sem, Bpe + qh + 1)
                        s_hd3.inc(vector.tensor_copy(
                            OTn[:, qh * T:(qh + 1) * T], psC[:, :]))

                    pend = None
                    for g in range(H * C):
                        qh, c = divmod(g, C)
                        if mask:
                            vector.wait_ge(s_se.sem, Bse + 2 * g + 3)
                            s_sm.inc(vector.scalar_tensor_tensor(
                                pbuf[g % NPB][:, :], lay["gqt"][:, :],
                                lay["gkc"][:, c:c + 1], pbuf[g % NPB][:, :],
                                op0=ALU.is_lt, op1=ALU.mult))
                            if g == H * C - 1:
                                s_sm.inc(vector.scalar_tensor_tensor(
                                    dmy[0:1, 1:2], lay["gqt"][0:1, 0:1],
                                    lay["gkc"][0:1, 0:1], dmy[0:1, 1:2],
                                    op0=ALU.is_lt, op1=ALU.mult))
                        if pend is not None:
                            dve_headend(pend)
                            pend = None
                        if c == C - 1:
                            pend = qh
                    dve_headend(pend)

                @blk.tensor
                def _(tensor):
                    def emit_S(g):
                        qh, c = divmod(g, C)
                        tensor.wait_ge(s_kv.sem, Bkv + 32 * (g + 1))
                        if g == 0:
                            tensor.wait_ge(cJ.sem, cJ.v)
                        for hf in range(2):
                            if g >= 2:
                                tensor.wait_ge(s_se.sem,
                                               Bse + 2 * (g - 2) + hf + 1)
                            mm = tensor.matmul(
                                psS[g % 2][:, hf * 512:(hf + 1) * 512],
                                kbuf[g % NKV][:, :],
                                QT[:, qh * T + hf * 512:qh * T + (hf + 1) * 512],
                                start=True, stop=True, skip_group_check=True)
                            s_ss.inc(mm)

                    def emit_AV(g):
                        qh, c = divmod(g, C)
                        if mask:
                            tensor.wait_ge(s_sm.sem, Bsm + g + 2)
                        else:
                            tensor.wait_ge(s_se.sem, Bse + 2 * g + 3)
                        if c == 0 and qh >= 1:
                            tensor.wait_ge(s_hd3.sem, B3 + qh)
                        for qc in range(8):
                            mm = tensor.matmul(
                                psO(qc),
                                pbuf[g % NPB][:, qc * 128:(qc + 1) * 128],
                                vbuf[g % NKV][:, :],
                                start=(c == 0 and qc % 2 == 0),
                                stop=(c == C - 1),
                                skip_group_check=True)
                        s_sa.inc(mm)

                    def emit_headend(p):
                        tensor.wait_ge(s_hd1.sem, B1 + p + 1)
                        for qc in range(8):
                            mm = tensor.transpose(
                                psC[:, qc * 128:(qc + 1) * 128],
                                scr[0][:, qc * 128:(qc + 1) * 128],
                                ident[:, :])
                        s_hpe.inc(mm)

                    pend_he = None
                    for g in range(H * C):
                        emit_S(g)
                        if pend_he is not None:
                            emit_headend(pend_he)
                            pend_he = None
                        if g >= 2:
                            emit_AV(g - 2)
                            if (g - 2) % C == C - 1:
                                pend_he = (g - 2) // C
                    emit_AV(H * C - 2)
                    emit_AV(H * C - 1)
                    emit_headend(H - 1)

                @blk.scalar
                def _(scalar):
                    for g in range(H * C):
                        qh, c = divmod(g, C)
                        for hf in range(2):
                            scalar.wait_ge(s_ss.sem, Bss + 2 * g + hf + 1)
                            if hf == 0 and g >= NPB:
                                scalar.wait_ge(s_sa.sem, Bsa + g - NPB + 1)
                            s_se.inc(scalar.activation(
                                pbuf[g % NPB][:, hf * 512:(hf + 1) * 512],
                                psS[g % 2][:, hf * 512:(hf + 1) * 512],
                                AF.Exp, bias=negc[l][:, qh:qh + 1], scale=ISQ))
                    s_se.inc(scalar.activation(
                        dmy[0:1, 0:1], psS[0][0:1, 0:1], AF.Exp,
                        bias=negc[l][0:1, 0:1], scale=ISQ))

        def phase_wo(l, wo):
            memdbg(f"wo_l{l}")
            with ExitStack() as pctx, nc.Block(f"wo_l{l}") as blk:
                P = pctx.enter_context
                wobf = QT
                Bld = s_ldp.v
                Bmm = cB.v
                Bx = cC.v

                @blk.gpsimd
                def _(gpsimd):
                    for grp in range(2):
                        if grp == 1:
                            gpsimd.wait_ge(cB.sem, Bmm + 8)
                        for h8 in range(8):
                            h = grp * 8 + h8
                            s_ldp.inc(gpsimd.dma_start(
                                out=wobf[:, h8 * D:(h8 + 1) * D],
                                in_=wo[h * 128:(h + 1) * 128, :]), 16)

                @blk.tensor
                def _(tensor):
                    for grp in range(2):
                        tensor.wait_ge(s_ldp.sem, Bld + 16 * 8 * (grp + 1))
                        for t in range(NT):
                            pl, ph = [(psA, psB), (psC, psD)][t % 2]
                            if grp * NT + t >= 2:
                                tensor.wait_ge(cC.sem, Bx + grp * NT + t - 1)
                            for h8 in range(8):
                                h = grp * 8 + h8
                                for hf in range(4):
                                    p = pl if hf < 2 else ph
                                    mm = tensor.matmul(
                                        p[:, (hf % 2) * 512:(hf % 2 + 1) * 512],
                                        OTn[:, h * T + t * 128:h * T + (t + 1) * 128],
                                        wobf[:, h8 * D + hf * 512:
                                             h8 * D + (hf + 1) * 512],
                                        start=(h8 == 0), stop=(h8 == 7),
                                        skip_group_check=True)
                            cB.inc(mm)

                @blk.vector
                def _(vector):
                    for grp in range(2):
                        for t in range(NT):
                            pl, ph = [(psA, psB), (psC, psD)][t % 2]
                            vector.wait_ge(cB.sem, Bmm + grp * NT + t + 1)
                            h = cH.inc(vector.tensor_tensor(
                                X[:, t * D:t * D + 1024],
                                X[:, t * D:t * D + 1024], pl[:, :], ALU.add))
                            vector.wait_ge(cH.sem, h)
                            cC.inc(vector.tensor_tensor(
                                X[:, t * D + 1024:(t + 1) * D],
                                X[:, t * D + 1024:(t + 1) * D], ph[:, :],
                                ALU.add))

        def phase_ffn0():
            memdbg("ffn0")
            with ExitStack() as pctx, nc.Block("ffn0") as blk:
                P = pctx.enter_context
                dsb = P(sb([128, D], F32))
                gs_sb = P(sb([128, NIC], F32))
                us_sb = P(sb([128, NIC], F32))
                xadd = scr[2][:, 1024:2048]
                h_sb = P(sb([128, NIC * 512], BF16))
                gbf = [P(sb([128, D], BF16)) for _ in range(2)]
                ubf = [P(sb([128, D], BF16)) for _ in range(2)]
                dbf = [P(sb([128, 1024], BF16))
                       for _ in range(2)]
                silu = [P(sb([128, 512], BF16))
                        for _ in range(2)]
                usb = [P(sb([128, 512], BF16))
                       for _ in range(2)]
                Bl = s_ld.v
                Bc = cA.v
                Bgu = cB.v
                Bsil = cC.v
                Buev = cD.v
                Bh = cE.v
                Bxa = cF.v
                Bdn = cG.v
                Bdc = cI.v
                Bdl = s_ldp.v
                BJ = cJ.v

                @blk.sync
                def _(sync):
                    cJ.inc(sync.dma_start(out=dsb[:, :], in_=ds_b), 16)
                    cJ.inc(sync.dma_start(out=gs_sb[:, :], in_=gs_c), 16)
                    cJ.inc(sync.dma_start(out=us_sb[:, :], in_=us_c), 16)
                    for hh in range(2):
                        for ic in range(NIC):
                            i = hh * NIC + ic
                            if i >= 1:
                                sync.wait_ge(cA.sem, Bc + i)
                            s_ld.inc(sync.dma_start(
                                out=scr[0][:, :],
                                in_=gwT.rearrange("(ic p) k -> ic p k", p=128)
                                [ic]), 16)
                            s_ld.inc(sync.dma_start(
                                out=scr[1][:, :],
                                in_=uwT.rearrange("(ic p) k -> ic p k", p=128)
                                [ic]), 16)

                @blk.gpsimd
                def _(gpsimd):
                    for hh in range(2):
                        for dmh in range(2):
                            for ic in range(NIC):
                                j = (hh * 2 + dmh) * NIC + ic
                                if j >= 1:
                                    gpsimd.wait_ge(cI.sem, Bdc + j)
                                s_ldp.inc(gpsimd.dma_start(
                                    out=scr[2][:, 0:1024],
                                    in_=dwT[ic * 128:(ic + 1) * 128,
                                            dmh * 1024:(dmh + 1) * 1024]), 16)

                @blk.scalar
                def _(scalar):
                    for hh in range(2):
                        for ic in range(NIC):
                            i = hh * NIC + ic
                            scalar.wait_ge(s_ld.sem, Bl + 32 * (i + 1))
                            if i >= 2:
                                scalar.wait_ge(cB.sem, Bgu + i - 1)
                            scalar.copy(gbf[i % 2][:, :], scr[0][:, :])
                            cA.inc(scalar.copy(ubf[i % 2][:, :], scr[1][:, :]))
                            scalar.wait_ge(cB.sem, Bgu + i + 1)
                            if i == 0:
                                scalar.wait_ge(cJ.sem, BJ + 48)
                            psGU = [psA, psB][i % 2]
                            cC.inc(scalar.activation(
                                silu[i % 2][:, :], psGU[:, 0:512], AF.Silu,
                                scale=gs_sb[:, ic:ic + 1]))
                        for dmh in range(2):
                            for ic in range(NIC):
                                j = (hh * 2 + dmh) * NIC + ic
                                scalar.wait_ge(s_ldp.sem, Bdl + 16 * (j + 1))
                                if j >= 2:
                                    scalar.wait_ge(cG.sem, Bdn + j - 1)
                                cI.inc(scalar.copy(dbf[j % 2][:, :],
                                                   scr[2][:, 0:1024]))

                @blk.tensor
                def _(tensor):
                    for hh in range(2):
                        for ic in range(NIC):
                            i = hh * NIC + ic
                            psGU = [psA, psB][i % 2]
                            tensor.wait_ge(cA.sem, Bc + i + 1)
                            if i >= 2:
                                tensor.wait_ge(cC.sem, Bsil + i - 1)
                                tensor.wait_ge(cD.sem, Buev + i - 1)
                            if ic == 0 and hh == 1:
                                tensor.wait_ge(cF.sem, Bxa + 8)
                            for d in range(ND):
                                rhs = xnT[:, d * T + hh * 512:
                                          d * T + (hh + 1) * 512]
                                tensor.matmul(
                                    psGU[:, 0:512],
                                    gbf[i % 2][:, d * 128:(d + 1) * 128], rhs,
                                    start=(d == 0), stop=(d == ND - 1),
                                    skip_group_check=True)
                                mm = tensor.matmul(
                                    psGU[:, 512:1024],
                                    ubf[i % 2][:, d * 128:(d + 1) * 128], rhs,
                                    start=(d == 0), stop=(d == ND - 1),
                                    skip_group_check=True)
                            cB.inc(mm)
                        tensor.wait_ge(cE.sem, Bh + (hh + 1) * NIC)
                        pstch = [psA, psB, psC, psD]
                        for dmh in range(2):
                            for ic in range(NIC):
                                j = (hh * 2 + dmh) * NIC + ic
                                tensor.wait_ge(cI.sem, Bdc + j + 1)
                                if dmh == 0 and ic == 0:
                                    tensor.wait_ge(cC.sem, Bsil + (hh + 1) * NIC)
                                    tensor.wait_ge(cD.sem, Buev + (hh + 1) * NIC)
                                if ic == 0 and (hh * 2 + dmh) >= 1:
                                    tensor.wait_ge(cF.sem, Bxa + (hh * 2 + dmh) * 4)
                                for tch in range(4):
                                    for hf in range(2):
                                        mm = tensor.matmul(
                                            pstch[tch][:, hf * 512:(hf + 1) * 512],
                                            h_sb[:, ic * 512 + tch * 128:
                                                 ic * 512 + (tch + 1) * 128],
                                            dbf[j % 2][:, hf * 512:(hf + 1) * 512],
                                            start=(ic == 0), stop=(ic == NIC - 1),
                                            skip_group_check=True)
                                cG.inc(mm)

                @blk.vector
                def _(vector):
                    vector.wait_ge(cJ.sem, BJ + 48)
                    for hh in range(2):
                        for ic in range(NIC):
                            i = hh * NIC + ic
                            psGU = [psA, psB][i % 2]
                            vector.wait_ge(cB.sem, Bgu + i + 1)
                            h = cD.inc(vector.tensor_scalar(
                                usb[i % 2][:, :], psGU[:, 512:1024],
                                us_sb[:, ic:ic + 1], None, op0=ALU.mult))
                            vector.wait_ge(cD.sem, h)
                            vector.wait_ge(cC.sem, Bsil + i + 1)
                            cE.inc(vector.tensor_tensor(
                                h_sb[:, ic * 512:(ic + 1) * 512],
                                silu[i % 2][:, :], usb[i % 2][:, :], ALU.mult))
                        pstch = [psA, psB, psC, psD]
                        for dmh in range(2):
                            for tch in range(4):
                                vector.wait_ge(cG.sem,
                                               Bdn + (hh * 2 + dmh + 1) * NIC)
                                tg = hh * 4 + tch
                                xsl = X[:, tg * D + dmh * 1024:
                                        tg * D + (dmh + 1) * 1024]
                                h = cH.inc(vector.tensor_tensor(
                                    xadd, pstch[tch][:, :],
                                    dsb[:, dmh * 1024:(dmh + 1) * 1024],
                                    ALU.mult))
                                vector.wait_ge(cH.sem, h)
                                cF.inc(vector.tensor_tensor(
                                    xsl, xsl, xadd, ALU.add))

        def phase_ffn1():
            memdbg("ffn1")
            with ExitStack() as pctx, nc.Block("ffn1") as blk:
                P = pctx.enter_context
                abf = P(sb([128, ND * 512], BF16))
                bbf = P(sb([128, 4 * D], BF16))
                wrT = P(sb([128, ND * 8], BF16))
                wrst = P(sb([8, D], F32))
                brsb = P(sb([1, 8], F32))
                p8 = P(sb([128, 8], F32))
                ow = P(sb([128, 9], F32))
                sst = P(sb([128, 4], F32))
                hmf = P(sb([128, 512], F32))
                hTb = [P(sb([128, 512], BF16))
                       for _ in range(2)]
                Bl = s_ld.v
                Bc = cA.v
                Bex = cI.v
                Bpe = cB.v
                Btr = cG.v
                BwT = cD.v
                Bsx = cC.v
                Bmk = cF.v
                Bhev = cE.v
                BxH = cH.v

                @blk.sync
                def _(sync):
                    for d in range(ND):
                        if d >= 2:
                            sync.wait_ge(cA.sem, Bc + d - 1)
                        s_ld.inc(sync.dma_start(
                            out=scr[d % 2][:, 0:512],
                            in_=a_T[d * 128:(d + 1) * 128, :]), 16)
                    for b in range(4):
                        sync.wait_ge(cA.sem, Bc + ND + b)
                        s_ld.inc(sync.dma_start(
                            out=scr[b % 2][:, :],
                            in_=b_sT[b * 128:(b + 1) * 128, :]), 16)
                    s_ld.inc(sync.dma_start(out=wrst[:, :], in_=wr_n), 16)
                    s_ld.inc(sync.dma_start(out=brsb[:, :], in_=br_r), 16)

                @blk.scalar
                def _(scalar):
                    for d in range(ND):
                        scalar.wait_ge(s_ld.sem, Bl + 16 * (d + 1))
                        cA.inc(scalar.copy(abf[:, d * 512:(d + 1) * 512],
                                           scr[d % 2][:, 0:512]))
                    for b in range(4):
                        scalar.wait_ge(s_ld.sem, Bl + 16 * (ND + b + 1))
                        cA.inc(scalar.mul(bbf[:, b * D:(b + 1) * D],
                                          scr[b % 2][:, :], 1.0 / RK))
                    for t in range(NT):
                        scalar.wait_ge(cC.sem, Bsx + 5 * t + 1)
                        cI.inc(scalar.activation(
                            p8[:, :], psB[:, 0:8], AF.Exp,
                            bias=sst[:, 0:1]))

                @blk.tensor
                def _(tensor):
                    tensor.wait_ge(s_ld.sem, Bl + 16 * (ND + 4 + 2))
                    for d in range(ND):
                        mm = tensor.transpose(
                            psC[:, d * 8:(d + 1) * 8],
                            wrst[:, d * 128:(d + 1) * 128], ident[0:8, 0:8])
                    cB.inc(mm)
                    tensor.wait_ge(cA.sem, Bc + ND + 4)
                    tensor.wait_ge(cD.sem, BwT + 1)
                    for t in range(NT):
                        if t >= 1:
                            tensor.wait_ge(cI.sem, Bex + t)
                            tensor.wait_ge(cC.sem, Bsx + 5 * t)
                        for d in range(ND):
                            tensor.matmul(
                                psB[:, 0:8],
                                xnT[:, d * T + t * 128:d * T + (t + 1) * 128],
                                wrT[:, d * 8:(d + 1) * 8],
                                start=(d == 0), stop=False,
                                skip_group_check=True)
                        mm = tensor.matmul(
                            psB[:, 0:8], ones_row_f[:, :], brsb[:, :],
                            start=False, stop=True, skip_group_check=True)
                        cB.inc(mm)
                        if t >= 1:
                            tensor.wait_ge(cF.sem, Bmk + t)
                        for d in range(ND):
                            mm = tensor.matmul(
                                psA[:, 0:512],
                                xnT[:, d * T + t * 128:d * T + (t + 1) * 128],
                                abf[:, d * 512:(d + 1) * 512],
                                start=(d == 0), stop=(d == ND - 1),
                                skip_group_check=True)
                        cB.inc(mm)
                        tensor.wait_ge(cF.sem, Bmk + t + 1)
                        if t >= 1:
                            tensor.wait_ge(cE.sem, Bhev + t)
                        for b in range(4):
                            mm = tensor.transpose(
                                psB[:, 512 + b * 128:512 + (b + 1) * 128],
                                hmf[:, b * 128:(b + 1) * 128], ident[:, :])
                        cG.inc(mm)
                        tensor.wait_ge(cE.sem, Bhev + t + 1)
                        if t >= 1:
                            tensor.wait_ge(cH.sem, BxH + 2 * t)
                        for b in range(4):
                            for hf in range(4):
                                p = psC if hf < 2 else psD
                                mm = tensor.matmul(
                                    p[:, (hf % 2) * 512:(hf % 2 + 1) * 512],
                                    hTb[t % 2][:, b * 128:(b + 1) * 128],
                                    bbf[:, b * D + hf * 512:b * D + (hf + 1) * 512],
                                    start=(b == 0), stop=(b == 3),
                                    skip_group_check=True)
                        cB.inc(mm)

                @blk.vector
                def _(vector):
                    vector.wait_ge(cB.sem, Bpe + 1)
                    cD.inc(vector.tensor_copy(wrT[:, :], psC[:, 0:ND * 8]))
                    for t in range(NT):
                        vector.wait_ge(cB.sem, Bpe + 1 + 3 * t + 1)
                        h = cC.inc(vector.reduce_max(
                            sst[:, 0:1], psB[:, 0:8], axis=AX, negate=True))
                        vector.wait_ge(cC.sem, h)
                        h = cC.inc(vector.reduce_max(
                            sst[:, 1:2], psB[:, 0:8], axis=AX))
                        vector.wait_ge(cC.sem, h)
                        vector.wait_ge(cI.sem, Bex + t + 1)
                        h = cC.inc(vector.reduce_sum(
                            sst[:, 2:3], p8[:, :], axis=AX))
                        vector.wait_ge(cC.sem, h)
                        h = cC.inc(vector.reciprocal(sst[:, 3:4], sst[:, 2:3]))
                        vector.wait_ge(cC.sem, h)
                        h = cC.inc(vector.tensor_scalar(
                            ow[:, 0:8], psB[:, 0:8], sst[:, 1:2], sst[:, 3:4],
                            op0=ALU.is_ge, op1=ALU.mult))
                        vector.wait_ge(cC.sem, h)
                        vector.wait_ge(cB.sem, Bpe + 1 + 3 * t + 2)
                        if t >= 1:
                            vector.wait_ge(cG.sem, Btr + t)
                        for e in range(NE):
                            inst = vector.tensor_scalar(
                                hmf[:, e * 64:(e + 1) * 64],
                                psA[:, e * 64:(e + 1) * 64],
                                ow[:, e:e + 1], None, op0=ALU.mult)
                        cF.inc(inst)
                        vector.wait_ge(cG.sem, Btr + t + 1)
                        if t >= 2:
                            vector.wait_ge(cB.sem, Bpe + 1 + 3 * (t - 1))
                        cE.inc(vector.tensor_copy(hTb[t % 2][:, :],
                                                  psB[:, 512:1024]))
                        vector.wait_ge(cB.sem, Bpe + 1 + 3 * t + 3)
                        cH.inc(vector.tensor_tensor(
                            X[:, t * D:t * D + 1024],
                            X[:, t * D:t * D + 1024], psC[:, :], ALU.add))
                        cH.inc(vector.tensor_tensor(
                            X[:, t * D + 1024:(t + 1) * D],
                            X[:, t * D + 1024:(t + 1) * D], psD[:, :],
                            ALU.add))

        def dump_x(dest_ap):
            with nc.Block("dump") as blk:
                @blk.sync
                def _(sync):
                    s_st.inc(sync.dma_start(
                        out=dest_ap.rearrange("(t p) d -> p t d", p=128),
                        in_=X[:, :].rearrange("p (t d) -> p t d", d=D)), 16)
                    if DEV_DUMP_KV:
                        kgo = nc.dram_tensor("kg0_tap", [NCORES * 128, 4 * T],
                                             BF16, kind="ExternalOutput").ap()
                        vgo = nc.dram_tensor("vg0_tap", [SEQ, 4 * HD], BF16,
                                             kind="ExternalOutput").ap()
                        nco = nc.dram_tensor("negc_tap", [128, 16], F32,
                                             kind="ExternalOutput").ap()
                        s_st.inc(sync.dma_start(out=kgo, in_=kg[0].ap()), 16)
                        s_st.inc(sync.dma_start(out=vgo, in_=vg[0].ap()), 16)
                        s_st.inc(sync.dma_start(out=nco, in_=negc[0][:, :]), 16)
                    sync.wait_ge(s_st.sem, s_st.v)

        def attn_layer(l):
            global QT, OTn
            with ExitStack() as actx:
                A = actx.enter_context
                lay.clear()
                if l == 1:
                    lay["gqt"] = A(nc.sbuf_tensor("gqt_s", [128, T], F32))
                    lay["gkc"] = A(nc.sbuf_tensor("gkc_s", [128, NKC], F32))
                QT = A(nc.sbuf_tensor(f"QT{l}", [128, 16 * T], BF16))
                ictx = ExitStack()
                IA = ictx.enter_context
                if l == 0:
                    lay["cosf"] = IA(nc.sbuf_tensor("cosf_s", [128, T], F32))
                    lay["sinf"] = IA(nc.sbuf_tensor("sinf_s", [128, T], F32))
                    lay["shiftm"] = IA(nc.sbuf_tensor("shiftm_s",
                                                      [128, 128], F32))
                else:
                    lay["qzb"] = IA(nc.sbuf_tensor("qzb_s", [128, T], F32))
                if l == 0:
                    phase_proj(0, wk0, 4, "k")
                    phase_v_cached(0, wv0, 4, 4)
                    phase_proj(0, wq0, 16, "q")
                    ictx.close()
                    if DEV_DUMP_KV:
                        qtap = nc.dram_tensor("qt_tap", [128, 16 * T], BF16,
                                              kind="ExternalOutput").ap()
                        with nc.Block("qtdump") as qblk:
                            @qblk.sync
                            def _(sync):
                                s_st.inc(sync.dma_start(out=qtap,
                                                        in_=QT[:, :]), 16)
                                sync.wait_ge(s_st.sem, s_st.v)
                    OTn = A(nc.sbuf_tensor(f"OTn{l}", [128, 16 * T], BF16))
                    phase_flash(0, 16, 4, False)
                    if DEV_DUMP_KV:
                        otap = nc.dram_tensor("otn_tap", [128, 16 * T], BF16,
                                              kind="ExternalOutput").ap()
                        with nc.Block("otdump") as oblk:
                            @oblk.sync
                            def _(sync):
                                s_st.inc(sync.dma_start(out=otap,
                                                        in_=OTn[:, :]), 16)
                                sync.wait_ge(s_st.sem, s_st.v)
                    phase_wo(0, wo0)
                else:
                    phase_proj(1, wk1, 16, "k")
                    phase_v_fmajor(1, wv1, 16)
                    phase_proj(1, wq1, 16, "q")
                    ictx.close()
                    OTn = A(nc.sbuf_tensor(f"OTn{l}", [128, 16 * T], BF16))
                    phase_flash(1, 16, 16, True)
                    phase_wo(1, wo1)

        phase_ln("l0a")
        attn_layer(0)
        if "l0_attn" in taps:
            dump_x(taps["l0_attn"])
        if stop_after == "l0_attn":
            dump_x(y_out)
            return nc, din
        phase_ln("l0f")
        phase_ffn0()
        if "l0_ffn" in taps:
            dump_x(taps["l0_ffn"])
        if stop_after == "l0_ffn":
            dump_x(y_out)
            return nc, din
        phase_ln("l1a")
        attn_layer(1)
        if "l1_attn" in taps:
            dump_x(taps["l1_attn"])
        if stop_after == "l1_attn":
            dump_x(y_out)
            return nc, din
        phase_ln("l1f")
        phase_ffn1()
        dump_x(y_out)
    return nc, din



def _rope_tables(pos):
    inv = 1.0 / (10000.0 ** (np.arange(0, HD, 2, dtype=np.float32) / HD))
    fr = pos[:, None].astype(np.float32) * inv[None, :]
    emb = np.concatenate([fr, fr], axis=-1)
    return np.cos(emb).astype(np.float32), np.sin(emb).astype(np.float32)


def _shift_mat():
    m = np.zeros((128, 128), dtype=np.float32)
    for k in range(128):
        m[k, (k + 1) % 128] = 1.0
    return m


def _tile_w(w, nf):
    K, F = w.shape
    a = w.reshape(K // 128, 128, nf, F // nf)
    a = a.transpose(2, 1, 0, 3)
    return np.ascontiguousarray(a.reshape(F, K))


def _core_inputs(inputs, r):
    x = np.ascontiguousarray(
        inputs["L0_attn_x"][0, r * T:(r + 1) * T, :]).astype(np.float32, copy=False)
    pos = np.arange(r * T, (r + 1) * T)
    cos, sin = _rope_tables(pos)
    cosf = np.ascontiguousarray(cos.T)
    sinf = np.ascontiguousarray(sin.T)
    gq = np.tile(pos.astype(np.float32)[None, :], (128, 1))
    qz = np.ones((128, T), dtype=np.float32)
    if r == NCORES - 1:
        gq[:, T - 1] = -1.0
        qz[:, T - 1] = 0.0
    gkcols = (np.arange(NKC, dtype=np.float32)[None, :] * 128
              + np.arange(128, dtype=np.float32)[:, None])
    gs = inputs["L0_ffn_gate_s"].astype(np.float32)
    us = inputs["L0_ffn_up_s"].astype(np.float32)
    ds = inputs["L0_ffn_down_s"].astype(np.float32)
    A = np.asarray(inputs["L1_ffn_A"], dtype=np.float32)
    B = np.asarray(inputs["L1_ffn_B"], dtype=np.float32)
    d = dict(
        x=x,
        wq0=_tile_w(np.asarray(inputs["L0_attn_Wq"], dtype=np.float32), 16),
        wk0=_tile_w(np.asarray(inputs["L0_attn_Wk"], dtype=np.float32), 4),
        wv0=np.asarray(inputs["L0_attn_Wv"], dtype=np.float32),
        wo0=np.asarray(inputs["L0_attn_Wo"], dtype=np.float32),
        gwT=_tile_w(np.asarray(inputs["L0_ffn_gate_w"], dtype=np.float32).T, NIC),
        uwT=_tile_w(np.asarray(inputs["L0_ffn_up_w"], dtype=np.float32).T, NIC),
        dwT=np.ascontiguousarray(np.asarray(inputs["L0_ffn_down_w"],
                                            dtype=np.float32).T),
        gs_c=np.ascontiguousarray(gs.reshape(NIC, 128).T),
        us_c=np.ascontiguousarray(us.reshape(NIC, 128).T),
        ds_b=np.ascontiguousarray(np.tile(ds[None, :], (128, 1))),
        wq1=_tile_w(np.asarray(inputs["L1_attn_Wq"], dtype=np.float32), 16),
        wk1=_tile_w(np.asarray(inputs["L1_attn_Wk"], dtype=np.float32), 16),
        wv1=_tile_w(np.asarray(inputs["L1_attn_Wv"], dtype=np.float32), 16),
        wo1=np.asarray(inputs["L1_attn_Wo"], dtype=np.float32),
        a_T=np.ascontiguousarray(A.reshape(NE * RK, D).T),
        b_sT=np.ascontiguousarray(B.transpose(0, 2, 1).reshape(NE * RK, D)),
        wr_n=np.asarray(inputs["L1_ffn_Wr"], dtype=np.float32),
        br_r=np.ascontiguousarray(
            np.asarray(inputs["L1_ffn_br"], dtype=np.float32)[None, :]),
        ident=np.eye(128, dtype=np.float32),
        shiftm=_shift_mat(),
        cosf=cosf, sinf=sinf, gqt=gq, qzb=qz,
        gkc=np.ascontiguousarray(gkcols),
    )
    return d


_CACHE = {}


def kernel(**inputs):
    key = "full"
    if key not in _CACHE:
        _CACHE[key] = build_program()
    nc, _ = _CACHE[key]
    in_maps = [_core_inputs(inputs, r) for r in range(NCORES)]
    res = run_bass_kernel_spmd(nc, in_maps, core_ids=list(range(NCORES)))
    out = np.concatenate([res.results[r]["y"] for r in range(NCORES)], axis=0)
    return out.reshape(1, SEQ, D).astype(np.float32)


# version 07
# speedup vs baseline: 2.0168x; 1.0874x over previous
from contextlib import ExitStack
import math
import numpy as np
import ml_dtypes
import concourse.bass as bass
import concourse.mybir as mybir
from concourse.bass_utils import run_bass_kernel_spmd

F32 = mybir.dt.float32
BF16 = mybir.dt.bfloat16
AF = mybir.ActivationFunctionType
ALU = mybir.AluOpType
AX = mybir.AxisListType.X

NCORES = 8
T = 1024
NT = 8
D = 2048
ND = 16
HD = 128
SEQ = 8192
NKC = 64
INTER = 5632
NIC = 44
NE, RK = 8, 64
ISQ = 1.0 / math.sqrt(128.0)


class Ctr:
    def __init__(self, ctx, nc, name):
        self.sem = ctx.enter_context(nc.semaphore(name))
        self.v = 0

    def inc(self, inst, k=1):
        inst.then_inc(self.sem, k)
        self.v += k
        return self.v


DEV_DUMP_KV = False


def build_program(stop_after=None, tap_x=()):
    nc = bass.Bass("TRN2", target_bir_lowering=False)
    din = {}

    def dI(name, shape, dtype=F32):
        din[name] = nc.dram_tensor(name, shape, dtype, kind="ExternalInput").ap()
        return din[name]

    x_in = dI("x", [T, D])
    wq0 = dI("wq0", [D, 16 * HD])
    wk0 = dI("wk0", [4 * HD, D])
    wv0 = dI("wv0", [D, 4 * HD])
    wo0 = dI("wo0", [16 * HD, D])
    gwT = dI("gwT", [INTER, D])
    uwT = dI("uwT", [INTER, D])
    dwT = dI("dwT", [INTER, D])
    gs_c = dI("gs_c", [128, NIC])
    us_c = dI("us_c", [128, NIC])
    ds_b = dI("ds_b", [128, D])
    wq1 = dI("wq1", [D, D])
    wk1 = dI("wk1", [D, D])
    wv1 = dI("wv1", [D, D])
    wo1 = dI("wo1", [D, D])
    a_T = dI("a_T", [D, NE * RK])
    b_sT = dI("b_sT", [NE * RK, D])
    wr_n = dI("wr_n", [NE, D])
    br_r = dI("br_r", [1, NE])
    ident_in = dI("ident", [128, 128])
    shift_in = dI("shiftm", [128, 128])
    cosf_in = dI("cosf", [128, T])
    sinf_in = dI("sinf", [128, T])
    gq_in = dI("gqt", [128, T])
    qzb_in = dI("qzb", [128, T])
    gkc_in = dI("gkc", [128, NKC])

    y_out = nc.dram_tensor("y", [T, D], F32, kind="ExternalOutput").ap()
    taps = {}
    for tname in tap_x:
        taps[tname] = nc.dram_tensor("tap_" + tname, [T, D], F32,
                                     kind="ExternalOutput").ap()

    kin = [nc.dram_tensor("kin0", [128, 4 * T], BF16),
           nc.dram_tensor("kin1", [128, 16 * T], BF16)]
    vin = [nc.dram_tensor("vin0", [T, 4 * HD], BF16),
           nc.dram_tensor("vin1", [T, 16 * HD], BF16)]
    kg = [nc.dram_tensor("kg0", [NCORES * 128, 4 * T], BF16),
          nc.dram_tensor("kg1", [NCORES * 128, 16 * T], BF16)]
    vg = [nc.dram_tensor("vg0", [SEQ, 4 * HD], BF16),
          nc.dram_tensor("vg1", [SEQ, 16 * HD], BF16)]
    kml = [nc.dram_tensor("kml0", [1, 4], F32),
           nc.dram_tensor("kml1", [1, 16], F32)]
    kmg = [nc.dram_tensor("kmg0", [1, 4], F32),
           nc.dram_tensor("kmg1", [1, 16], F32)]

    ctx = ExitStack()
    with ctx:
        E = ctx.enter_context

        X = E(nc.sbuf_tensor("X", [128, NT * D], F32))
        xnT = E(nc.sbuf_tensor("xnT", [128, ND * T], BF16))
        ident = E(nc.sbuf_tensor("ident_s", [128, 128], F32))
        ones_col_bf = E(nc.sbuf_tensor("ones_col_bf", [128, 1], BF16))
        ones_row_f = E(nc.sbuf_tensor("ones_row_f", [1, 128], F32))
        eps_col = E(nc.sbuf_tensor("eps_col", [128, 1], F32))
        negc = [E(nc.sbuf_tensor("negc0", [128, 16], F32)),
                E(nc.sbuf_tensor("negc1", [128, 16], F32))]
        nrow = E(nc.sbuf_tensor("nrow", [1, 1024], F32))
        scr = [E(nc.sbuf_tensor(f"scr{i}", [128, D], F32)) for i in range(3)]
        psA = E(nc.psum_tensor("psA", [128, 1024], F32))
        psB = E(nc.psum_tensor("psB", [128, 1024], F32))
        psC = E(nc.psum_tensor("psC", [128, 1024], F32))
        psD = E(nc.psum_tensor("psD", [128, 1024], F32))
        lay = {}
        _anon = [0]

        import os
        import concourse.bass_primitives as _bp

        def memdbg(tag):
            if os.environ.get("KMEMDBG"):
                print("MEM", tag, nc.bytes_allocated(_bp.MemorySpace.SBUF))

        def sb(shape, dtype):
            _anon[0] += 1
            return nc.sbuf_tensor(f"tt{_anon[0]}", shape, dtype)

        s_ld = Ctr(ctx, nc, "s_ld")
        s_st = Ctr(ctx, nc, "s_st")
        s_ldp = Ctr(ctx, nc, "s_ldp")
        s_cc = [Ctr(ctx, nc, f"s_cc{i}") for i in range(6)]
        cA = Ctr(ctx, nc, "cA")
        cB = Ctr(ctx, nc, "cB")
        cC = Ctr(ctx, nc, "cC")
        cD = Ctr(ctx, nc, "cD")
        cE = Ctr(ctx, nc, "cE")
        cF = Ctr(ctx, nc, "cF")
        cG = Ctr(ctx, nc, "cG")
        cH = Ctr(ctx, nc, "cH")
        cI = Ctr(ctx, nc, "cI")
        cJ = Ctr(ctx, nc, "cJ")
        s_kv = Ctr(ctx, nc, "s_kv")
        s_ss = Ctr(ctx, nc, "s_ss")
        s_se = Ctr(ctx, nc, "s_se")
        s_sm = Ctr(ctx, nc, "s_sm")
        s_sa = Ctr(ctx, nc, "s_sa")
        s_hd1 = Ctr(ctx, nc, "s_hd1")
        s_hd2 = Ctr(ctx, nc, "s_hd2")
        s_hd3 = Ctr(ctx, nc, "s_hd3")
        s_hpe = Ctr(ctx, nc, "s_hpe")

        with nc.Block("init") as blk:
            @blk.sync
            def _(sync):
                s_ld.inc(sync.dma_start(
                    out=X[:, :].rearrange("p (t d) -> p t d", d=D),
                    in_=x_in.rearrange("(t p) d -> p t d", p=128)), 16)
                s_ld.inc(sync.dma_start(out=ident[:, :], in_=ident_in), 16)

            @blk.vector
            def _(vector):
                vector.memset(ones_col_bf[:, :], 1.0)
                vector.memset(eps_col[:, :], 1e-5)
                vector.memset(ones_row_f[:, :], 1.0)

        def phase_ln(tag):
            with ExitStack() as pctx, nc.Block("ln_" + tag) as blk:
                P = pctx.enter_context
                nst = P(sb([128, 6 * NT], F32))
                sqd = P(sb([128, D], BF16))

                def sl(j, k):
                    return nst[:, 6 * j + k:6 * j + k + 1]

                Ba = cA.v
                Ba2 = cB.v
                Bd = cC.v
                Bd2 = cD.v
                Bp = cE.v
                Bev = cF.v

                @blk.scalar
                def _(scalar):
                    scalar.wait_ge(s_ld.sem, 32)
                    for j in range(NT):
                        cA.inc(scalar.activation(
                            sqd[:, :], X[:, j * D:(j + 1) * D], AF.Square,
                            accum_out=sl(j, 1)))
                    for j in range(NT):
                        scalar.wait_ge(cC.sem, Bd + j + 1)
                        cB.inc(scalar.activation(
                            sl(j, 4), sl(j, 3), AF.Sqrt, bias=eps_col[:, :]))

                @blk.vector
                def _(vector):
                    vector.wait_ge(s_ld.sem, 32)
                    for j in range(NT):
                        xj = X[:, j * D:(j + 1) * D]
                        h = cH.inc(vector.reduce_sum(sl(j, 0), xj, axis=AX,
                                                     negate=True))
                        vector.wait_ge(cH.sem, h)
                        h = cH.inc(vector.tensor_scalar(
                            sl(j, 0), sl(j, 0), 1.0 / D, None, op0=ALU.mult))
                        vector.wait_ge(cH.sem, h)
                        h = cH.inc(vector.tensor_tensor(
                            sl(j, 2), sl(j, 0), sl(j, 0), ALU.mult))
                        vector.wait_ge(cH.sem, h)
                        vector.wait_ge(cA.sem, Ba + j + 1)
                        cC.inc(vector.scalar_tensor_tensor(
                            sl(j, 3), sl(j, 1), 1.0 / D, sl(j, 2),
                            op0=ALU.mult, op1=ALU.subtract))
                    xv = xnT[:, :].rearrange("p (d t) -> p d t", t=T)

                    def evac(j):
                        vector.wait_ge(cE.sem, Bp + 2 * j + 1)
                        cF.inc(vector.tensor_copy(
                            xv[:, 0:8, j * 128:(j + 1) * 128],
                            psC[:, :].rearrange("p (a b) -> p a b", b=128)))
                        vector.wait_ge(cE.sem, Bp + 2 * j + 2)
                        cF.inc(vector.tensor_copy(
                            xv[:, 8:16, j * 128:(j + 1) * 128],
                            psD[:, :].rearrange("p (a b) -> p a b", b=128)))

                    for j in range(NT):
                        xj = X[:, j * D:(j + 1) * D]
                        vector.wait_ge(cB.sem, Ba2 + j + 1)
                        h = cH.inc(vector.reciprocal(sl(j, 5), sl(j, 4)))
                        vector.wait_ge(cH.sem, h)
                        if j >= 2:
                            vector.wait_ge(cE.sem, Bp + 2 * (j - 2) + 2)
                        cD.inc(vector.tensor_scalar(
                            scr[j % 2][:, :], xj, sl(j, 0), sl(j, 5),
                            op0=ALU.add, op1=ALU.mult))
                        if j >= 1:
                            evac(j - 1)
                    evac(NT - 1)

                @blk.tensor
                def _(tensor):
                    for j in range(NT):
                        tensor.wait_ge(cD.sem, Bd2 + j + 1)
                        if j >= 1:
                            tensor.wait_ge(cF.sem, Bev + 2 * j)
                        for d in range(8):
                            mm = tensor.transpose(
                                psC[:, d * 128:(d + 1) * 128],
                                scr[j % 2][:, d * 128:(d + 1) * 128],
                                ident[:, :])
                        cE.inc(mm)
                        for d in range(8, 16):
                            mm = tensor.transpose(
                                psD[:, (d - 8) * 128:(d - 7) * 128],
                                scr[j % 2][:, d * 128:(d + 1) * 128],
                                ident[:, :])
                        cE.inc(mm)

        def phase_proj(l, w, NP, kind):
            rope = (l == 0)
            qz = (l == 1 and kind == "q")
            memdbg(f"proj_{kind}_l{l}")
            with ExitStack() as pctx, nc.Block(f"proj_{kind}_l{l}") as blk:
                P = pctx.enter_context
                wbf = [P(sb([128, D], BF16)) for _ in range(3)]
                kts = [P(sb([128, T], BF16)) for _ in range(2)]
                sqb = P(sb([128, T], BF16))
                qf32 = [scr[0][:, 0:1024], scr[1][:, 0:1024]]
                ps0 = [psA, psC]
                ps1 = [psB, psD]
                consts = []
                if kind == "k":
                    if rope:
                        consts = [(cosf_in, lay["cosf"]), (sinf_in, lay["sinf"]),
                                  (shift_in, lay["shiftm"])]
                    if l == 1:
                        consts = [(gq_in, lay["gqt"]), (qzb_in, lay["qzb"]),
                                  (gkc_in, lay["gkc"])]
                if rope:
                    cosf, sinf, shiftm = (lay["cosf"], lay["sinf"],
                                          lay["shiftm"])
                if qz:
                    qzb = lay["qzb"]
                Bl = s_ld.v
                Bldp = s_ldp.v
                Bp1 = cB.v
                Bqf = cC.v
                Broll = cD.v
                Bdst = cE.v
                Bsq = cF.v
                Bnm = cG.v
                Bnr = cI.v
                Bkd = s_st.v
                NCST = len(consts)
                if kind == "k":
                    lay["c_ld"] = s_ld.v + 16 * NCST

                @blk.sync
                def _(sync):
                    for csrc, cdst in consts:
                        s_ld.inc(sync.dma_start(out=cdst[:, :], in_=csrc), 16)
                    if kind == "k":
                        for i in range(NP):
                            sync.wait_ge(cE.sem, Bdst + i + 1)
                            s_st.inc(sync.dma_start(
                                out=kin[l].ap()[:, i * T:(i + 1) * T],
                                in_=kts[i % 2][:, :]), 16)

                @blk.gpsimd
                def _(gpsimd):
                    for i in range(NP):
                        if i >= 3:
                            gpsimd.wait_ge(cB.sem, Bp1 + i - 2)
                        s_ldp.inc(gpsimd.dma_start(
                            out=wbf[i % 3][:, :],
                            in_=w.rearrange("(fc p) k -> fc p k", p=128)[i]), 16)
                    if kind == "q":
                        gpsimd.wait_ge(s_cc[3 * l + 2].sem, 1)
                        s_ldp.inc(gpsimd.dma_start(
                            out=nrow[0:1, 48:48 + (16 if l else 4)],
                            in_=kmg[l].ap()), 16)

                @blk.tensor
                def _(tensor):
                    if rope or qz:
                        tensor.wait_ge(s_ld.sem, lay.get("c_ld", 0))
                    for i in range(NP):
                        p0, p1 = ps0[i % 2], ps1[i % 2]
                        tensor.wait_ge(s_ldp.sem, Bldp + 16 * (i + 1))
                        if i >= 2:
                            if rope:
                                tensor.wait_ge(cC.sem, Bqf + i - 1)
                            else:
                                tensor.wait_ge(cE.sem, Bdst + i - 1)
                        for d in range(ND):
                            for hf in range(2):
                                mm = tensor.matmul(
                                    p0[:, hf * 512:(hf + 1) * 512],
                                    wbf[i % 3][:, d * 128:(d + 1) * 128],
                                    xnT[:, d * T + hf * 512:d * T + (hf + 1) * 512],
                                    start=(d == 0), stop=(d == ND - 1),
                                    skip_group_check=True)
                        cB.inc(mm)
                        if rope:
                            tensor.wait_ge(cC.sem, Bqf + i + 1)
                            if i >= 2:
                                tensor.wait_ge(cI.sem, Bnr + i - 1)
                            for hf in range(2):
                                mm = tensor.matmul(
                                    p1[:, hf * 512:(hf + 1) * 512],
                                    shiftm[:, :],
                                    qf32[i % 2][:, hf * 512:(hf + 1) * 512],
                                    start=True, stop=True,
                                    skip_group_check=True)
                            cD.inc(mm)
                        tensor.wait_ge(cF.sem, Bsq + i + 1)
                        if not rope and i >= 2:
                            tensor.wait_ge(cI.sem, Bnr + i - 1)
                        for hf in range(2):
                            mm = tensor.matmul(
                                p1[0:1, hf * 512:(hf + 1) * 512],
                                ones_col_bf[:, :],
                                sqb[:, hf * 512:(hf + 1) * 512],
                                start=True, stop=True, skip_group_check=True)
                        cG.inc(mm)
                    if kind == "q":
                        tensor.wait_ge(cC.sem,
                                       Bqf + (NP if rope else 0) + 1)
                        cB.inc(tensor.matmul(psA[:, 0:16], ones_row_f[:, :],
                                             nrow[0:1, 96:112], start=True,
                                             stop=True, skip_group_check=True))

                @blk.vector
                def _(vector):
                    if rope or qz:
                        vector.wait_ge(s_ld.sem, lay.get("c_ld", 0))
                    for i in range(NP):
                        p0, p1 = ps0[i % 2], ps1[i % 2]
                        dest = (QT[:, i * T:(i + 1) * T] if kind == "q"
                                else kts[i % 2][:, :])
                        vector.wait_ge(cB.sem, Bp1 + i + 1)
                        if kind == "k" and i >= 2:
                            vector.wait_ge(s_st.sem, Bkd + 16 * (i - 1))
                        if i >= 1:
                            vector.wait_ge(cG.sem, Bnm + i)
                        if rope:
                            h = cC.inc(vector.tensor_copy(qf32[i % 2][:, :],
                                                          p0[:, :]))
                            vector.wait_ge(cD.sem, Broll + i + 1)
                            vector.wait_ge(cC.sem, h)
                            h = cH.inc(vector.tensor_tensor(
                                scr[2][:, 0:1024], qf32[i % 2][:, :],
                                cosf[:, :], ALU.mult))
                            vector.wait_ge(cH.sem, h)
                            h = cH.inc(vector.tensor_tensor(
                                scr[2][:, 1024:2048], p1[:, :], sinf[:, :],
                                ALU.mult))
                            vector.wait_ge(cH.sem, h)
                            cE.inc(vector.tensor_tensor(
                                dest, scr[2][:, 0:1024], scr[2][:, 1024:2048],
                                ALU.add))
                        else:
                            if qz:
                                cE.inc(vector.tensor_tensor(
                                    dest, p0[:, :], qzb[:, :], ALU.mult))
                            else:
                                cE.inc(vector.tensor_copy(dest, p0[:, :]))
                        vector.wait_ge(cE.sem, Bdst + i + 1)
                        cF.inc(vector.tensor_tensor(sqb[:, :], dest, dest,
                                                    ALU.mult))
                        vector.wait_ge(cG.sem, Bnm + i + 1)
                        slot = (nrow[0:1, i:i + 1] if kind == "q"
                                else nrow[0:1, 32 + i:32 + i + 1])
                        cI.inc(vector.reduce_max(slot, p1[0:1, :], axis=AX))
                    if kind == "q":
                        HKV = 16 if l == 1 else 4
                        vector.wait_ge(s_ldp.sem, s_ldp.v)
                        vector.wait_ge(cI.sem, Bnr + NP)
                        if HKV == 16:
                            h = cH.inc(vector.tensor_tensor(
                                nrow[0:1, 64:80], nrow[0:1, 0:16],
                                nrow[0:1, 48:64], ALU.mult))
                        else:
                            h = cH.inc(vector.tensor_tensor(
                                nrow[0:1, 64:80].rearrange(
                                    "p (a b) -> p a b", b=4),
                                nrow[0:1, 0:16].rearrange(
                                    "p (a b) -> p a b", b=4),
                                nrow[0:1, 48:52].rearrange(
                                    "p (a b) -> p a b", b=1)
                                .broadcast_to([1, 4, 4]), ALU.mult))
                        vector.wait_ge(cH.sem, h)
                        vector.wait_ge(cA.sem, cA.v + 1)
                        cC.inc(vector.tensor_scalar(
                            nrow[0:1, 96:112], nrow[0:1, 80:96], -1.0, None,
                            op0=ALU.mult))
                        vector.wait_ge(cB.sem, cB.v)
                        cD.inc(vector.tensor_copy(negc[l][:, :], psA[:, 0:16]))

                @blk.scalar
                def _(scalar):
                    if kind == "q":
                        scalar.wait_ge(cH.sem, cH.v)
                        cA.inc(scalar.activation(nrow[0:1, 80:96],
                                                 nrow[0:1, 64:80], AF.Sqrt,
                                                 scale=1.0 / 128.0))

        def phase_v_cached(l, wv, HKV, knrow_hkv):
            FW = HKV * HD
            memdbg(f"v_l{l}")
            with ExitStack() as pctx, nc.Block(f"v_l{l}") as blk:
                P = pctx.enter_context
                wvbf = P(sb([128, ND * FW], BF16))
                vstg = [P(sb([128, FW], BF16)) for _ in range(2)]
                Bld = s_ldp.v
                Bmm = cB.v
                Bev = cC.v
                Bst = s_st.v

                @blk.gpsimd
                def _(gpsimd):
                    for d in range(ND):
                        s_ldp.inc(gpsimd.dma_start(
                            out=wvbf[:, d * FW:(d + 1) * FW],
                            in_=wv[d * 128:(d + 1) * 128, :]), 16)
                    gpsimd.wait_ge(s_st.sem, Bst + 16 * NT)
                    s_cc[3 * l + 0].inc(gpsimd.collective_compute(
                        "AllGather", ALU.bypass,
                        replica_groups=[list(range(NCORES))],
                        ins=[kin[l].ap().opt()], outs=[kg[l].ap().opt()]))
                    s_cc[3 * l + 1].inc(gpsimd.collective_compute(
                        "AllGather", ALU.bypass,
                        replica_groups=[list(range(NCORES))],
                        ins=[vin[l].ap().opt()], outs=[vg[l].ap().opt()]))
                    s_ldp.inc(gpsimd.dma_start(
                        out=kml[l].ap(), in_=nrow[0:1, 32:32 + knrow_hkv]), 16)
                    gpsimd.wait_ge(s_ldp.sem, s_ldp.v)
                    s_cc[3 * l + 2].inc(gpsimd.collective_compute(
                        "AllReduce", ALU.max,
                        replica_groups=[list(range(NCORES))],
                        ins=[kml[l].ap().opt()], outs=[kmg[l].ap().opt()]))

                @blk.tensor
                def _(tensor):
                    tensor.wait_ge(s_ldp.sem, Bld + 16 * ND)
                    for t in range(NT):
                        p0 = [psA, psB][t % 2]
                        if t >= 2:
                            tensor.wait_ge(cC.sem, Bev + t - 1)
                        for d in range(ND):
                            mm = tensor.matmul(
                                p0[:, 0:FW],
                                xnT[:, d * T + t * 128:d * T + (t + 1) * 128],
                                wvbf[:, d * FW:(d + 1) * FW],
                                start=(d == 0), stop=(d == ND - 1),
                                skip_group_check=True)
                        cB.inc(mm)

                @blk.vector
                def _(vector):
                    for t in range(NT):
                        p0 = [psA, psB][t % 2]
                        vector.wait_ge(cB.sem, Bmm + t + 1)
                        if t >= 2:
                            vector.wait_ge(s_st.sem, Bst + 16 * (t - 1))
                        cC.inc(vector.tensor_copy(vstg[t % 2][:, :],
                                                  p0[:, 0:FW]))

                @blk.sync
                def _(sync):
                    for t in range(NT):
                        sync.wait_ge(cC.sem, Bev + t + 1)
                        s_st.inc(sync.dma_start(
                            out=vin[l].ap()[t * 128:(t + 1) * 128, :],
                            in_=vstg[t % 2][:, :]), 16)

        def phase_v_fmajor(l, wv, knrow_hkv):
            memdbg(f"v_l{l}")
            with ExitStack() as pctx, nc.Block(f"v_l{l}") as blk:
                P = pctx.enter_context
                wbf = [P(sb([128, D], BF16)) for _ in range(3)]
                vstg = [P(sb([128, NT * 128], BF16)) for _ in range(2)]
                Bld = s_ldp.v
                Bp1 = cB.v
                Bf = cC.v
                Btr = cD.v
                Bbe = cE.v
                Bst = s_st.v

                @blk.gpsimd
                def _(gpsimd):
                    for fc in range(16):
                        if fc >= 3:
                            gpsimd.wait_ge(cB.sem, Bp1 + fc - 2)
                        s_ldp.inc(gpsimd.dma_start(
                            out=wbf[fc % 3][:, :],
                            in_=wv.rearrange("(fc p) k -> fc p k", p=128)[fc]),
                            16)
                    gpsimd.wait_ge(s_st.sem, Bst + 16 * 16)
                    s_cc[3 * l + 0].inc(gpsimd.collective_compute(
                        "AllGather", ALU.bypass,
                        replica_groups=[list(range(NCORES))],
                        ins=[kin[l].ap().opt()], outs=[kg[l].ap().opt()]))
                    s_cc[3 * l + 1].inc(gpsimd.collective_compute(
                        "AllGather", ALU.bypass,
                        replica_groups=[list(range(NCORES))],
                        ins=[vin[l].ap().opt()], outs=[vg[l].ap().opt()]))
                    s_ldp.inc(gpsimd.dma_start(
                        out=kml[l].ap(), in_=nrow[0:1, 32:32 + knrow_hkv]), 16)
                    gpsimd.wait_ge(s_ldp.sem, s_ldp.v)
                    s_cc[3 * l + 2].inc(gpsimd.collective_compute(
                        "AllReduce", ALU.max,
                        replica_groups=[list(range(NCORES))],
                        ins=[kml[l].ap().opt()], outs=[kmg[l].ap().opt()]))

                @blk.tensor
                def _(tensor):
                    for fc in range(16):
                        p0 = [psA, psC][fc % 2]
                        p1 = [psB, psD][fc % 2]
                        tensor.wait_ge(s_ldp.sem, Bld + 16 * (fc + 1))
                        if fc >= 2:
                            tensor.wait_ge(cC.sem, Bf + fc - 1)
                        for d in range(ND):
                            for hf in range(2):
                                mm = tensor.matmul(
                                    p0[:, hf * 512:(hf + 1) * 512],
                                    wbf[fc % 3][:, d * 128:(d + 1) * 128],
                                    xnT[:, d * T + hf * 512:d * T + (hf + 1) * 512],
                                    start=(d == 0), stop=(d == ND - 1),
                                    skip_group_check=True)
                        cB.inc(mm)
                        tensor.wait_ge(cC.sem, Bf + fc + 1)
                        if fc >= 2:
                            tensor.wait_ge(cE.sem, Bbe + fc - 1)
                        for t in range(NT):
                            mm = tensor.transpose(
                                p1[:, t * 128:(t + 1) * 128],
                                scr[2][:, t * 128:(t + 1) * 128],
                                ident[:, :])
                        cD.inc(mm)

                @blk.vector
                def _(vector):
                    for fc in range(16):
                        p0 = [psA, psC][fc % 2]
                        p1 = [psB, psD][fc % 2]
                        vector.wait_ge(cB.sem, Bp1 + fc + 1)
                        if fc >= 1:
                            vector.wait_ge(cD.sem, Btr + fc)
                        h = cC.inc(vector.tensor_copy(scr[2][:, 0:1024],
                                                      p0[:, :]))
                        vector.wait_ge(cC.sem, h)
                        vector.wait_ge(cD.sem, Btr + fc + 1)
                        if fc >= 2:
                            vector.wait_ge(s_st.sem, Bst + 16 * (fc - 1))
                        cE.inc(vector.tensor_copy(vstg[fc % 2][:, :], p1[:, :]))

                @blk.sync
                def _(sync):
                    for fc in range(16):
                        sync.wait_ge(cE.sem, Bbe + fc + 1)
                        s_st.inc(sync.dma_start(
                            out=vin[l].ap().rearrange("(t p) f -> p t f", p=128)
                            [:, :, fc * 128:(fc + 1) * 128],
                            in_=vstg[fc % 2][:, :].rearrange(
                                "p (t f) -> p t f", f=128)), 16)

        def phase_flash(l, H, HKV, mask):
            C = NKC
            memdbg(f"flash_l{l}")
            with ExitStack() as pctx, nc.Block(f"flash_l{l}") as blk:
                P = pctx.enter_context
                NKV = 6
                kbuf = [P(sb([128, 128], BF16)) for _ in range(NKV)]
                vbuf = [P(sb([128, 129], BF16)) for _ in range(NKV)]
                NPB = 5 if mask else 4
                pbuf = [P(sb([128, T], BF16)) for _ in range(NPB)]
                rec8 = P(sb([128, 8], F32))
                dmy = nrow[0:1, 256:264]
                Bkv = s_kv.v
                Bss = s_ss.v
                Bse = s_se.v
                Bsm = s_sm.v
                Bsa = s_sa.v
                B1 = s_hd1.v
                B3 = s_hd3.v
                Bpe = s_hpe.v
                kgl = kg[l].ap()
                vgl = vg[l].ap()
                psS = [psA, psB]

                def kv_idx(qh, c):
                    kvh = qh // (H // HKV)
                    return (kgl[(c // 8) * 128:(c // 8 + 1) * 128,
                                kvh * T + (c % 8) * 128:
                                kvh * T + (c % 8 + 1) * 128],
                            vgl[c * 128:(c + 1) * 128,
                                kvh * 128:(kvh + 1) * 128])

                def psO(qc):
                    p = psC if qc < 4 else psD
                    return p[:, (qc % 4) * 256:(qc % 4) * 256 + 129]

                @blk.sync
                def _(sync):
                    sync.wait_ge(s_cc[3 * l + 0].sem, 1)
                    sync.wait_ge(s_cc[3 * l + 1].sem, 1)
                    for g in range(H * C):
                        qh, c = divmod(g, C)
                        if g >= NKV:
                            sync.wait_ge(s_sa.sem, Bsa + g - NKV + 1)
                        ksrc, vsrc = kv_idx(qh, c)
                        s_kv.inc(sync.dma_start(out=kbuf[g % NKV][:, :],
                                                in_=ksrc), 16)
                        s_kv.inc(sync.dma_start(out=vbuf[g % NKV][:, 0:128],
                                                in_=vsrc), 16)

                @blk.vector
                def _(vector):
                    if mask:
                        vector.wait_ge(s_ld.sem, lay.get("c_ld", 0))
                    for b in range(NKV):
                        inst = vector.memset(vbuf[b][:, 128:129], 1.0)
                    cJ.inc(inst)

                    def dve_headend(qh):
                        vector.wait_ge(s_sa.sem, Bsa + (qh + 1) * C)
                        h = cH.inc(vector.reciprocal(
                            rec8[:, 0:4].rearrange("p (a b) -> p a b", b=1),
                            psC[:, :].rearrange("p (a b) -> p a b", b=256)
                            [:, :, 128:129]))
                        h = cH.inc(vector.reciprocal(
                            rec8[:, 4:8].rearrange("p (a b) -> p a b", b=1),
                            psD[:, :].rearrange("p (a b) -> p a b", b=256)
                            [:, :, 128:129]))
                        vector.wait_ge(cH.sem, h)
                        for qc in range(8):
                            p = psC if qc < 4 else psD
                            inst2 = vector.tensor_scalar(
                                scr[0][:, qc * 128:(qc + 1) * 128],
                                p[:, (qc % 4) * 256:(qc % 4) * 256 + 128],
                                rec8[:, qc:qc + 1], None, op0=ALU.mult)
                        s_hd1.inc(inst2)
                        vector.wait_ge(s_hpe.sem, Bpe + qh + 1)
                        s_hd3.inc(vector.tensor_copy(
                            OTn[:, qh * T:(qh + 1) * T], psC[:, :]))

                    pend = None
                    for g in range(H * C):
                        qh, c = divmod(g, C)
                        if mask:
                            vector.wait_ge(s_se.sem, Bse + 2 * g + 3)
                            s_sm.inc(vector.scalar_tensor_tensor(
                                pbuf[g % NPB][:, :], lay["gqt"][:, :],
                                lay["gkc"][:, c:c + 1], pbuf[g % NPB][:, :],
                                op0=ALU.is_lt, op1=ALU.mult))
                            if g == H * C - 1:
                                s_sm.inc(vector.scalar_tensor_tensor(
                                    dmy[0:1, 1:2], lay["gqt"][0:1, 0:1],
                                    lay["gkc"][0:1, 0:1], dmy[0:1, 1:2],
                                    op0=ALU.is_lt, op1=ALU.mult))
                        if pend is not None:
                            dve_headend(pend)
                            pend = None
                        if c == C - 1:
                            pend = qh
                    dve_headend(pend)

                @blk.tensor
                def _(tensor):
                    def emit_S(g):
                        qh, c = divmod(g, C)
                        tensor.wait_ge(s_kv.sem, Bkv + 32 * (g + 1))
                        if g == 0:
                            tensor.wait_ge(cJ.sem, cJ.v)
                        for hf in range(2):
                            if g >= 2:
                                tensor.wait_ge(s_se.sem,
                                               Bse + 2 * (g - 2) + hf + 1)
                            mm = tensor.matmul(
                                psS[g % 2][:, hf * 512:(hf + 1) * 512],
                                kbuf[g % NKV][:, :],
                                QT[:, qh * T + hf * 512:qh * T + (hf + 1) * 512],
                                start=True, stop=True, skip_group_check=True)
                            s_ss.inc(mm)

                    def emit_AV(g):
                        qh, c = divmod(g, C)
                        if mask:
                            tensor.wait_ge(s_sm.sem, Bsm + g + 2)
                        else:
                            tensor.wait_ge(s_se.sem, Bse + 2 * g + 3)
                        if c == 0 and qh >= 1:
                            tensor.wait_ge(s_hd3.sem, B3 + qh)
                        for qc in range(8):
                            mm = tensor.matmul(
                                psO(qc),
                                pbuf[g % NPB][:, qc * 128:(qc + 1) * 128],
                                vbuf[g % NKV][:, :],
                                start=(c == 0 and qc % 2 == 0),
                                stop=(c == C - 1),
                                skip_group_check=True)
                        s_sa.inc(mm)

                    def emit_headend(p):
                        tensor.wait_ge(s_hd1.sem, B1 + p + 1)
                        for qc in range(8):
                            mm = tensor.transpose(
                                psC[:, qc * 128:(qc + 1) * 128],
                                scr[0][:, qc * 128:(qc + 1) * 128],
                                ident[:, :])
                        s_hpe.inc(mm)

                    pend_he = None
                    for g in range(H * C):
                        emit_S(g)
                        if pend_he is not None:
                            emit_headend(pend_he)
                            pend_he = None
                        if g >= 2:
                            emit_AV(g - 2)
                            if (g - 2) % C == C - 1:
                                pend_he = (g - 2) // C
                    emit_AV(H * C - 2)
                    emit_AV(H * C - 1)
                    emit_headend(H - 1)

                @blk.scalar
                def _(scalar):
                    for g in range(H * C):
                        qh, c = divmod(g, C)
                        for hf in range(2):
                            scalar.wait_ge(s_ss.sem, Bss + 2 * g + hf + 1)
                            if hf == 0 and g >= NPB:
                                scalar.wait_ge(s_sa.sem, Bsa + g - NPB + 1)
                            s_se.inc(scalar.activation(
                                pbuf[g % NPB][:, hf * 512:(hf + 1) * 512],
                                psS[g % 2][:, hf * 512:(hf + 1) * 512],
                                AF.Exp, bias=negc[l][:, qh:qh + 1], scale=ISQ))
                    s_se.inc(scalar.activation(
                        dmy[0:1, 0:1], psS[0][0:1, 0:1], AF.Exp,
                        bias=negc[l][0:1, 0:1], scale=ISQ))

        def phase_wo(l, wo):
            memdbg(f"wo_l{l}")
            with ExitStack() as pctx, nc.Block(f"wo_l{l}") as blk:
                P = pctx.enter_context
                wobf = QT
                Bld = s_ldp.v
                Bmm = cB.v
                Bx = cC.v

                @blk.gpsimd
                def _(gpsimd):
                    for grp in range(2):
                        if grp == 1:
                            gpsimd.wait_ge(cB.sem, Bmm + 8)
                        for h8 in range(8):
                            h = grp * 8 + h8
                            s_ldp.inc(gpsimd.dma_start(
                                out=wobf[:, h8 * D:(h8 + 1) * D],
                                in_=wo[h * 128:(h + 1) * 128, :]), 16)

                @blk.tensor
                def _(tensor):
                    for grp in range(2):
                        tensor.wait_ge(s_ldp.sem, Bld + 16 * 8 * (grp + 1))
                        for t in range(NT):
                            pl, ph = [(psA, psB), (psC, psD)][t % 2]
                            if grp * NT + t >= 2:
                                tensor.wait_ge(cC.sem, Bx + grp * NT + t - 1)
                            for h8 in range(8):
                                h = grp * 8 + h8
                                for hf in range(4):
                                    p = pl if hf < 2 else ph
                                    mm = tensor.matmul(
                                        p[:, (hf % 2) * 512:(hf % 2 + 1) * 512],
                                        OTn[:, h * T + t * 128:h * T + (t + 1) * 128],
                                        wobf[:, h8 * D + hf * 512:
                                             h8 * D + (hf + 1) * 512],
                                        start=(h8 == 0), stop=(h8 == 7),
                                        skip_group_check=True)
                            cB.inc(mm)

                @blk.vector
                def _(vector):
                    for grp in range(2):
                        for t in range(NT):
                            pl, ph = [(psA, psB), (psC, psD)][t % 2]
                            vector.wait_ge(cB.sem, Bmm + grp * NT + t + 1)
                            h = cH.inc(vector.tensor_tensor(
                                X[:, t * D:t * D + 1024],
                                X[:, t * D:t * D + 1024], pl[:, :], ALU.add))
                            vector.wait_ge(cH.sem, h)
                            cC.inc(vector.tensor_tensor(
                                X[:, t * D + 1024:(t + 1) * D],
                                X[:, t * D + 1024:(t + 1) * D], ph[:, :],
                                ALU.add))

        def phase_ffn0():
            memdbg("ffn0")
            with ExitStack() as pctx, nc.Block("ffn0") as blk:
                P = pctx.enter_context
                dsb = P(sb([128, D], F32))
                gs_sb = P(sb([128, NIC], F32))
                us_sb = P(sb([128, NIC], F32))
                xadd = scr[2][:, 1024:2048]
                h_sb = P(sb([128, NIC * 512], BF16))
                gbf = [P(sb([128, D], BF16)) for _ in range(2)]
                ubf = [P(sb([128, D], BF16)) for _ in range(2)]
                dbf = [P(sb([128, 1024], BF16))
                       for _ in range(2)]
                silu = [P(sb([128, 512], BF16))
                        for _ in range(2)]
                usb = [P(sb([128, 512], BF16))
                       for _ in range(2)]
                Bl = s_ld.v
                Bc = cA.v
                Bgu = cB.v
                Bsil = cC.v
                Buev = cD.v
                Bh = cE.v
                Bxa = cF.v
                Bdn = cG.v
                Bdc = cI.v
                Bdl = s_ldp.v
                BJ = cJ.v

                @blk.sync
                def _(sync):
                    cJ.inc(sync.dma_start(out=dsb[:, :], in_=ds_b), 16)
                    cJ.inc(sync.dma_start(out=gs_sb[:, :], in_=gs_c), 16)
                    cJ.inc(sync.dma_start(out=us_sb[:, :], in_=us_c), 16)

                @blk.gpsimd
                def _(gpsimd):
                    for hh in range(2):
                        for ic in range(NIC):
                            i = hh * NIC + ic
                            if i >= 2:
                                gpsimd.wait_ge(cB.sem, Bgu + i - 1)
                            s_ld.inc(gpsimd.dma_start(
                                out=gbf[i % 2][:, :],
                                in_=gwT.rearrange("(ic p) k -> ic p k", p=128)
                                [ic]), 16)
                            s_ld.inc(gpsimd.dma_start(
                                out=ubf[i % 2][:, :],
                                in_=uwT.rearrange("(ic p) k -> ic p k", p=128)
                                [ic]), 16)
                        for dmh in range(2):
                            for ic in range(NIC):
                                j = (hh * 2 + dmh) * NIC + ic
                                if j >= 2:
                                    gpsimd.wait_ge(cG.sem, Bdn + j - 1)
                                s_ldp.inc(gpsimd.dma_start(
                                    out=dbf[j % 2][:, :],
                                    in_=dwT[ic * 128:(ic + 1) * 128,
                                            dmh * 1024:(dmh + 1) * 1024]), 16)

                @blk.scalar
                def _(scalar):
                    for hh in range(2):
                        for ic in range(NIC):
                            i = hh * NIC + ic
                            scalar.wait_ge(cB.sem, Bgu + i + 1)
                            if i == 0:
                                scalar.wait_ge(cJ.sem, BJ + 48)
                            psGU = [psA, psB][i % 2]
                            cC.inc(scalar.activation(
                                silu[i % 2][:, :], psGU[:, 0:512], AF.Silu,
                                scale=gs_sb[:, ic:ic + 1]))

                @blk.tensor
                def _(tensor):
                    for hh in range(2):
                        for ic in range(NIC):
                            i = hh * NIC + ic
                            psGU = [psA, psB][i % 2]
                            tensor.wait_ge(s_ld.sem, Bl + 32 * (i + 1))
                            if i >= 2:
                                tensor.wait_ge(cC.sem, Bsil + i - 1)
                                tensor.wait_ge(cD.sem, Buev + i - 1)
                            if ic == 0 and hh == 1:
                                tensor.wait_ge(cF.sem, Bxa + 8)
                            for d in range(ND):
                                rhs = xnT[:, d * T + hh * 512:
                                          d * T + (hh + 1) * 512]
                                tensor.matmul(
                                    psGU[:, 0:512],
                                    gbf[i % 2][:, d * 128:(d + 1) * 128], rhs,
                                    start=(d == 0), stop=(d == ND - 1),
                                    skip_group_check=True)
                                mm = tensor.matmul(
                                    psGU[:, 512:1024],
                                    ubf[i % 2][:, d * 128:(d + 1) * 128], rhs,
                                    start=(d == 0), stop=(d == ND - 1),
                                    skip_group_check=True)
                            cB.inc(mm)
                        tensor.wait_ge(cE.sem, Bh + (hh + 1) * NIC)
                        pstch = [psA, psB, psC, psD]
                        for dmh in range(2):
                            for ic in range(NIC):
                                j = (hh * 2 + dmh) * NIC + ic
                                tensor.wait_ge(s_ldp.sem, Bdl + 16 * (j + 1))
                                if dmh == 0 and ic == 0:
                                    tensor.wait_ge(cC.sem, Bsil + (hh + 1) * NIC)
                                    tensor.wait_ge(cD.sem, Buev + (hh + 1) * NIC)
                                if ic == 0 and (hh * 2 + dmh) >= 1:
                                    tensor.wait_ge(cF.sem, Bxa + (hh * 2 + dmh) * 4)
                                for tch in range(4):
                                    for hf in range(2):
                                        mm = tensor.matmul(
                                            pstch[tch][:, hf * 512:(hf + 1) * 512],
                                            h_sb[:, ic * 512 + tch * 128:
                                                 ic * 512 + (tch + 1) * 128],
                                            dbf[j % 2][:, hf * 512:(hf + 1) * 512],
                                            start=(ic == 0), stop=(ic == NIC - 1),
                                            skip_group_check=True)
                                cG.inc(mm)

                @blk.vector
                def _(vector):
                    vector.wait_ge(cJ.sem, BJ + 48)
                    for hh in range(2):
                        for ic in range(NIC):
                            i = hh * NIC + ic
                            psGU = [psA, psB][i % 2]
                            vector.wait_ge(cB.sem, Bgu + i + 1)
                            h = cD.inc(vector.tensor_scalar(
                                usb[i % 2][:, :], psGU[:, 512:1024],
                                us_sb[:, ic:ic + 1], None, op0=ALU.mult))
                            vector.wait_ge(cD.sem, h)
                            vector.wait_ge(cC.sem, Bsil + i + 1)
                            cE.inc(vector.tensor_tensor(
                                h_sb[:, ic * 512:(ic + 1) * 512],
                                silu[i % 2][:, :], usb[i % 2][:, :], ALU.mult))
                        pstch = [psA, psB, psC, psD]
                        for dmh in range(2):
                            for tch in range(4):
                                vector.wait_ge(cG.sem,
                                               Bdn + (hh * 2 + dmh + 1) * NIC)
                                tg = hh * 4 + tch
                                xsl = X[:, tg * D + dmh * 1024:
                                        tg * D + (dmh + 1) * 1024]
                                h = cH.inc(vector.tensor_tensor(
                                    xadd, pstch[tch][:, :],
                                    dsb[:, dmh * 1024:(dmh + 1) * 1024],
                                    ALU.mult))
                                vector.wait_ge(cH.sem, h)
                                cF.inc(vector.tensor_tensor(
                                    xsl, xsl, xadd, ALU.add))

        def phase_ffn1():
            memdbg("ffn1")
            with ExitStack() as pctx, nc.Block("ffn1") as blk:
                P = pctx.enter_context
                abf = P(sb([128, ND * 512], BF16))
                bbf = P(sb([128, 4 * D], BF16))
                wrT = P(sb([128, ND * 8], BF16))
                wrst = P(sb([8, D], F32))
                brsb = P(sb([1, 8], F32))
                p8 = P(sb([128, 8], F32))
                ow = P(sb([128, 9], F32))
                sst = P(sb([128, 4], F32))
                hmf = P(sb([128, 512], F32))
                hTb = [P(sb([128, 512], BF16))
                       for _ in range(2)]
                Bl = s_ld.v
                Bc = cA.v
                Bex = cI.v
                Bpe = cB.v
                Btr = cG.v
                BwT = cD.v
                Bsx = cC.v
                Bmk = cF.v
                Bhev = cE.v
                BxH = cH.v

                @blk.sync
                def _(sync):
                    for d in range(ND):
                        if d >= 2:
                            sync.wait_ge(cA.sem, Bc + d - 1)
                        s_ld.inc(sync.dma_start(
                            out=scr[d % 2][:, 0:512],
                            in_=a_T[d * 128:(d + 1) * 128, :]), 16)
                    for b in range(4):
                        sync.wait_ge(cA.sem, Bc + ND + b)
                        s_ld.inc(sync.dma_start(
                            out=scr[b % 2][:, :],
                            in_=b_sT[b * 128:(b + 1) * 128, :]), 16)
                    s_ld.inc(sync.dma_start(out=wrst[:, :], in_=wr_n), 16)
                    s_ld.inc(sync.dma_start(out=brsb[:, :], in_=br_r), 16)

                @blk.scalar
                def _(scalar):
                    for d in range(ND):
                        scalar.wait_ge(s_ld.sem, Bl + 16 * (d + 1))
                        cA.inc(scalar.copy(abf[:, d * 512:(d + 1) * 512],
                                           scr[d % 2][:, 0:512]))
                    for b in range(4):
                        scalar.wait_ge(s_ld.sem, Bl + 16 * (ND + b + 1))
                        cA.inc(scalar.mul(bbf[:, b * D:(b + 1) * D],
                                          scr[b % 2][:, :], 1.0 / RK))
                    for t in range(NT):
                        scalar.wait_ge(cC.sem, Bsx + 5 * t + 1)
                        cI.inc(scalar.activation(
                            p8[:, :], psB[:, 0:8], AF.Exp,
                            bias=sst[:, 0:1]))

                @blk.tensor
                def _(tensor):
                    tensor.wait_ge(s_ld.sem, Bl + 16 * (ND + 4 + 2))
                    for d in range(ND):
                        mm = tensor.transpose(
                            psC[:, d * 8:(d + 1) * 8],
                            wrst[:, d * 128:(d + 1) * 128], ident[0:8, 0:8])
                    cB.inc(mm)
                    tensor.wait_ge(cA.sem, Bc + ND + 4)
                    tensor.wait_ge(cD.sem, BwT + 1)
                    for t in range(NT):
                        if t >= 1:
                            tensor.wait_ge(cI.sem, Bex + t)
                            tensor.wait_ge(cC.sem, Bsx + 5 * t)
                        for d in range(ND):
                            tensor.matmul(
                                psB[:, 0:8],
                                xnT[:, d * T + t * 128:d * T + (t + 1) * 128],
                                wrT[:, d * 8:(d + 1) * 8],
                                start=(d == 0), stop=False,
                                skip_group_check=True)
                        mm = tensor.matmul(
                            psB[:, 0:8], ones_row_f[:, :], brsb[:, :],
                            start=False, stop=True, skip_group_check=True)
                        cB.inc(mm)
                        if t >= 1:
                            tensor.wait_ge(cF.sem, Bmk + t)
                        for d in range(ND):
                            mm = tensor.matmul(
                                psA[:, 0:512],
                                xnT[:, d * T + t * 128:d * T + (t + 1) * 128],
                                abf[:, d * 512:(d + 1) * 512],
                                start=(d == 0), stop=(d == ND - 1),
                                skip_group_check=True)
                        cB.inc(mm)
                        tensor.wait_ge(cF.sem, Bmk + t + 1)
                        if t >= 1:
                            tensor.wait_ge(cE.sem, Bhev + t)
                        for b in range(4):
                            mm = tensor.transpose(
                                psB[:, 512 + b * 128:512 + (b + 1) * 128],
                                hmf[:, b * 128:(b + 1) * 128], ident[:, :])
                        cG.inc(mm)
                        tensor.wait_ge(cE.sem, Bhev + t + 1)
                        if t >= 1:
                            tensor.wait_ge(cH.sem, BxH + 2 * t)
                        for b in range(4):
                            for hf in range(4):
                                p = psC if hf < 2 else psD
                                mm = tensor.matmul(
                                    p[:, (hf % 2) * 512:(hf % 2 + 1) * 512],
                                    hTb[t % 2][:, b * 128:(b + 1) * 128],
                                    bbf[:, b * D + hf * 512:b * D + (hf + 1) * 512],
                                    start=(b == 0), stop=(b == 3),
                                    skip_group_check=True)
                        cB.inc(mm)

                @blk.vector
                def _(vector):
                    vector.wait_ge(cB.sem, Bpe + 1)
                    cD.inc(vector.tensor_copy(wrT[:, :], psC[:, 0:ND * 8]))
                    for t in range(NT):
                        vector.wait_ge(cB.sem, Bpe + 1 + 3 * t + 1)
                        h = cC.inc(vector.reduce_max(
                            sst[:, 0:1], psB[:, 0:8], axis=AX, negate=True))
                        vector.wait_ge(cC.sem, h)
                        h = cC.inc(vector.reduce_max(
                            sst[:, 1:2], psB[:, 0:8], axis=AX))
                        vector.wait_ge(cC.sem, h)
                        vector.wait_ge(cI.sem, Bex + t + 1)
                        h = cC.inc(vector.reduce_sum(
                            sst[:, 2:3], p8[:, :], axis=AX))
                        vector.wait_ge(cC.sem, h)
                        h = cC.inc(vector.reciprocal(sst[:, 3:4], sst[:, 2:3]))
                        vector.wait_ge(cC.sem, h)
                        h = cC.inc(vector.tensor_scalar(
                            ow[:, 0:8], psB[:, 0:8], sst[:, 1:2], sst[:, 3:4],
                            op0=ALU.is_ge, op1=ALU.mult))
                        vector.wait_ge(cC.sem, h)
                        vector.wait_ge(cB.sem, Bpe + 1 + 3 * t + 2)
                        if t >= 1:
                            vector.wait_ge(cG.sem, Btr + t)
                        for e in range(NE):
                            inst = vector.tensor_scalar(
                                hmf[:, e * 64:(e + 1) * 64],
                                psA[:, e * 64:(e + 1) * 64],
                                ow[:, e:e + 1], None, op0=ALU.mult)
                        cF.inc(inst)
                        vector.wait_ge(cG.sem, Btr + t + 1)
                        if t >= 2:
                            vector.wait_ge(cB.sem, Bpe + 1 + 3 * (t - 1))
                        cE.inc(vector.tensor_copy(hTb[t % 2][:, :],
                                                  psB[:, 512:1024]))
                        vector.wait_ge(cB.sem, Bpe + 1 + 3 * t + 3)
                        cH.inc(vector.tensor_tensor(
                            X[:, t * D:t * D + 1024],
                            X[:, t * D:t * D + 1024], psC[:, :], ALU.add))
                        cH.inc(vector.tensor_tensor(
                            X[:, t * D + 1024:(t + 1) * D],
                            X[:, t * D + 1024:(t + 1) * D], psD[:, :],
                            ALU.add))

        def dump_x(dest_ap):
            with nc.Block("dump") as blk:
                @blk.sync
                def _(sync):
                    s_st.inc(sync.dma_start(
                        out=dest_ap.rearrange("(t p) d -> p t d", p=128),
                        in_=X[:, :].rearrange("p (t d) -> p t d", d=D)), 16)
                    if DEV_DUMP_KV:
                        kgo = nc.dram_tensor("kg0_tap", [NCORES * 128, 4 * T],
                                             BF16, kind="ExternalOutput").ap()
                        vgo = nc.dram_tensor("vg0_tap", [SEQ, 4 * HD], BF16,
                                             kind="ExternalOutput").ap()
                        nco = nc.dram_tensor("negc_tap", [128, 16], F32,
                                             kind="ExternalOutput").ap()
                        s_st.inc(sync.dma_start(out=kgo, in_=kg[0].ap()), 16)
                        s_st.inc(sync.dma_start(out=vgo, in_=vg[0].ap()), 16)
                        s_st.inc(sync.dma_start(out=nco, in_=negc[0][:, :]), 16)
                    sync.wait_ge(s_st.sem, s_st.v)

        def attn_layer(l):
            global QT, OTn
            with ExitStack() as actx:
                A = actx.enter_context
                lay.clear()
                if l == 1:
                    lay["gqt"] = A(nc.sbuf_tensor("gqt_s", [128, T], F32))
                    lay["gkc"] = A(nc.sbuf_tensor("gkc_s", [128, NKC], F32))
                QT = A(nc.sbuf_tensor(f"QT{l}", [128, 16 * T], BF16))
                ictx = ExitStack()
                IA = ictx.enter_context
                if l == 0:
                    lay["cosf"] = IA(nc.sbuf_tensor("cosf_s", [128, T], F32))
                    lay["sinf"] = IA(nc.sbuf_tensor("sinf_s", [128, T], F32))
                    lay["shiftm"] = IA(nc.sbuf_tensor("shiftm_s",
                                                      [128, 128], F32))
                else:
                    lay["qzb"] = IA(nc.sbuf_tensor("qzb_s", [128, T], F32))
                if l == 0:
                    phase_proj(0, wk0, 4, "k")
                    phase_v_cached(0, wv0, 4, 4)
                    phase_proj(0, wq0, 16, "q")
                    ictx.close()
                    if DEV_DUMP_KV:
                        qtap = nc.dram_tensor("qt_tap", [128, 16 * T], BF16,
                                              kind="ExternalOutput").ap()
                        with nc.Block("qtdump") as qblk:
                            @qblk.sync
                            def _(sync):
                                s_st.inc(sync.dma_start(out=qtap,
                                                        in_=QT[:, :]), 16)
                                sync.wait_ge(s_st.sem, s_st.v)
                    OTn = A(nc.sbuf_tensor(f"OTn{l}", [128, 16 * T], BF16))
                    phase_flash(0, 16, 4, False)
                    if DEV_DUMP_KV:
                        otap = nc.dram_tensor("otn_tap", [128, 16 * T], BF16,
                                              kind="ExternalOutput").ap()
                        with nc.Block("otdump") as oblk:
                            @oblk.sync
                            def _(sync):
                                s_st.inc(sync.dma_start(out=otap,
                                                        in_=OTn[:, :]), 16)
                                sync.wait_ge(s_st.sem, s_st.v)
                    phase_wo(0, wo0)
                else:
                    phase_proj(1, wk1, 16, "k")
                    phase_v_fmajor(1, wv1, 16)
                    phase_proj(1, wq1, 16, "q")
                    ictx.close()
                    OTn = A(nc.sbuf_tensor(f"OTn{l}", [128, 16 * T], BF16))
                    phase_flash(1, 16, 16, True)
                    phase_wo(1, wo1)

        phase_ln("l0a")
        attn_layer(0)
        if "l0_attn" in taps:
            dump_x(taps["l0_attn"])
        if stop_after == "l0_attn":
            dump_x(y_out)
            return nc, din
        phase_ln("l0f")
        phase_ffn0()
        if "l0_ffn" in taps:
            dump_x(taps["l0_ffn"])
        if stop_after == "l0_ffn":
            dump_x(y_out)
            return nc, din
        phase_ln("l1a")
        attn_layer(1)
        if "l1_attn" in taps:
            dump_x(taps["l1_attn"])
        if stop_after == "l1_attn":
            dump_x(y_out)
            return nc, din
        phase_ln("l1f")
        phase_ffn1()
        dump_x(y_out)
    return nc, din



def _rope_tables(pos):
    inv = 1.0 / (10000.0 ** (np.arange(0, HD, 2, dtype=np.float32) / HD))
    fr = pos[:, None].astype(np.float32) * inv[None, :]
    emb = np.concatenate([fr, fr], axis=-1)
    return np.cos(emb).astype(np.float32), np.sin(emb).astype(np.float32)


def _shift_mat():
    m = np.zeros((128, 128), dtype=np.float32)
    for k in range(128):
        m[k, (k + 1) % 128] = 1.0
    return m


def _tile_w(w, nf):
    K, F = w.shape
    a = w.reshape(K // 128, 128, nf, F // nf)
    a = a.transpose(2, 1, 0, 3)
    return np.ascontiguousarray(a.reshape(F, K))


def _core_inputs(inputs, r):
    x = np.ascontiguousarray(
        inputs["L0_attn_x"][0, r * T:(r + 1) * T, :]).astype(np.float32, copy=False)
    pos = np.arange(r * T, (r + 1) * T)
    cos, sin = _rope_tables(pos)
    cosf = np.ascontiguousarray(cos.T)
    sinf = np.ascontiguousarray(sin.T)
    gq = np.tile(pos.astype(np.float32)[None, :], (128, 1))
    qz = np.ones((128, T), dtype=np.float32)
    if r == NCORES - 1:
        gq[:, T - 1] = -1.0
        qz[:, T - 1] = 0.0
    gkcols = (np.arange(NKC, dtype=np.float32)[None, :] * 128
              + np.arange(128, dtype=np.float32)[:, None])
    gs = inputs["L0_ffn_gate_s"].astype(np.float32)
    us = inputs["L0_ffn_up_s"].astype(np.float32)
    ds = inputs["L0_ffn_down_s"].astype(np.float32)
    A = np.asarray(inputs["L1_ffn_A"], dtype=np.float32)
    B = np.asarray(inputs["L1_ffn_B"], dtype=np.float32)
    d = dict(
        x=x,
        wq0=_tile_w(np.asarray(inputs["L0_attn_Wq"], dtype=np.float32), 16),
        wk0=_tile_w(np.asarray(inputs["L0_attn_Wk"], dtype=np.float32), 4),
        wv0=np.asarray(inputs["L0_attn_Wv"], dtype=np.float32),
        wo0=np.asarray(inputs["L0_attn_Wo"], dtype=np.float32),
        gwT=_tile_w(np.asarray(inputs["L0_ffn_gate_w"], dtype=np.float32).T, NIC),
        uwT=_tile_w(np.asarray(inputs["L0_ffn_up_w"], dtype=np.float32).T, NIC),
        dwT=np.ascontiguousarray(np.asarray(inputs["L0_ffn_down_w"],
                                            dtype=np.float32).T),
        gs_c=np.ascontiguousarray(gs.reshape(NIC, 128).T),
        us_c=np.ascontiguousarray(us.reshape(NIC, 128).T),
        ds_b=np.ascontiguousarray(np.tile(ds[None, :], (128, 1))),
        wq1=_tile_w(np.asarray(inputs["L1_attn_Wq"], dtype=np.float32), 16),
        wk1=_tile_w(np.asarray(inputs["L1_attn_Wk"], dtype=np.float32), 16),
        wv1=_tile_w(np.asarray(inputs["L1_attn_Wv"], dtype=np.float32), 16),
        wo1=np.asarray(inputs["L1_attn_Wo"], dtype=np.float32),
        a_T=np.ascontiguousarray(A.reshape(NE * RK, D).T),
        b_sT=np.ascontiguousarray(B.transpose(0, 2, 1).reshape(NE * RK, D)),
        wr_n=np.asarray(inputs["L1_ffn_Wr"], dtype=np.float32),
        br_r=np.ascontiguousarray(
            np.asarray(inputs["L1_ffn_br"], dtype=np.float32)[None, :]),
        ident=np.eye(128, dtype=np.float32),
        shiftm=_shift_mat(),
        cosf=cosf, sinf=sinf, gqt=gq, qzb=qz,
        gkc=np.ascontiguousarray(gkcols),
    )
    return d


_CACHE = {}


def kernel(**inputs):
    key = "full"
    if key not in _CACHE:
        _CACHE[key] = build_program()
    nc, _ = _CACHE[key]
    in_maps = [_core_inputs(inputs, r) for r in range(NCORES)]
    res = run_bass_kernel_spmd(nc, in_maps, core_ids=list(range(NCORES)))
    out = np.concatenate([res.results[r]["y"] for r in range(NCORES)], axis=0)
    return out.reshape(1, SEQ, D).astype(np.float32)


# version 08 (final)
# speedup vs baseline: 2.0419x; 1.0124x over previous
from contextlib import ExitStack
import math
import numpy as np
import ml_dtypes
import concourse.bass as bass
import concourse.mybir as mybir
from concourse.bass_utils import run_bass_kernel_spmd

F32 = mybir.dt.float32
BF16 = mybir.dt.bfloat16
AF = mybir.ActivationFunctionType
ALU = mybir.AluOpType
AX = mybir.AxisListType.X

NCORES = 8
T = 1024
NT = 8
D = 2048
ND = 16
HD = 128
SEQ = 8192
NKC = 64
INTER = 5632
NIC = 44
NE, RK = 8, 64
ISQ = 1.0 / math.sqrt(128.0)


class Ctr:
    def __init__(self, ctx, nc, name):
        self.sem = ctx.enter_context(nc.semaphore(name))
        self.v = 0

    def inc(self, inst, k=1):
        inst.then_inc(self.sem, k)
        self.v += k
        return self.v


DEV_DUMP_KV = False


def build_program(stop_after=None, tap_x=()):
    nc = bass.Bass("TRN2", target_bir_lowering=False)
    din = {}

    def dI(name, shape, dtype=F32):
        din[name] = nc.dram_tensor(name, shape, dtype, kind="ExternalInput").ap()
        return din[name]

    x_in = dI("x", [T, D])
    wq0 = dI("wq0", [D, 16 * HD])
    wk0 = dI("wk0", [4 * HD, D])
    wv0 = dI("wv0", [D, 4 * HD])
    wo0 = dI("wo0", [16 * HD, D])
    gwT = dI("gwT", [INTER, D])
    uwT = dI("uwT", [INTER, D])
    dwT = dI("dwT", [INTER, D])
    gs_c = dI("gs_c", [128, NIC])
    us_c = dI("us_c", [128, NIC])
    ds_b = dI("ds_b", [128, D])
    wq1 = dI("wq1", [D, D])
    wk1 = dI("wk1", [D, D])
    wv1 = dI("wv1", [D, D])
    wo1 = dI("wo1", [D, D])
    a_T = dI("a_T", [D, NE * RK])
    b_sT = dI("b_sT", [NE * RK, D])
    wr_n = dI("wr_n", [NE, D])
    br_r = dI("br_r", [1, NE])
    ident_in = dI("ident", [128, 128])
    shift_in = dI("shiftm", [128, 128])
    cosf_in = dI("cosf", [128, T])
    sinf_in = dI("sinf", [128, T])
    qzb_in = dI("qzb", [128, T])
    idxk_in = dI("idxk", [128, NKC], mybir.dt.uint32)
    qloc_in = dI("qloc", [128, T])
    gkl_in = dI("gkl", [128, 8])
    zw_in = dI("zw", [128, 2])
    killt_in = dI("killt", [128, NKC])

    y_out = nc.dram_tensor("y", [T, D], F32, kind="ExternalOutput").ap()
    taps = {}
    for tname in tap_x:
        taps[tname] = nc.dram_tensor("tap_" + tname, [T, D], F32,
                                     kind="ExternalOutput").ap()

    kin = [nc.dram_tensor("kin0", [128, 4 * T], BF16),
           nc.dram_tensor("kin1", [T, 16 * HD], BF16)]
    vin = [nc.dram_tensor("vin0", [T, 4 * HD], BF16),
           nc.dram_tensor("vin1", [T, 16 * HD], BF16)]
    kg = [nc.dram_tensor("kg0", [NCORES * 128, 4 * T], BF16),
          nc.dram_tensor("kg1", [SEQ, 16 * HD], BF16)]
    vg = [nc.dram_tensor("vg0", [SEQ, 4 * HD], BF16),
          nc.dram_tensor("vg1", [SEQ, 16 * HD], BF16)]
    svl_d = nc.dram_tensor("svl_d", [128, 16], F32)
    svg_d = nc.dram_tensor("svg_d", [128, 16], F32)
    kml = [nc.dram_tensor("kml0", [1, 4], F32),
           nc.dram_tensor("kml1", [1, 16], F32)]
    kmg = [nc.dram_tensor("kmg0", [1, 4], F32),
           nc.dram_tensor("kmg1", [1, 16], F32)]

    ctx = ExitStack()
    with ctx:
        E = ctx.enter_context

        X = E(nc.sbuf_tensor("X", [128, NT * D], F32))
        xnT = E(nc.sbuf_tensor("xnT", [128, ND * T], BF16))
        ident = E(nc.sbuf_tensor("ident_s", [128, 128], F32))
        ones_col_bf = E(nc.sbuf_tensor("ones_col_bf", [128, 1], BF16))
        ones_row_f = E(nc.sbuf_tensor("ones_row_f", [1, 128], F32))
        eps_col = E(nc.sbuf_tensor("eps_col", [128, 1], F32))
        negc = [E(nc.sbuf_tensor("negc0", [128, 16], F32)),
                E(nc.sbuf_tensor("negc1", [128, 16], F32))]
        nrow = E(nc.sbuf_tensor("nrow", [1, 1024], F32))
        scr = [E(nc.sbuf_tensor(f"scr{i}", [128, D], F32)) for i in range(3)]
        psA = E(nc.psum_tensor("psA", [128, 1024], F32))
        psB = E(nc.psum_tensor("psB", [128, 1024], F32))
        psC = E(nc.psum_tensor("psC", [128, 1024], F32))
        psD = E(nc.psum_tensor("psD", [128, 1024], F32))
        lay = {}
        _anon = [0]

        import os
        import concourse.bass_primitives as _bp

        def memdbg(tag):
            if os.environ.get("KMEMDBG"):
                print("MEM", tag, nc.bytes_allocated(_bp.MemorySpace.SBUF))

        def sb(shape, dtype):
            _anon[0] += 1
            return nc.sbuf_tensor(f"tt{_anon[0]}", shape, dtype)

        s_ld = Ctr(ctx, nc, "s_ld")
        s_st = Ctr(ctx, nc, "s_st")
        s_ldp = Ctr(ctx, nc, "s_ldp")
        s_cc = [Ctr(ctx, nc, f"s_cc{i}") for i in range(6)]
        cA = Ctr(ctx, nc, "cA")
        cB = Ctr(ctx, nc, "cB")
        cC = Ctr(ctx, nc, "cC")
        cD = Ctr(ctx, nc, "cD")
        cE = Ctr(ctx, nc, "cE")
        cF = Ctr(ctx, nc, "cF")
        cG = Ctr(ctx, nc, "cG")
        cH = Ctr(ctx, nc, "cH")
        cI = Ctr(ctx, nc, "cI")
        cJ = Ctr(ctx, nc, "cJ")
        s_kv = Ctr(ctx, nc, "s_kv")
        s_ss = Ctr(ctx, nc, "s_ss")
        s_se = Ctr(ctx, nc, "s_se")
        s_sm = Ctr(ctx, nc, "s_sm")
        s_sa = Ctr(ctx, nc, "s_sa")
        s_hd1 = Ctr(ctx, nc, "s_hd1")
        s_hd2 = Ctr(ctx, nc, "s_hd2")
        s_hd3 = Ctr(ctx, nc, "s_hd3")
        s_hpe = Ctr(ctx, nc, "s_hpe")

        with nc.Block("init") as blk:
            @blk.sync
            def _(sync):
                s_ld.inc(sync.dma_start(
                    out=X[:, :].rearrange("p (t d) -> p t d", d=D),
                    in_=x_in.rearrange("(t p) d -> p t d", p=128)), 16)
                s_ld.inc(sync.dma_start(out=ident[:, :], in_=ident_in), 16)

            @blk.vector
            def _(vector):
                vector.memset(ones_col_bf[:, :], 1.0)
                vector.memset(eps_col[:, :], 1e-5)
                vector.memset(ones_row_f[:, :], 1.0)

        def phase_ln(tag):
            with ExitStack() as pctx, nc.Block("ln_" + tag) as blk:
                P = pctx.enter_context
                nst = P(sb([128, 6 * NT], F32))
                sqd = P(sb([128, D], BF16))

                def sl(j, k):
                    return nst[:, 6 * j + k:6 * j + k + 1]

                Ba = cA.v
                Ba2 = cB.v
                Bd = cC.v
                Bd2 = cD.v
                Bp = cE.v
                Bev = cF.v

                @blk.scalar
                def _(scalar):
                    scalar.wait_ge(s_ld.sem, 32)
                    for j in range(NT):
                        cA.inc(scalar.activation(
                            sqd[:, :], X[:, j * D:(j + 1) * D], AF.Square,
                            accum_out=sl(j, 1)))
                    for j in range(NT):
                        scalar.wait_ge(cC.sem, Bd + j + 1)
                        cB.inc(scalar.activation(
                            sl(j, 4), sl(j, 3), AF.Sqrt, bias=eps_col[:, :]))

                @blk.vector
                def _(vector):
                    vector.wait_ge(s_ld.sem, 32)
                    for j in range(NT):
                        xj = X[:, j * D:(j + 1) * D]
                        h = cH.inc(vector.reduce_sum(sl(j, 0), xj, axis=AX,
                                                     negate=True))
                        vector.wait_ge(cH.sem, h)
                        h = cH.inc(vector.tensor_scalar(
                            sl(j, 0), sl(j, 0), 1.0 / D, None, op0=ALU.mult))
                        vector.wait_ge(cH.sem, h)
                        h = cH.inc(vector.tensor_tensor(
                            sl(j, 2), sl(j, 0), sl(j, 0), ALU.mult))
                        vector.wait_ge(cH.sem, h)
                        vector.wait_ge(cA.sem, Ba + j + 1)
                        cC.inc(vector.scalar_tensor_tensor(
                            sl(j, 3), sl(j, 1), 1.0 / D, sl(j, 2),
                            op0=ALU.mult, op1=ALU.subtract))
                    xv = xnT[:, :].rearrange("p (d t) -> p d t", t=T)

                    def evac(j):
                        vector.wait_ge(cE.sem, Bp + 2 * j + 1)
                        cF.inc(vector.tensor_copy(
                            xv[:, 0:8, j * 128:(j + 1) * 128],
                            psC[:, :].rearrange("p (a b) -> p a b", b=128)))
                        vector.wait_ge(cE.sem, Bp + 2 * j + 2)
                        cF.inc(vector.tensor_copy(
                            xv[:, 8:16, j * 128:(j + 1) * 128],
                            psD[:, :].rearrange("p (a b) -> p a b", b=128)))

                    for j in range(NT):
                        xj = X[:, j * D:(j + 1) * D]
                        vector.wait_ge(cB.sem, Ba2 + j + 1)
                        h = cH.inc(vector.reciprocal(sl(j, 5), sl(j, 4)))
                        vector.wait_ge(cH.sem, h)
                        if j >= 2:
                            vector.wait_ge(cE.sem, Bp + 2 * (j - 2) + 2)
                        cD.inc(vector.tensor_scalar(
                            scr[j % 2][:, :], xj, sl(j, 0), sl(j, 5),
                            op0=ALU.add, op1=ALU.mult))
                        if j >= 1:
                            evac(j - 1)
                    evac(NT - 1)

                @blk.tensor
                def _(tensor):
                    for j in range(NT):
                        tensor.wait_ge(cD.sem, Bd2 + j + 1)
                        if j >= 1:
                            tensor.wait_ge(cF.sem, Bev + 2 * j)
                        for d in range(8):
                            mm = tensor.transpose(
                                psC[:, d * 128:(d + 1) * 128],
                                scr[j % 2][:, d * 128:(d + 1) * 128],
                                ident[:, :])
                        cE.inc(mm)
                        for d in range(8, 16):
                            mm = tensor.transpose(
                                psD[:, (d - 8) * 128:(d - 7) * 128],
                                scr[j % 2][:, d * 128:(d + 1) * 128],
                                ident[:, :])
                        cE.inc(mm)

        def phase_proj(l, w, NP, kind):
            rope = (l == 0)
            qz = (l == 1 and kind == "q")
            memdbg(f"proj_{kind}_l{l}")
            with ExitStack() as pctx, nc.Block(f"proj_{kind}_l{l}") as blk:
                P = pctx.enter_context
                wbf = [P(sb([128, D], BF16)) for _ in range(3)]
                kts = [P(sb([128, T], BF16)) for _ in range(2)]
                sqb = P(sb([128, T], BF16))
                qf32 = [scr[0][:, 0:1024], scr[1][:, 0:1024]]
                ps0 = [psA, psC]
                ps1 = [psB, psD]
                consts = []
                if kind == "k":
                    if rope:
                        consts = [(cosf_in, lay["cosf"]), (sinf_in, lay["sinf"]),
                                  (shift_in, lay["shiftm"])]
                    if l == 1:
                        consts = [(qzb_in, lay["qzb"]), (idxk_in, lay["idxk"]),
                                  (qloc_in, lay["qloc"]), (gkl_in, lay["gkl"]),
                                  (zw_in, lay["zw"]),
                                  (killt_in, lay["killt"])]
                if rope:
                    cosf, sinf, shiftm = (lay["cosf"], lay["sinf"],
                                          lay["shiftm"])
                if qz:
                    qzb = lay["qzb"]
                Bl = s_ld.v
                Bldp = s_ldp.v
                Bp1 = cB.v
                Bqf = cC.v
                Broll = cD.v
                Bdst = cE.v
                Bsq = cF.v
                Bnm = cG.v
                Bnr = cI.v
                Bkd = s_st.v
                NCST = len(consts)
                if kind == "k":
                    lay["c_ld"] = s_ld.v + 16 * NCST

                @blk.sync
                def _(sync):
                    for csrc, cdst in consts:
                        s_ld.inc(sync.dma_start(out=cdst[:, :], in_=csrc), 16)
                    if kind == "k":
                        for i in range(NP):
                            sync.wait_ge(cE.sem, Bdst + i + 1)
                            if l == 0:
                                s_st.inc(sync.dma_start(
                                    out=kin[l].ap()[:, i * T:(i + 1) * T],
                                    in_=kts[i % 2][:, :]), 16)
                            else:
                                s_st.inc(sync.dma_start(
                                    out=kin[l].ap().rearrange(
                                        "(j p) (h f) -> p h j f", p=128,
                                        f=128)[:, i],
                                    in_=kts[i % 2][:, :].rearrange(
                                        "p (j f) -> p j f", f=128)), 16)

                @blk.gpsimd
                def _(gpsimd):
                    for i in range(NP):
                        if i >= 3:
                            gpsimd.wait_ge(cB.sem, Bp1 + i - 2)
                        s_ldp.inc(gpsimd.dma_start(
                            out=wbf[i % 3][:, :],
                            in_=w.rearrange("(fc p) k -> fc p k", p=128)[i]), 16)
                    if kind == "q":
                        gpsimd.wait_ge(s_cc[3 * l + 2].sem, 1)
                        s_ldp.inc(gpsimd.dma_start(
                            out=nrow[0:1, 48:48 + (16 if l else 4)],
                            in_=kmg[l].ap()), 16)
                        if l == 1:
                            gpsimd.wait_ge(s_cc[5].sem, 1)
                            s_ldp.inc(gpsimd.dma_start(
                                out=lay["svg"][:, :], in_=svg_d.ap()), 16)

                @blk.tensor
                def _(tensor):
                    if rope or qz:
                        tensor.wait_ge(s_ld.sem, lay.get("c_ld", 0))
                    for i in range(NP):
                        p0, p1 = ps0[i % 2], ps1[i % 2]
                        tensor.wait_ge(s_ldp.sem, Bldp + 16 * (i + 1))
                        if i >= 2:
                            if rope:
                                tensor.wait_ge(cC.sem, Bqf + i - 1)
                            else:
                                tensor.wait_ge(cE.sem, Bdst + i - 1)
                        for d in range(ND):
                            for hf in range(2):
                                mm = tensor.matmul(
                                    p0[:, hf * 512:(hf + 1) * 512],
                                    wbf[i % 3][:, d * 128:(d + 1) * 128],
                                    xnT[:, d * T + hf * 512:d * T + (hf + 1) * 512],
                                    start=(d == 0), stop=(d == ND - 1),
                                    skip_group_check=True)
                        cB.inc(mm)
                        if rope:
                            tensor.wait_ge(cC.sem, Bqf + i + 1)
                            if i >= 2:
                                tensor.wait_ge(cI.sem, Bnr + i - 1)
                            for hf in range(2):
                                mm = tensor.matmul(
                                    p1[:, hf * 512:(hf + 1) * 512],
                                    shiftm[:, :],
                                    qf32[i % 2][:, hf * 512:(hf + 1) * 512],
                                    start=True, stop=True,
                                    skip_group_check=True)
                            cD.inc(mm)
                        tensor.wait_ge(cF.sem, Bsq + i + 1)
                        if not rope and i >= 2:
                            tensor.wait_ge(cI.sem, Bnr + i - 1)
                        for hf in range(2):
                            mm = tensor.matmul(
                                p1[0:1, hf * 512:(hf + 1) * 512],
                                ones_col_bf[:, :],
                                sqb[:, hf * 512:(hf + 1) * 512],
                                start=True, stop=True, skip_group_check=True)
                        cG.inc(mm)
                    if kind == "q":
                        tensor.wait_ge(cC.sem,
                                       Bqf + (NP if rope else 0) + 1)
                        cB.inc(tensor.matmul(psA[:, 0:16], ones_row_f[:, :],
                                             nrow[0:1, 96:112], start=True,
                                             stop=True, skip_group_check=True))

                @blk.vector
                def _(vector):
                    if rope or qz:
                        vector.wait_ge(s_ld.sem, lay.get("c_ld", 0))
                    for i in range(NP):
                        p0, p1 = ps0[i % 2], ps1[i % 2]
                        dest = (QT[:, i * T:(i + 1) * T] if kind == "q"
                                else kts[i % 2][:, :])
                        vector.wait_ge(cB.sem, Bp1 + i + 1)
                        if kind == "k" and i >= 2:
                            vector.wait_ge(s_st.sem, Bkd + 16 * (i - 1))
                        if i >= 1:
                            vector.wait_ge(cG.sem, Bnm + i)
                        if rope:
                            h = cC.inc(vector.tensor_copy(qf32[i % 2][:, :],
                                                          p0[:, :]))
                            vector.wait_ge(cD.sem, Broll + i + 1)
                            vector.wait_ge(cC.sem, h)
                            h = cH.inc(vector.tensor_tensor(
                                scr[2][:, 0:1024], qf32[i % 2][:, :],
                                cosf[:, :], ALU.mult))
                            vector.wait_ge(cH.sem, h)
                            h = cH.inc(vector.tensor_tensor(
                                scr[2][:, 1024:2048], p1[:, :], sinf[:, :],
                                ALU.mult))
                            vector.wait_ge(cH.sem, h)
                            cE.inc(vector.tensor_tensor(
                                dest, scr[2][:, 0:1024], scr[2][:, 1024:2048],
                                ALU.add))
                        else:
                            if qz:
                                cE.inc(vector.tensor_tensor(
                                    dest, p0[:, :], qzb[:, :], ALU.mult))
                            else:
                                cE.inc(vector.tensor_copy(dest, p0[:, :]))
                        vector.wait_ge(cE.sem, Bdst + i + 1)
                        cF.inc(vector.tensor_tensor(sqb[:, :], dest, dest,
                                                    ALU.mult))
                        vector.wait_ge(cG.sem, Bnm + i + 1)
                        slot = (nrow[0:1, i:i + 1] if kind == "q"
                                else nrow[0:1, 32 + i:32 + i + 1])
                        cI.inc(vector.reduce_max(slot, p1[0:1, :], axis=AX))
                    if kind == "q":
                        HKV = 16 if l == 1 else 4
                        vector.wait_ge(s_ldp.sem, s_ldp.v)
                        vector.wait_ge(cI.sem, Bnr + NP)
                        if HKV == 16:
                            h = cH.inc(vector.tensor_tensor(
                                nrow[0:1, 64:80], nrow[0:1, 0:16],
                                nrow[0:1, 48:64], ALU.mult))
                        else:
                            h = cH.inc(vector.tensor_tensor(
                                nrow[0:1, 64:80].rearrange(
                                    "p (a b) -> p a b", b=4),
                                nrow[0:1, 0:16].rearrange(
                                    "p (a b) -> p a b", b=4),
                                nrow[0:1, 48:52].rearrange(
                                    "p (a b) -> p a b", b=1)
                                .broadcast_to([1, 4, 4]), ALU.mult))
                        vector.wait_ge(cH.sem, h)
                        vector.wait_ge(cA.sem, cA.v + 1)
                        cC.inc(vector.tensor_scalar(
                            nrow[0:1, 96:112], nrow[0:1, 80:96], -1.0, None,
                            op0=ALU.mult))
                        vector.wait_ge(cB.sem, cB.v)
                        cD.inc(vector.tensor_copy(negc[l][:, :], psA[:, 0:16]))

                @blk.scalar
                def _(scalar):
                    if kind == "q":
                        scalar.wait_ge(cH.sem, cH.v)
                        cA.inc(scalar.activation(nrow[0:1, 80:96],
                                                 nrow[0:1, 64:80], AF.Sqrt,
                                                 scale=1.0 / 128.0))

        def phase_v_cached(l, wv, HKV, knrow_hkv):
            FW = HKV * HD
            memdbg(f"v_l{l}")
            with ExitStack() as pctx, nc.Block(f"v_l{l}") as blk:
                P = pctx.enter_context
                wvbf = P(sb([128, ND * FW], BF16))
                vstg = [P(sb([128, FW], BF16)) for _ in range(2)]
                Bld = s_ldp.v
                Bmm = cB.v
                Bev = cC.v
                Bst = s_st.v

                @blk.gpsimd
                def _(gpsimd):
                    for d in range(ND):
                        s_ldp.inc(gpsimd.dma_start(
                            out=wvbf[:, d * FW:(d + 1) * FW],
                            in_=wv[d * 128:(d + 1) * 128, :]), 16)
                    gpsimd.wait_ge(s_st.sem, Bst + 16 * NT)
                    s_cc[3 * l + 0].inc(gpsimd.collective_compute(
                        "AllGather", ALU.bypass,
                        replica_groups=[list(range(NCORES))],
                        ins=[kin[l].ap().opt()], outs=[kg[l].ap().opt()]))
                    s_cc[3 * l + 1].inc(gpsimd.collective_compute(
                        "AllGather", ALU.bypass,
                        replica_groups=[list(range(NCORES))],
                        ins=[vin[l].ap().opt()], outs=[vg[l].ap().opt()]))
                    s_ldp.inc(gpsimd.dma_start(
                        out=kml[l].ap(), in_=nrow[0:1, 32:32 + knrow_hkv]), 16)
                    gpsimd.wait_ge(s_ldp.sem, s_ldp.v)
                    s_cc[3 * l + 2].inc(gpsimd.collective_compute(
                        "AllReduce", ALU.max,
                        replica_groups=[list(range(NCORES))],
                        ins=[kml[l].ap().opt()], outs=[kmg[l].ap().opt()]))

                @blk.tensor
                def _(tensor):
                    tensor.wait_ge(s_ldp.sem, Bld + 16 * ND)
                    for t in range(NT):
                        p0 = [psA, psB][t % 2]
                        if t >= 2:
                            tensor.wait_ge(cC.sem, Bev + t - 1)
                        for d in range(ND):
                            mm = tensor.matmul(
                                p0[:, 0:FW],
                                xnT[:, d * T + t * 128:d * T + (t + 1) * 128],
                                wvbf[:, d * FW:(d + 1) * FW],
                                start=(d == 0), stop=(d == ND - 1),
                                skip_group_check=True)
                        cB.inc(mm)

                @blk.vector
                def _(vector):
                    for t in range(NT):
                        p0 = [psA, psB][t % 2]
                        vector.wait_ge(cB.sem, Bmm + t + 1)
                        if t >= 2:
                            vector.wait_ge(s_st.sem, Bst + 16 * (t - 1))
                        cC.inc(vector.tensor_copy(vstg[t % 2][:, :],
                                                  p0[:, 0:FW]))

                @blk.sync
                def _(sync):
                    for t in range(NT):
                        sync.wait_ge(cC.sem, Bev + t + 1)
                        s_st.inc(sync.dma_start(
                            out=vin[l].ap()[t * 128:(t + 1) * 128, :],
                            in_=vstg[t % 2][:, :]), 16)

        def phase_v_fmajor(l, wv, knrow_hkv):
            memdbg(f"v_l{l}")
            with ExitStack() as pctx, nc.Block(f"v_l{l}") as blk:
                P = pctx.enter_context
                wbf = [P(sb([128, D], BF16)) for _ in range(3)]
                vstg = [P(sb([128, NT * 128], BF16)) for _ in range(2)]
                svl = P(sb([128, 16], F32))
                Bsv = cI.v
                Bld = s_ldp.v
                Bp1 = cB.v
                Bf = cC.v
                Btr = cD.v
                Bbe = cE.v
                Bst = s_st.v

                @blk.gpsimd
                def _(gpsimd):
                    for fc in range(16):
                        if fc >= 3:
                            gpsimd.wait_ge(cB.sem, Bp1 + fc - 2)
                        s_ldp.inc(gpsimd.dma_start(
                            out=wbf[fc % 3][:, :],
                            in_=wv.rearrange("(fc p) k -> fc p k", p=128)[fc]),
                            16)
                    gpsimd.wait_ge(s_st.sem, Bst + 16 * 16)
                    s_cc[3 * l + 0].inc(gpsimd.collective_compute(
                        "AllGather", ALU.bypass,
                        replica_groups=[list(range(NCORES))],
                        ins=[kin[l].ap().opt()], outs=[kg[l].ap().opt()]))
                    s_cc[3 * l + 1].inc(gpsimd.collective_compute(
                        "AllGather", ALU.bypass,
                        replica_groups=[list(range(NCORES))],
                        ins=[vin[l].ap().opt()], outs=[vg[l].ap().opt()]))
                    s_ldp.inc(gpsimd.dma_start(
                        out=kml[l].ap(), in_=nrow[0:1, 32:32 + knrow_hkv]), 16)
                    gpsimd.wait_ge(cI.sem, Bsv + 16)
                    s_ldp.inc(gpsimd.dma_start(out=svl_d.ap(),
                                               in_=svl[:, :]), 16)
                    gpsimd.wait_ge(s_ldp.sem, s_ldp.v)
                    s_cc[3 * l + 2].inc(gpsimd.collective_compute(
                        "AllReduce", ALU.max,
                        replica_groups=[list(range(NCORES))],
                        ins=[kml[l].ap().opt()], outs=[kmg[l].ap().opt()]))
                    s_cc[5].inc(gpsimd.collective_compute(
                        "AllReduce", ALU.add,
                        replica_groups=[list(range(NCORES))],
                        ins=[svl_d.ap().opt()], outs=[svg_d.ap().opt()]))

                @blk.tensor
                def _(tensor):
                    for fc in range(16):
                        p0 = [psA, psC][fc % 2]
                        p1 = [psB, psD][fc % 2]
                        tensor.wait_ge(s_ldp.sem, Bld + 16 * (fc + 1))
                        if fc >= 2:
                            tensor.wait_ge(cC.sem, Bf + fc - 1)
                        for d in range(ND):
                            for hf in range(2):
                                mm = tensor.matmul(
                                    p0[:, hf * 512:(hf + 1) * 512],
                                    wbf[fc % 3][:, d * 128:(d + 1) * 128],
                                    xnT[:, d * T + hf * 512:d * T + (hf + 1) * 512],
                                    start=(d == 0), stop=(d == ND - 1),
                                    skip_group_check=True)
                        cB.inc(mm)
                        tensor.wait_ge(cC.sem, Bf + fc + 1)
                        if fc >= 2:
                            tensor.wait_ge(cE.sem, Bbe + fc - 1)
                        for t in range(NT):
                            mm = tensor.transpose(
                                p1[:, t * 128:(t + 1) * 128],
                                scr[1 + fc % 2][:, t * 128:(t + 1) * 128],
                                ident[:, :])
                        cD.inc(mm)

                @blk.vector
                def _(vector):
                    for fc in range(16):
                        p0 = [psA, psC][fc % 2]
                        p1 = [psB, psD][fc % 2]
                        vector.wait_ge(cB.sem, Bp1 + fc + 1)
                        cI.inc(vector.reduce_sum(svl[:, fc:fc + 1], p0[:, :],
                                                 axis=AX))
                        if fc >= 2:
                            vector.wait_ge(cD.sem, Btr + fc - 1)
                        h = cC.inc(vector.tensor_copy(scr[1 + fc % 2][:, 0:1024],
                                                      p0[:, :]))
                        vector.wait_ge(cC.sem, h)
                        vector.wait_ge(cD.sem, Btr + fc + 1)
                        if fc >= 2:
                            vector.wait_ge(s_st.sem, Bst + 16 * (fc - 1))
                        cE.inc(vector.tensor_copy(vstg[fc % 2][:, :], p1[:, :]))

                @blk.sync
                def _(sync):
                    for fc in range(16):
                        sync.wait_ge(cE.sem, Bbe + fc + 1)
                        s_st.inc(sync.dma_start(
                            out=vin[l].ap().rearrange("(t p) f -> p t f", p=128)
                            [:, :, fc * 128:(fc + 1) * 128],
                            in_=vstg[fc % 2][:, :].rearrange(
                                "p (t f) -> p t f", f=128)), 16)

        def phase_flash(l, H, HKV, mask):
            C = NKC
            memdbg(f"flash_l{l}")
            with ExitStack() as pctx, nc.Block(f"flash_l{l}") as blk:
                P = pctx.enter_context
                NKV = 6
                NPB = 5 if mask else 6
                kbuf = [P(sb([128, 128], BF16)) for _ in range(NKV)]
                vbuf = [P(sb([128, 129], BF16)) for _ in range(NKV)]
                pbuf = [P(sb([128, T], BF16)) for _ in range(NPB)]
                rec8 = P(sb([128, 8], F32))
                dmy = nrow[0:1, 256:264]
                bm = scr[1][:, 0:1024]
                Bkv = s_kv.v
                Bss = s_ss.v
                Bse = s_se.v
                Bsm = s_sm.v
                Bsa = s_sa.v
                B1 = s_hd1.v
                B3 = s_hd3.v
                Bpe = s_hpe.v
                BJ = cJ.v
                kgl = kg[l].ap()
                vgl = vg[l].ap()
                psS = [psA, psB]

                def kv_idx(qh, c):
                    kvh = qh // (H // HKV)
                    return (kgl[(c // 8) * 128:(c // 8 + 1) * 128,
                                kvh * T + (c % 8) * 128:
                                kvh * T + (c % 8 + 1) * 128],
                            vgl[c * 128:(c + 1) * 128,
                                kvh * 128:(kvh + 1) * 128])

                def psO(qc):
                    p = psC if qc < 4 else psD
                    return p[:, (qc % 4) * 256:(qc % 4) * 256 + 129]

                if not mask:
                    @blk.sync
                    def _(sync):
                        sync.wait_ge(s_cc[3 * l + 0].sem, 1)
                        sync.wait_ge(s_cc[3 * l + 1].sem, 1)
                        for g in range(H * C):
                            qh, c = divmod(g, C)
                            if g >= NKV:
                                sync.wait_ge(s_sa.sem, Bsa + g - NKV + 1)
                            ksrc, vsrc = kv_idx(qh, c)
                            s_kv.inc(sync.dma_start(out=kbuf[g % NKV][:, :],
                                                    in_=ksrc), 16)
                            s_kv.inc(sync.dma_start(out=vbuf[g % NKV][:, 0:128],
                                                    in_=vsrc), 16)
                else:
                    @blk.gpsimd
                    def _(gpsimd):
                        gpsimd.wait_ge(s_cc[3 * l + 0].sem, 1)
                        gpsimd.wait_ge(s_cc[3 * l + 1].sem, 1)
                        for g in range(H * C):
                            qh, i = divmod(g, C)
                            if g >= NKV:
                                gpsimd.wait_ge(s_sa.sem, Bsa + g - NKV + 1)
                            s_kv.inc(gpsimd.indirect_dma_start(
                                kbuf[g % NKV][:, :], None,
                                kgl[:, 0:128],
                                bass.IndirectOffsetOnAxis(
                                    lay["idxk"][:, i:i + 1], 0),
                                element_offset=qh * 128), 16)
                            s_kv.inc(gpsimd.indirect_dma_start(
                                vbuf[g % NKV][:, 0:128], None,
                                vgl[:, 0:128],
                                bass.IndirectOffsetOnAxis(
                                    lay["idxk"][:, i:i + 1], 0),
                                element_offset=qh * 128), 16)

                @blk.vector
                def _(vector):
                    if mask:
                        vector.wait_ge(s_ld.sem, lay.get("c_ld", 0))
                        inst = vector.tensor_tensor(
                            bm.rearrange("p (h i) -> p h i", i=C),
                            negc[l][:, :].rearrange("p (h o) -> p h o", o=1)
                            .broadcast_to([128, 16, C]),
                            lay["killt"][:, :].rearrange("p (o i) -> p o i",
                                                         o=1)
                            .broadcast_to([128, 16, C]), ALU.add)
                        cJ.inc(inst)
                    for b in range(NKV):
                        inst = vector.memset(vbuf[b][:, 128:129], 1.0)
                    cJ.inc(inst)

                    def dve_headend(qh):
                        vector.wait_ge(s_sa.sem, Bsa + (qh + 1) * C)
                        h = cH.inc(vector.reciprocal(
                            rec8[:, 0:4].rearrange("p (a b) -> p a b", b=1),
                            psC[:, :].rearrange("p (a b) -> p a b", b=256)
                            [:, :, 128:129]))
                        h = cH.inc(vector.reciprocal(
                            rec8[:, 4:8].rearrange("p (a b) -> p a b", b=1),
                            psD[:, :].rearrange("p (a b) -> p a b", b=256)
                            [:, :, 128:129]))
                        vector.wait_ge(cH.sem, h)
                        for qc in range(8):
                            p = psC if qc < 4 else psD
                            inst2 = vector.tensor_scalar(
                                scr[0][:, qc * 128:(qc + 1) * 128],
                                p[:, (qc % 4) * 256:(qc % 4) * 256 + 128],
                                rec8[:, qc:qc + 1], None, op0=ALU.mult)
                        s_hd1.inc(inst2)
                        vector.wait_ge(s_hpe.sem, Bpe + qh + 1)
                        h = s_hd3.inc(vector.tensor_copy(
                            OTn[:, qh * T:(qh + 1) * T], psC[:, :]))
                        if mask:
                            oc = OTn[:, qh * T + 1023:qh * T + 1024]
                            vector.wait_ge(s_hd3.sem, h)
                            h2 = cH.inc(vector.tensor_scalar(
                                rec8[:, 0:1], oc, lay["zw"][:, 0:1], None,
                                op0=ALU.mult))
                            vector.wait_ge(cH.sem, h2)
                            h2 = cH.inc(vector.scalar_tensor_tensor(
                                oc, lay["svg"][:, qh:qh + 1],
                                lay["zw"][:, 1:2], rec8[:, 0:1],
                                op0=ALU.mult, op1=ALU.add))
                            vector.wait_ge(cH.sem, h2)

                    pend = None
                    for g in range(H * C):
                        qh, i = divmod(g, C)
                        if mask and i < 8:
                            vector.wait_ge(s_se.sem, Bse + 2 * g + 3)
                            s_sm.inc(vector.scalar_tensor_tensor(
                                pbuf[g % NPB][:, i * 128:T],
                                lay["qloc"][:, i * 128:T],
                                lay["gkl"][:, i:i + 1],
                                pbuf[g % NPB][:, i * 128:T],
                                op0=ALU.is_lt, op1=ALU.mult))
                            if i == 7:
                                s_sm.inc(vector.scalar_tensor_tensor(
                                    dmy[0:1, 1:2], lay["qloc"][0:1, 0:1],
                                    lay["gkl"][0:1, 0:1], dmy[0:1, 1:2],
                                    op0=ALU.is_lt, op1=ALU.mult))
                        if pend is not None:
                            dve_headend(pend)
                            pend = None
                        if i == C - 1:
                            pend = qh
                    dve_headend(pend)

                @blk.tensor
                def _(tensor):
                    def emit_S(g):
                        qh, c = divmod(g, C)
                        tensor.wait_ge(s_kv.sem, Bkv + 32 * (g + 1))
                        if g == 0:
                            tensor.wait_ge(cJ.sem, cJ.v)
                        for hf in range(2):
                            if g >= 2:
                                tensor.wait_ge(s_se.sem,
                                               Bse + 2 * (g - 2) + hf + 1)
                            mm = tensor.matmul(
                                psS[g % 2][:, hf * 512:(hf + 1) * 512],
                                kbuf[g % NKV][:, :],
                                QT[:, qh * T + hf * 512:qh * T + (hf + 1) * 512],
                                start=True, stop=True, skip_group_check=True)
                            s_ss.inc(mm)

                    def emit_AV(g):
                        qh, i = divmod(g, C)
                        if mask and i < 8:
                            tensor.wait_ge(s_sm.sem, Bsm + qh * 9 + i + 2)
                        else:
                            tensor.wait_ge(s_se.sem, Bse + 2 * g + 3)
                        if i == 0 and qh >= 1:
                            tensor.wait_ge(s_hd3.sem, B3 + qh)
                        for qc in range(8):
                            mm = tensor.matmul(
                                psO(qc),
                                pbuf[g % NPB][:, qc * 128:(qc + 1) * 128],
                                vbuf[g % NKV][:, :],
                                start=(i == 0 and qc % 2 == 0),
                                stop=(i == C - 1),
                                skip_group_check=True)
                        s_sa.inc(mm)

                    def emit_headend(p):
                        tensor.wait_ge(s_hd1.sem, B1 + p + 1)
                        for qc in range(8):
                            mm = tensor.transpose(
                                psC[:, qc * 128:(qc + 1) * 128],
                                scr[0][:, qc * 128:(qc + 1) * 128],
                                ident[:, :])
                        s_hpe.inc(mm)

                    pend_he = None
                    for g in range(H * C):
                        emit_S(g)
                        if pend_he is not None:
                            emit_headend(pend_he)
                            pend_he = None
                        if g >= 2:
                            emit_AV(g - 2)
                            if (g - 2) % C == C - 1:
                                pend_he = (g - 2) // C
                    emit_AV(H * C - 2)
                    emit_AV(H * C - 1)
                    emit_headend(H - 1)

                @blk.scalar
                def _(scalar):
                    for g in range(H * C):
                        qh, i = divmod(g, C)
                        bias = (bm[:, qh * C + i:qh * C + i + 1] if mask
                                else negc[l][:, qh:qh + 1])
                        for hf in range(2):
                            scalar.wait_ge(s_ss.sem, Bss + 2 * g + hf + 1)
                            if hf == 0 and g >= NPB:
                                scalar.wait_ge(s_sa.sem, Bsa + g - NPB + 1)
                            s_se.inc(scalar.activation(
                                pbuf[g % NPB][:, hf * 512:(hf + 1) * 512],
                                psS[g % 2][:, hf * 512:(hf + 1) * 512],
                                AF.Exp, bias=bias, scale=ISQ))
                    s_se.inc(scalar.activation(
                        dmy[0:1, 0:1], psS[0][0:1, 0:1], AF.Exp,
                        bias=negc[l][0:1, 0:1], scale=ISQ))

        def phase_wo(l, wo):
            memdbg(f"wo_l{l}")
            with ExitStack() as pctx, nc.Block(f"wo_l{l}") as blk:
                P = pctx.enter_context
                wobf = QT
                Bld = s_ldp.v
                Bmm = cB.v
                Bx = cC.v

                @blk.gpsimd
                def _(gpsimd):
                    for grp in range(2):
                        if grp == 1:
                            gpsimd.wait_ge(cB.sem, Bmm + 8)
                        for h8 in range(8):
                            h = grp * 8 + h8
                            s_ldp.inc(gpsimd.dma_start(
                                out=wobf[:, h8 * D:(h8 + 1) * D],
                                in_=wo[h * 128:(h + 1) * 128, :]), 16)

                @blk.tensor
                def _(tensor):
                    for grp in range(2):
                        tensor.wait_ge(s_ldp.sem, Bld + 16 * 8 * (grp + 1))
                        for t in range(NT):
                            pl, ph = [(psA, psB), (psC, psD)][t % 2]
                            if grp * NT + t >= 2:
                                tensor.wait_ge(cC.sem, Bx + grp * NT + t - 1)
                            for h8 in range(8):
                                h = grp * 8 + h8
                                for hf in range(4):
                                    p = pl if hf < 2 else ph
                                    mm = tensor.matmul(
                                        p[:, (hf % 2) * 512:(hf % 2 + 1) * 512],
                                        OTn[:, h * T + t * 128:h * T + (t + 1) * 128],
                                        wobf[:, h8 * D + hf * 512:
                                             h8 * D + (hf + 1) * 512],
                                        start=(h8 == 0), stop=(h8 == 7),
                                        skip_group_check=True)
                            cB.inc(mm)

                @blk.vector
                def _(vector):
                    for grp in range(2):
                        for t in range(NT):
                            pl, ph = [(psA, psB), (psC, psD)][t % 2]
                            vector.wait_ge(cB.sem, Bmm + grp * NT + t + 1)
                            h = cH.inc(vector.tensor_tensor(
                                X[:, t * D:t * D + 1024],
                                X[:, t * D:t * D + 1024], pl[:, :], ALU.add))
                            vector.wait_ge(cH.sem, h)
                            cC.inc(vector.tensor_tensor(
                                X[:, t * D + 1024:(t + 1) * D],
                                X[:, t * D + 1024:(t + 1) * D], ph[:, :],
                                ALU.add))

        def phase_ffn0():
            memdbg("ffn0")
            with ExitStack() as pctx, nc.Block("ffn0") as blk:
                P = pctx.enter_context
                dsb = P(sb([128, D], F32))
                gs_sb = P(sb([128, NIC], F32))
                us_sb = P(sb([128, NIC], F32))
                xadd = scr[2][:, 1024:2048]
                h_sb = P(sb([128, NIC * 512], BF16))
                gbf = [P(sb([128, D], BF16)) for _ in range(2)]
                ubf = [P(sb([128, D], BF16)) for _ in range(2)]
                dbf = [P(sb([128, 1024], BF16))
                       for _ in range(2)]
                silu = [P(sb([128, 512], BF16))
                        for _ in range(2)]
                usb = [P(sb([128, 512], BF16))
                       for _ in range(2)]
                Bl = s_ld.v
                Bc = cA.v
                Bgu = cB.v
                Bsil = cC.v
                Buev = cD.v
                Bh = cE.v
                Bxa = cF.v
                Bdn = cG.v
                Bdc = cI.v
                Bdl = s_ldp.v
                BJ = cJ.v

                @blk.sync
                def _(sync):
                    cJ.inc(sync.dma_start(out=dsb[:, :], in_=ds_b), 16)
                    cJ.inc(sync.dma_start(out=gs_sb[:, :], in_=gs_c), 16)
                    cJ.inc(sync.dma_start(out=us_sb[:, :], in_=us_c), 16)

                @blk.gpsimd
                def _(gpsimd):
                    for hh in range(2):
                        for ic in range(NIC):
                            i = hh * NIC + ic
                            if i >= 2:
                                gpsimd.wait_ge(cB.sem, Bgu + i - 1)
                            s_ld.inc(gpsimd.dma_start(
                                out=gbf[i % 2][:, :],
                                in_=gwT.rearrange("(ic p) k -> ic p k", p=128)
                                [ic]), 16)
                            s_ld.inc(gpsimd.dma_start(
                                out=ubf[i % 2][:, :],
                                in_=uwT.rearrange("(ic p) k -> ic p k", p=128)
                                [ic]), 16)
                        for dmh in range(2):
                            for ic in range(NIC):
                                j = (hh * 2 + dmh) * NIC + ic
                                if j >= 2:
                                    gpsimd.wait_ge(cG.sem, Bdn + j - 1)
                                s_ldp.inc(gpsimd.dma_start(
                                    out=dbf[j % 2][:, :],
                                    in_=dwT[ic * 128:(ic + 1) * 128,
                                            dmh * 1024:(dmh + 1) * 1024]), 16)

                @blk.scalar
                def _(scalar):
                    for hh in range(2):
                        for ic in range(NIC):
                            i = hh * NIC + ic
                            scalar.wait_ge(cB.sem, Bgu + i + 1)
                            if i == 0:
                                scalar.wait_ge(cJ.sem, BJ + 48)
                            psGU = [psA, psB][i % 2]
                            cC.inc(scalar.activation(
                                silu[i % 2][:, :], psGU[:, 0:512], AF.Silu,
                                scale=gs_sb[:, ic:ic + 1]))

                @blk.tensor
                def _(tensor):
                    for hh in range(2):
                        for ic in range(NIC):
                            i = hh * NIC + ic
                            psGU = [psA, psB][i % 2]
                            tensor.wait_ge(s_ld.sem, Bl + 32 * (i + 1))
                            if i >= 2:
                                tensor.wait_ge(cC.sem, Bsil + i - 1)
                                tensor.wait_ge(cD.sem, Buev + i - 1)
                            if ic == 0 and hh == 1:
                                tensor.wait_ge(cF.sem, Bxa + 8)
                            for d in range(ND):
                                rhs = xnT[:, d * T + hh * 512:
                                          d * T + (hh + 1) * 512]
                                tensor.matmul(
                                    psGU[:, 0:512],
                                    gbf[i % 2][:, d * 128:(d + 1) * 128], rhs,
                                    start=(d == 0), stop=(d == ND - 1),
                                    skip_group_check=True)
                                mm = tensor.matmul(
                                    psGU[:, 512:1024],
                                    ubf[i % 2][:, d * 128:(d + 1) * 128], rhs,
                                    start=(d == 0), stop=(d == ND - 1),
                                    skip_group_check=True)
                            cB.inc(mm)
                        tensor.wait_ge(cE.sem, Bh + (hh + 1) * NIC)
                        pstch = [psA, psB, psC, psD]
                        for dmh in range(2):
                            for ic in range(NIC):
                                j = (hh * 2 + dmh) * NIC + ic
                                tensor.wait_ge(s_ldp.sem, Bdl + 16 * (j + 1))
                                if dmh == 0 and ic == 0:
                                    tensor.wait_ge(cC.sem, Bsil + (hh + 1) * NIC)
                                    tensor.wait_ge(cD.sem, Buev + (hh + 1) * NIC)
                                if ic == 0 and (hh * 2 + dmh) >= 1:
                                    tensor.wait_ge(cF.sem, Bxa + (hh * 2 + dmh) * 4)
                                for tch in range(4):
                                    for hf in range(2):
                                        mm = tensor.matmul(
                                            pstch[tch][:, hf * 512:(hf + 1) * 512],
                                            h_sb[:, ic * 512 + tch * 128:
                                                 ic * 512 + (tch + 1) * 128],
                                            dbf[j % 2][:, hf * 512:(hf + 1) * 512],
                                            start=(ic == 0), stop=(ic == NIC - 1),
                                            skip_group_check=True)
                                cG.inc(mm)

                @blk.vector
                def _(vector):
                    vector.wait_ge(cJ.sem, BJ + 48)
                    for hh in range(2):
                        for ic in range(NIC):
                            i = hh * NIC + ic
                            psGU = [psA, psB][i % 2]
                            vector.wait_ge(cB.sem, Bgu + i + 1)
                            h = cD.inc(vector.tensor_scalar(
                                usb[i % 2][:, :], psGU[:, 512:1024],
                                us_sb[:, ic:ic + 1], None, op0=ALU.mult))
                            vector.wait_ge(cD.sem, h)
                            vector.wait_ge(cC.sem, Bsil + i + 1)
                            cE.inc(vector.tensor_tensor(
                                h_sb[:, ic * 512:(ic + 1) * 512],
                                silu[i % 2][:, :], usb[i % 2][:, :], ALU.mult))
                        pstch = [psA, psB, psC, psD]
                        for dmh in range(2):
                            for tch in range(4):
                                vector.wait_ge(cG.sem,
                                               Bdn + (hh * 2 + dmh + 1) * NIC)
                                tg = hh * 4 + tch
                                xsl = X[:, tg * D + dmh * 1024:
                                        tg * D + (dmh + 1) * 1024]
                                h = cH.inc(vector.tensor_tensor(
                                    xadd, pstch[tch][:, :],
                                    dsb[:, dmh * 1024:(dmh + 1) * 1024],
                                    ALU.mult))
                                vector.wait_ge(cH.sem, h)
                                cF.inc(vector.tensor_tensor(
                                    xsl, xsl, xadd, ALU.add))

        def phase_ffn1():
            memdbg("ffn1")
            with ExitStack() as pctx, nc.Block("ffn1") as blk:
                P = pctx.enter_context
                abf = P(sb([128, ND * 512], BF16))
                bbf = P(sb([128, 4 * D], BF16))
                wrT = P(sb([128, ND * 8], BF16))
                wrst = P(sb([8, D], F32))
                brsb = P(sb([1, 8], F32))
                p8 = P(sb([128, 8 * 8], F32))
                ow = P(sb([128, 8 * 8], F32))
                sst = P(sb([128, 4 * 8], F32))
                hmf = [P(sb([128, 512], F32)) for _ in range(2)]
                hTb = [P(sb([128, 512], BF16)) for _ in range(2)]
                Bl = s_ld.v
                Bldp = s_ldp.v
                Bc = cA.v
                Bex = cI.v
                Bpe = cB.v
                Btr = cG.v
                BwT = cD.v
                Bsx = cC.v
                Bmk = cF.v
                Bhev = cE.v
                BxH = cH.v

                @blk.gpsimd
                def _(gpsimd):
                    for d in range(ND):
                        s_ldp.inc(gpsimd.dma_start(
                            out=abf[:, d * 512:(d + 1) * 512],
                            in_=a_T[d * 128:(d + 1) * 128, :]), 16)

                @blk.sync
                def _(sync):
                    for b in range(4):
                        if b >= 2:
                            sync.wait_ge(cA.sem, Bc + b - 1)
                        s_ld.inc(sync.dma_start(
                            out=scr[b % 2][:, :],
                            in_=b_sT[b * 128:(b + 1) * 128, :]), 16)
                    s_ld.inc(sync.dma_start(out=wrst[:, :], in_=wr_n), 16)
                    s_ld.inc(sync.dma_start(out=brsb[:, :], in_=br_r), 16)

                @blk.scalar
                def _(scalar):
                    for b in range(4):
                        scalar.wait_ge(s_ld.sem, Bl + 16 * (b + 1))
                        cA.inc(scalar.mul(bbf[:, b * D:(b + 1) * D],
                                          scr[b % 2][:, :], 1.0 / RK))
                    for t in range(NT):
                        scalar.wait_ge(cC.sem, Bsx + 5 * t + 1)
                        pl_ = [psA, psB, psC, psD][t % 4]
                        cI.inc(scalar.activation(
                            p8[:, t * 8:(t + 1) * 8],
                            pl_[:, (t // 4) * 512:(t // 4) * 512 + 8], AF.Exp,
                            bias=sst[:, 4 * t:4 * t + 1]))

                @blk.tensor
                def _(tensor):
                    tensor.wait_ge(s_ld.sem, Bl + 16 * 6)
                    for d in range(ND):
                        mm = tensor.transpose(
                            psC[:, d * 8:(d + 1) * 8],
                            wrst[:, d * 128:(d + 1) * 128], ident[0:8, 0:8])
                    cB.inc(mm)
                    tensor.wait_ge(cD.sem, BwT + 1)
                    psall = [psA, psB, psC, psD]

                    def lgt(t):
                        return psall[t % 4][:, (t // 4) * 512:
                                            (t // 4) * 512 + 8]
                    for t in range(NT):
                        for d in range(ND):
                            tensor.matmul(
                                lgt(t),
                                xnT[:, d * T + t * 128:d * T + (t + 1) * 128],
                                wrT[:, d * 8:(d + 1) * 8],
                                start=(d == 0), stop=False,
                                skip_group_check=True)
                        mm = tensor.matmul(
                            lgt(t), ones_row_f[:, :],
                            brsb[:, :], start=False, stop=True,
                            skip_group_check=True)
                        cB.inc(mm)
                    tensor.wait_ge(s_ldp.sem, Bldp + 16 * ND)
                    tensor.wait_ge(cA.sem, Bc + 4)
                    tensor.wait_ge(cC.sem, Bsx + 5 * NT)
                    for t in range(NT):
                        if t >= 2:
                            tensor.wait_ge(cF.sem, Bmk + t - 1)
                        for d in range(ND):
                            mm = tensor.matmul(
                                psA[:, (t % 2) * 512:(t % 2 + 1) * 512],
                                xnT[:, d * T + t * 128:d * T + (t + 1) * 128],
                                abf[:, d * 512:(d + 1) * 512],
                                start=(d == 0), stop=(d == ND - 1),
                                skip_group_check=True)
                        cB.inc(mm)
                        tensor.wait_ge(cF.sem, Bmk + t + 1)
                        if t >= 1:
                            tensor.wait_ge(cE.sem, Bhev + t)
                        for b in range(4):
                            mm = tensor.transpose(
                                psB[:, 512 + b * 128:512 + (b + 1) * 128],
                                hmf[t % 2][:, b * 128:(b + 1) * 128],
                                ident[:, :])
                        cG.inc(mm)
                        tensor.wait_ge(cE.sem, Bhev + t + 1)
                        if t >= 1:
                            tensor.wait_ge(cH.sem, BxH + 2 * t)
                        for b in range(4):
                            for hf in range(4):
                                p = psC if hf < 2 else psD
                                mm = tensor.matmul(
                                    p[:, (hf % 2) * 512:(hf % 2 + 1) * 512],
                                    hTb[t % 2][:, b * 128:(b + 1) * 128],
                                    bbf[:, b * D + hf * 512:b * D + (hf + 1) * 512],
                                    start=(b == 0), stop=(b == 3),
                                    skip_group_check=True)
                        cB.inc(mm)

                @blk.vector
                def _(vector):
                    vector.wait_ge(cB.sem, Bpe + 1)
                    cD.inc(vector.tensor_copy(wrT[:, :], psC[:, 0:ND * 8]))
                    psall = [psA, psB, psC, psD]
                    for t in range(NT):
                        lg = psall[t % 4][:, (t // 4) * 512:(t // 4) * 512 + 8]
                        vector.wait_ge(cB.sem, Bpe + 1 + t + 1)
                        h = cC.inc(vector.reduce_max(
                            sst[:, 4 * t:4 * t + 1], lg, axis=AX, negate=True))
                        h = cC.inc(vector.reduce_max(
                            sst[:, 4 * t + 1:4 * t + 2], lg, axis=AX))
                        vector.wait_ge(cC.sem, h)
                        vector.wait_ge(cI.sem, Bex + t + 1)
                        h = cC.inc(vector.reduce_sum(
                            sst[:, 4 * t + 2:4 * t + 3],
                            p8[:, t * 8:(t + 1) * 8], axis=AX))
                        vector.wait_ge(cC.sem, h)
                        h = cC.inc(vector.reciprocal(
                            sst[:, 4 * t + 3:4 * t + 4],
                            sst[:, 4 * t + 2:4 * t + 3]))
                        vector.wait_ge(cC.sem, h)
                        h = cC.inc(vector.tensor_scalar(
                            ow[:, t * 8:(t + 1) * 8], lg,
                            sst[:, 4 * t + 1:4 * t + 2],
                            sst[:, 4 * t + 3:4 * t + 4],
                            op0=ALU.is_ge, op1=ALU.mult))
                        vector.wait_ge(cC.sem, h)
                    for t in range(NT):
                        vector.wait_ge(cB.sem, Bpe + 9 + 2 * t + 1)
                        for e in range(NE):
                            inst = vector.tensor_scalar(
                                hmf[t % 2][:, e * 64:(e + 1) * 64],
                                psA[:, (t % 2) * 512 + e * 64:
                                    (t % 2) * 512 + (e + 1) * 64],
                                ow[:, t * 8 + e:t * 8 + e + 1], None,
                                op0=ALU.mult)
                        cF.inc(inst)
                        vector.wait_ge(cG.sem, Btr + t + 1)
                        if t >= 2:
                            vector.wait_ge(cB.sem, Bpe + 9 + 2 * (t - 1) + 2)
                        cE.inc(vector.tensor_copy(hTb[t % 2][:, :],
                                                  psB[:, 512:1024]))
                        vector.wait_ge(cB.sem, Bpe + 9 + 2 * t + 2)
                        cH.inc(vector.tensor_tensor(
                            X[:, t * D:t * D + 1024],
                            X[:, t * D:t * D + 1024], psC[:, :], ALU.add))
                        cH.inc(vector.tensor_tensor(
                            X[:, t * D + 1024:(t + 1) * D],
                            X[:, t * D + 1024:(t + 1) * D], psD[:, :],
                            ALU.add))

        def dump_x(dest_ap):
            with nc.Block("dump") as blk:
                @blk.sync
                def _(sync):
                    s_st.inc(sync.dma_start(
                        out=dest_ap.rearrange("(t p) d -> p t d", p=128),
                        in_=X[:, :].rearrange("p (t d) -> p t d", d=D)), 16)
                    if DEV_DUMP_KV:
                        kgo = nc.dram_tensor("kg0_tap", [NCORES * 128, 4 * T],
                                             BF16, kind="ExternalOutput").ap()
                        vgo = nc.dram_tensor("vg0_tap", [SEQ, 4 * HD], BF16,
                                             kind="ExternalOutput").ap()
                        nco = nc.dram_tensor("negc_tap", [128, 16], F32,
                                             kind="ExternalOutput").ap()
                        s_st.inc(sync.dma_start(out=kgo, in_=kg[0].ap()), 16)
                        s_st.inc(sync.dma_start(out=vgo, in_=vg[0].ap()), 16)
                        s_st.inc(sync.dma_start(out=nco, in_=negc[0][:, :]), 16)
                    sync.wait_ge(s_st.sem, s_st.v)

        def attn_layer(l):
            global QT, OTn
            with ExitStack() as actx:
                A = actx.enter_context
                lay.clear()
                if l == 1:
                    lay["idxk"] = A(nc.sbuf_tensor("idxk_s", [128, NKC],
                                                   mybir.dt.uint32))
                    lay["qloc"] = A(nc.sbuf_tensor("qloc_s", [128, T], F32))
                    lay["gkl"] = A(nc.sbuf_tensor("gkl_s", [128, 8], F32))
                    lay["zw"] = A(nc.sbuf_tensor("zw_s", [128, 2], F32))
                    lay["svg"] = A(nc.sbuf_tensor("svg_s", [128, 16], F32))
                    lay["killt"] = A(nc.sbuf_tensor("killt_s", [128, NKC],
                                                    F32))
                QT = A(nc.sbuf_tensor(f"QT{l}", [128, 16 * T], BF16))
                ictx = ExitStack()
                IA = ictx.enter_context
                if l == 0:
                    lay["cosf"] = IA(nc.sbuf_tensor("cosf_s", [128, T], F32))
                    lay["sinf"] = IA(nc.sbuf_tensor("sinf_s", [128, T], F32))
                    lay["shiftm"] = IA(nc.sbuf_tensor("shiftm_s",
                                                      [128, 128], F32))
                else:
                    lay["qzb"] = IA(nc.sbuf_tensor("qzb_s", [128, T], F32))
                if l == 0:
                    phase_proj(0, wk0, 4, "k")
                    phase_v_cached(0, wv0, 4, 4)
                    phase_proj(0, wq0, 16, "q")
                    ictx.close()
                    if DEV_DUMP_KV:
                        qtap = nc.dram_tensor("qt_tap", [128, 16 * T], BF16,
                                              kind="ExternalOutput").ap()
                        with nc.Block("qtdump") as qblk:
                            @qblk.sync
                            def _(sync):
                                s_st.inc(sync.dma_start(out=qtap,
                                                        in_=QT[:, :]), 16)
                                sync.wait_ge(s_st.sem, s_st.v)
                    OTn = A(nc.sbuf_tensor(f"OTn{l}", [128, 16 * T], BF16))
                    phase_flash(0, 16, 4, False)
                    if DEV_DUMP_KV:
                        otap = nc.dram_tensor("otn_tap", [128, 16 * T], BF16,
                                              kind="ExternalOutput").ap()
                        with nc.Block("otdump") as oblk:
                            @oblk.sync
                            def _(sync):
                                s_st.inc(sync.dma_start(out=otap,
                                                        in_=OTn[:, :]), 16)
                                sync.wait_ge(s_st.sem, s_st.v)
                    phase_wo(0, wo0)
                else:
                    phase_proj(1, wk1, 16, "k")
                    phase_v_fmajor(1, wv1, 16)
                    phase_proj(1, wq1, 16, "q")
                    ictx.close()
                    OTn = A(nc.sbuf_tensor(f"OTn{l}", [128, 16 * T], BF16))
                    phase_flash(1, 16, 16, True)
                    phase_wo(1, wo1)

        phase_ln("l0a")
        attn_layer(0)
        if "l0_attn" in taps:
            dump_x(taps["l0_attn"])
        if stop_after == "l0_attn":
            dump_x(y_out)
            return nc, din
        phase_ln("l0f")
        phase_ffn0()
        if "l0_ffn" in taps:
            dump_x(taps["l0_ffn"])
        if stop_after == "l0_ffn":
            dump_x(y_out)
            return nc, din
        phase_ln("l1a")
        attn_layer(1)
        if "l1_attn" in taps:
            dump_x(taps["l1_attn"])
        if stop_after == "l1_attn":
            dump_x(y_out)
            return nc, din
        phase_ln("l1f")
        phase_ffn1()
        dump_x(y_out)
    return nc, din



def _rope_tables(pos):
    inv = 1.0 / (10000.0 ** (np.arange(0, HD, 2, dtype=np.float32) / HD))
    fr = pos[:, None].astype(np.float32) * inv[None, :]
    emb = np.concatenate([fr, fr], axis=-1)
    return np.cos(emb).astype(np.float32), np.sin(emb).astype(np.float32)


def _shift_mat():
    m = np.zeros((128, 128), dtype=np.float32)
    for k in range(128):
        m[k, (k + 1) % 128] = 1.0
    return m


def _tile_w(w, nf):
    K, F = w.shape
    a = w.reshape(K // 128, 128, nf, F // nf)
    a = a.transpose(2, 1, 0, 3)
    return np.ascontiguousarray(a.reshape(F, K))


def _core_inputs(inputs, r):
    x = np.ascontiguousarray(
        inputs["L0_attn_x"][0, r * T:(r + 1) * T, :]).astype(np.float32, copy=False)
    pos = np.arange(r * T, (r + 1) * T)
    cos, sin = _rope_tables(pos)
    cosf = np.ascontiguousarray(cos.T)
    sinf = np.ascontiguousarray(sin.T)
    qz = np.ones((128, T), dtype=np.float32)
    if r == NCORES - 1:
        qz[:, T - 1] = 0.0
    ordr = (np.arange(NKC) + 8 * r) % NKC
    idxk = ((ordr[None, :] * 128 + np.arange(128)[:, None]) * 16
            ).astype(np.uint32)
    killt = np.zeros((128, NKC), dtype=np.float32)
    killt[:, ordr < 8 * r] = -30000.0
    qloc = np.tile(np.arange(T, dtype=np.float32)[None, :], (128, 1))
    if r == NCORES - 1:
        qloc[:, T - 1] = -1.0
    gkl = (np.arange(8, dtype=np.float32)[None, :] * 128
           + np.arange(128, dtype=np.float32)[:, None])
    zw = np.zeros((128, 2), dtype=np.float32)
    zw[:, 0] = 1.0
    if r == NCORES - 1:
        zw[:, 0] = 0.0
        zw[:, 1] = 1.0 / SEQ
    gs = inputs["L0_ffn_gate_s"].astype(np.float32)
    us = inputs["L0_ffn_up_s"].astype(np.float32)
    ds = inputs["L0_ffn_down_s"].astype(np.float32)
    A = np.asarray(inputs["L1_ffn_A"], dtype=np.float32)
    B = np.asarray(inputs["L1_ffn_B"], dtype=np.float32)
    d = dict(
        x=x,
        wq0=_tile_w(np.asarray(inputs["L0_attn_Wq"], dtype=np.float32), 16),
        wk0=_tile_w(np.asarray(inputs["L0_attn_Wk"], dtype=np.float32), 4),
        wv0=np.asarray(inputs["L0_attn_Wv"], dtype=np.float32),
        wo0=np.asarray(inputs["L0_attn_Wo"], dtype=np.float32),
        gwT=_tile_w(np.asarray(inputs["L0_ffn_gate_w"], dtype=np.float32).T, NIC),
        uwT=_tile_w(np.asarray(inputs["L0_ffn_up_w"], dtype=np.float32).T, NIC),
        dwT=np.ascontiguousarray(np.asarray(inputs["L0_ffn_down_w"],
                                            dtype=np.float32).T),
        gs_c=np.ascontiguousarray(gs.reshape(NIC, 128).T),
        us_c=np.ascontiguousarray(us.reshape(NIC, 128).T),
        ds_b=np.ascontiguousarray(np.tile(ds[None, :], (128, 1))),
        wq1=_tile_w(np.asarray(inputs["L1_attn_Wq"], dtype=np.float32), 16),
        wk1=_tile_w(np.asarray(inputs["L1_attn_Wk"], dtype=np.float32), 16),
        wv1=_tile_w(np.asarray(inputs["L1_attn_Wv"], dtype=np.float32), 16),
        wo1=np.asarray(inputs["L1_attn_Wo"], dtype=np.float32),
        a_T=np.ascontiguousarray(A.reshape(NE * RK, D).T),
        b_sT=np.ascontiguousarray(B.transpose(0, 2, 1).reshape(NE * RK, D)),
        wr_n=np.asarray(inputs["L1_ffn_Wr"], dtype=np.float32),
        br_r=np.ascontiguousarray(
            np.asarray(inputs["L1_ffn_br"], dtype=np.float32)[None, :]),
        ident=np.eye(128, dtype=np.float32),
        shiftm=_shift_mat(),
        cosf=cosf, sinf=sinf, qzb=qz,
        idxk=np.ascontiguousarray(idxk),
        killt=np.ascontiguousarray(killt),
        qloc=np.ascontiguousarray(qloc),
        gkl=np.ascontiguousarray(gkl),
        zw=np.ascontiguousarray(zw),
    )
    return d


_CACHE = {}


def kernel(**inputs):
    key = "full"
    if key not in _CACHE:
        _CACHE[key] = build_program()
    nc, _ = _CACHE[key]
    in_maps = [_core_inputs(inputs, r) for r in range(NCORES)]
    res = run_bass_kernel_spmd(nc, in_maps, core_ids=list(range(NCORES)))
    out = np.concatenate([res.results[r]["y"] for r in range(NCORES)], axis=0)
    return out.reshape(1, SEQ, D).astype(np.float32)


# version 10
# speedup vs baseline: 2.0583x; 1.0080x over previous
from contextlib import ExitStack
import math
import numpy as np
import ml_dtypes
import concourse.bass as bass
import concourse.mybir as mybir
from concourse.bass_utils import run_bass_kernel_spmd

F32 = mybir.dt.float32
BF16 = mybir.dt.bfloat16
AF = mybir.ActivationFunctionType
ALU = mybir.AluOpType
AX = mybir.AxisListType.X

NCORES = 8
T = 1024
NT = 8
D = 2048
ND = 16
HD = 128
SEQ = 8192
NKC = 64
INTER = 5632
NIC = 44
NE, RK = 8, 64
ISQ = 1.0 / math.sqrt(128.0)


class Ctr:
    def __init__(self, ctx, nc, name):
        self.sem = ctx.enter_context(nc.semaphore(name))
        self.v = 0

    def inc(self, inst, k=1):
        inst.then_inc(self.sem, k)
        self.v += k
        return self.v


DEV_DUMP_KV = False


def build_program(stop_after=None, tap_x=()):
    nc = bass.Bass("TRN2", target_bir_lowering=False)
    din = {}

    def dI(name, shape, dtype=F32):
        din[name] = nc.dram_tensor(name, shape, dtype, kind="ExternalInput").ap()
        return din[name]

    x_in = dI("x", [T, D])
    wq0 = dI("wq0", [D, 16 * HD])
    wk0 = dI("wk0", [4 * HD, D])
    wv0 = dI("wv0", [D, 4 * HD])
    wo0 = dI("wo0", [16 * HD, D])
    gwT = dI("gwT", [INTER, D], BF16)
    uwT = dI("uwT", [INTER, D], BF16)
    dwT = dI("dwT", [INTER, D], BF16)
    gs_c = dI("gs_c", [128, NIC])
    us_c = dI("us_c", [128, NIC])
    ds_b = dI("ds_b", [128, D])
    wq1 = dI("wq1", [D, D])
    wk1 = dI("wk1", [D, D])
    wv1 = dI("wv1", [D, D])
    wo1 = dI("wo1", [D, D])
    a_T = dI("a_T", [D, NE * RK])
    b_sT = dI("b_sT", [NE * RK, D])
    wr_n = dI("wr_n", [NE, D])
    br_r = dI("br_r", [1, NE])
    ident_in = dI("ident", [128, 128])
    shift_in = dI("shiftm", [128, 128])
    cosf_in = dI("cosf", [128, T])
    sinf_in = dI("sinf", [128, T])
    qzb_in = dI("qzb", [128, T])
    idxk_in = dI("idxk", [128, NKC], mybir.dt.uint32)
    qloc_in = dI("qloc", [128, T])
    gkl_in = dI("gkl", [128, NKC])
    zw_in = dI("zw", [128, 2])
    killt_in = dI("killt", [128, NKC])

    y_out = nc.dram_tensor("y", [T, D], F32, kind="ExternalOutput").ap()
    taps = {}
    for tname in tap_x:
        taps[tname] = nc.dram_tensor("tap_" + tname, [T, D], F32,
                                     kind="ExternalOutput").ap()

    kin = [nc.dram_tensor("kin0", [128, 4 * T], BF16),
           nc.dram_tensor("kin1", [T, 16 * HD], BF16)]
    vin = [nc.dram_tensor("vin0", [T, 4 * HD], BF16),
           nc.dram_tensor("vin1", [T, 16 * HD], BF16)]
    kg = [nc.dram_tensor("kg0", [NCORES * 128, 4 * T], BF16),
          nc.dram_tensor("kg1", [SEQ, 16 * HD], BF16)]
    vg = [nc.dram_tensor("vg0", [SEQ, 4 * HD], BF16),
          nc.dram_tensor("vg1", [SEQ, 16 * HD], BF16)]
    svl_d = nc.dram_tensor("svl_d", [128, 16], F32)
    svg_d = nc.dram_tensor("svg_d", [128, 16], F32)
    kml = [nc.dram_tensor("kml0", [1, 4], F32),
           nc.dram_tensor("kml1", [1, 16], F32)]
    kmg = [nc.dram_tensor("kmg0", [1, 4], F32),
           nc.dram_tensor("kmg1", [1, 16], F32)]

    ctx = ExitStack()
    with ctx:
        E = ctx.enter_context

        X = E(nc.sbuf_tensor("X", [128, NT * D], F32))
        xnT = E(nc.sbuf_tensor("xnT", [128, ND * T], BF16))
        ident = E(nc.sbuf_tensor("ident_s", [128, 128], F32))
        ones_col_bf = E(nc.sbuf_tensor("ones_col_bf", [128, 1], BF16))
        ones_row_f = E(nc.sbuf_tensor("ones_row_f", [1, 128], F32))
        eps_col = E(nc.sbuf_tensor("eps_col", [128, 1], F32))
        negc = [E(nc.sbuf_tensor("negc0", [128, 16], F32)),
                E(nc.sbuf_tensor("negc1", [128, 16], F32))]
        nrow = E(nc.sbuf_tensor("nrow", [1, 1024], F32))
        scr = [E(nc.sbuf_tensor(f"scr{i}", [128, D], F32)) for i in range(3)]
        psA = E(nc.psum_tensor("psA", [128, 1024], F32))
        psB = E(nc.psum_tensor("psB", [128, 1024], F32))
        psC = E(nc.psum_tensor("psC", [128, 1024], F32))
        psD = E(nc.psum_tensor("psD", [128, 1024], F32))
        lay = {}
        _anon = [0]

        import os
        import concourse.bass_primitives as _bp

        def memdbg(tag):
            if os.environ.get("KMEMDBG"):
                print("MEM", tag, nc.bytes_allocated(_bp.MemorySpace.SBUF))

        def sb(shape, dtype):
            _anon[0] += 1
            return nc.sbuf_tensor(f"tt{_anon[0]}", shape, dtype)

        s_ld = Ctr(ctx, nc, "s_ld")
        s_st = Ctr(ctx, nc, "s_st")
        s_ldp = Ctr(ctx, nc, "s_ldp")
        s_cc = [Ctr(ctx, nc, f"s_cc{i}") for i in range(6)]
        cA = Ctr(ctx, nc, "cA")
        cB = Ctr(ctx, nc, "cB")
        cC = Ctr(ctx, nc, "cC")
        cD = Ctr(ctx, nc, "cD")
        cE = Ctr(ctx, nc, "cE")
        cF = Ctr(ctx, nc, "cF")
        cG = Ctr(ctx, nc, "cG")
        cH = Ctr(ctx, nc, "cH")
        cI = Ctr(ctx, nc, "cI")
        cJ = Ctr(ctx, nc, "cJ")
        s_kv = Ctr(ctx, nc, "s_kv")
        s_ss = Ctr(ctx, nc, "s_ss")
        s_se = Ctr(ctx, nc, "s_se")
        s_sm = Ctr(ctx, nc, "s_sm")
        s_sa = Ctr(ctx, nc, "s_sa")
        s_hd1 = Ctr(ctx, nc, "s_hd1")
        s_hd2 = Ctr(ctx, nc, "s_hd2")
        s_hd3 = Ctr(ctx, nc, "s_hd3")
        s_hpe = Ctr(ctx, nc, "s_hpe")

        with nc.Block("init") as blk:
            @blk.sync
            def _(sync):
                s_ld.inc(sync.dma_start(
                    out=X[:, :].rearrange("p (t d) -> p t d", d=D),
                    in_=x_in.rearrange("(t p) d -> p t d", p=128)), 16)
                s_ld.inc(sync.dma_start(out=ident[:, :], in_=ident_in), 16)

            @blk.vector
            def _(vector):
                vector.memset(ones_col_bf[:, :], 1.0)
                vector.memset(eps_col[:, :], 1e-5)
                vector.memset(ones_row_f[:, :], 1.0)

        def phase_ln(tag):
            with ExitStack() as pctx, nc.Block("ln_" + tag) as blk:
                P = pctx.enter_context
                nst = P(sb([128, 6 * NT], F32))
                sqd = P(sb([128, D], BF16))

                def sl(j, k):
                    return nst[:, 6 * j + k:6 * j + k + 1]

                Ba = cA.v
                Ba2 = cB.v
                Bd = cC.v
                Bd2 = cD.v
                Bp = cE.v
                Bev = cF.v

                @blk.scalar
                def _(scalar):
                    scalar.wait_ge(s_ld.sem, 32)
                    for j in range(NT):
                        cA.inc(scalar.activation(
                            sqd[:, :], X[:, j * D:(j + 1) * D], AF.Square,
                            accum_out=sl(j, 1)))
                    for j in range(NT):
                        scalar.wait_ge(cC.sem, Bd + j + 1)
                        cB.inc(scalar.activation(
                            sl(j, 4), sl(j, 3), AF.Sqrt, bias=eps_col[:, :]))

                @blk.vector
                def _(vector):
                    vector.wait_ge(s_ld.sem, 32)
                    for j in range(NT):
                        xj = X[:, j * D:(j + 1) * D]
                        h = cH.inc(vector.reduce_sum(sl(j, 0), xj, axis=AX,
                                                     negate=True))
                        vector.wait_ge(cH.sem, h)
                        h = cH.inc(vector.tensor_scalar(
                            sl(j, 0), sl(j, 0), 1.0 / D, None, op0=ALU.mult))
                        vector.wait_ge(cH.sem, h)
                        h = cH.inc(vector.tensor_tensor(
                            sl(j, 2), sl(j, 0), sl(j, 0), ALU.mult))
                        vector.wait_ge(cH.sem, h)
                        vector.wait_ge(cA.sem, Ba + j + 1)
                        cC.inc(vector.scalar_tensor_tensor(
                            sl(j, 3), sl(j, 1), 1.0 / D, sl(j, 2),
                            op0=ALU.mult, op1=ALU.subtract))
                    xv = xnT[:, :].rearrange("p (d t) -> p d t", t=T)

                    def evac(j):
                        vector.wait_ge(cE.sem, Bp + 2 * j + 1)
                        cF.inc(vector.tensor_copy(
                            xv[:, 0:8, j * 128:(j + 1) * 128],
                            psC[:, :].rearrange("p (a b) -> p a b", b=128)))
                        vector.wait_ge(cE.sem, Bp + 2 * j + 2)
                        cF.inc(vector.tensor_copy(
                            xv[:, 8:16, j * 128:(j + 1) * 128],
                            psD[:, :].rearrange("p (a b) -> p a b", b=128)))

                    for j in range(NT):
                        xj = X[:, j * D:(j + 1) * D]
                        vector.wait_ge(cB.sem, Ba2 + j + 1)
                        h = cH.inc(vector.reciprocal(sl(j, 5), sl(j, 4)))
                        vector.wait_ge(cH.sem, h)
                        if j >= 2:
                            vector.wait_ge(cE.sem, Bp + 2 * (j - 2) + 2)
                        cD.inc(vector.tensor_scalar(
                            scr[j % 2][:, :], xj, sl(j, 0), sl(j, 5),
                            op0=ALU.add, op1=ALU.mult))
                        if j >= 1:
                            evac(j - 1)
                    evac(NT - 1)

                @blk.tensor
                def _(tensor):
                    for j in range(NT):
                        tensor.wait_ge(cD.sem, Bd2 + j + 1)
                        if j >= 1:
                            tensor.wait_ge(cF.sem, Bev + 2 * j)
                        for d in range(8):
                            mm = tensor.transpose(
                                psC[:, d * 128:(d + 1) * 128],
                                scr[j % 2][:, d * 128:(d + 1) * 128],
                                ident[:, :])
                        cE.inc(mm)
                        for d in range(8, 16):
                            mm = tensor.transpose(
                                psD[:, (d - 8) * 128:(d - 7) * 128],
                                scr[j % 2][:, d * 128:(d + 1) * 128],
                                ident[:, :])
                        cE.inc(mm)

        def phase_proj(l, w, NP, kind):
            rope = (l == 0)
            qz = (l == 1 and kind == "q")
            memdbg(f"proj_{kind}_l{l}")
            with ExitStack() as pctx, nc.Block(f"proj_{kind}_l{l}") as blk:
                P = pctx.enter_context
                wbf = [P(sb([128, D], BF16)) for _ in range(3)]
                kts = [P(sb([128, T], BF16)) for _ in range(2)]
                sqb = P(sb([128, T], BF16))
                qf32 = [scr[0][:, 0:1024], scr[1][:, 0:1024]]
                ps0 = [psA, psC]
                ps1 = [psB, psD]
                consts = []
                if kind == "k":
                    if rope:
                        consts = [(cosf_in, lay["cosf"]), (sinf_in, lay["sinf"]),
                                  (shift_in, lay["shiftm"])]
                    if l == 1:
                        consts = [(qzb_in, lay["qzb"]), (idxk_in, lay["idxk"]),
                                  (qloc_in, lay["qloc"]), (gkl_in, lay["gkl"]),
                                  (zw_in, lay["zw"]),
                                  (killt_in, lay["killt"])]
                if rope:
                    cosf, sinf, shiftm = (lay["cosf"], lay["sinf"],
                                          lay["shiftm"])
                if qz:
                    qzb = lay["qzb"]
                Bl = s_ld.v
                Bldp = s_ldp.v
                Bp1 = cB.v
                Bqf = cC.v
                Broll = cD.v
                Bdst = cE.v
                Bsq = cF.v
                Bnm = cG.v
                Bnr = cI.v
                Bkd = s_st.v
                NCST = len(consts)
                if kind == "k":
                    lay["c_ld"] = s_ld.v + 16 * NCST

                @blk.sync
                def _(sync):
                    for csrc, cdst in consts:
                        s_ld.inc(sync.dma_start(out=cdst[:, :], in_=csrc), 16)
                    if kind == "k":
                        for i in range(NP):
                            sync.wait_ge(cE.sem, Bdst + i + 1)
                            if l == 0:
                                s_st.inc(sync.dma_start(
                                    out=kin[l].ap()[:, i * T:(i + 1) * T],
                                    in_=kts[i % 2][:, :]), 16)
                            else:
                                s_st.inc(sync.dma_start(
                                    out=kin[l].ap().rearrange(
                                        "(j p) (h f) -> p h j f", p=128,
                                        f=128)[:, i],
                                    in_=kts[i % 2][:, :].rearrange(
                                        "p (j f) -> p j f", f=128)), 16)

                @blk.gpsimd
                def _(gpsimd):
                    for i in range(NP):
                        if i >= 3:
                            gpsimd.wait_ge(cB.sem, Bp1 + i - 2)
                        s_ldp.inc(gpsimd.dma_start(
                            out=wbf[i % 3][:, :],
                            in_=w.rearrange("(fc p) k -> fc p k", p=128)[i]), 16)
                    if kind == "q":
                        gpsimd.wait_ge(s_cc[3 * l + 2].sem, 1)
                        s_ldp.inc(gpsimd.dma_start(
                            out=nrow[0:1, 48:48 + (16 if l else 4)],
                            in_=kmg[l].ap()), 16)
                        if l == 1:
                            gpsimd.wait_ge(s_cc[5].sem, 1)
                            s_ldp.inc(gpsimd.dma_start(
                                out=lay["svg"][:, :], in_=svg_d.ap()), 16)

                @blk.tensor
                def _(tensor):
                    if rope or qz:
                        tensor.wait_ge(s_ld.sem, lay.get("c_ld", 0))
                    for i in range(NP):
                        p0, p1 = ps0[i % 2], ps1[i % 2]
                        tensor.wait_ge(s_ldp.sem, Bldp + 16 * (i + 1))
                        if i >= 2:
                            if rope:
                                tensor.wait_ge(cC.sem, Bqf + i - 1)
                            else:
                                tensor.wait_ge(cE.sem, Bdst + i - 1)
                        for d in range(ND):
                            for hf in range(2):
                                mm = tensor.matmul(
                                    p0[:, hf * 512:(hf + 1) * 512],
                                    wbf[i % 3][:, d * 128:(d + 1) * 128],
                                    xnT[:, d * T + hf * 512:d * T + (hf + 1) * 512],
                                    start=(d == 0), stop=(d == ND - 1),
                                    skip_group_check=True)
                        cB.inc(mm)
                        if rope:
                            tensor.wait_ge(cC.sem, Bqf + i + 1)
                            if i >= 2:
                                tensor.wait_ge(cI.sem, Bnr + i - 1)
                            for hf in range(2):
                                mm = tensor.matmul(
                                    p1[:, hf * 512:(hf + 1) * 512],
                                    shiftm[:, :],
                                    qf32[i % 2][:, hf * 512:(hf + 1) * 512],
                                    start=True, stop=True,
                                    skip_group_check=True)
                            cD.inc(mm)
                        tensor.wait_ge(cF.sem, Bsq + i + 1)
                        if not rope and i >= 2:
                            tensor.wait_ge(cI.sem, Bnr + i - 1)
                        for hf in range(2):
                            mm = tensor.matmul(
                                p1[0:1, hf * 512:(hf + 1) * 512],
                                ones_col_bf[:, :],
                                sqb[:, hf * 512:(hf + 1) * 512],
                                start=True, stop=True, skip_group_check=True)
                        cG.inc(mm)
                    if kind == "q":
                        tensor.wait_ge(cC.sem,
                                       Bqf + (NP if rope else 0) + 1)
                        cB.inc(tensor.matmul(psA[:, 0:16], ones_row_f[:, :],
                                             nrow[0:1, 96:112], start=True,
                                             stop=True, skip_group_check=True))

                @blk.vector
                def _(vector):
                    if rope or qz:
                        vector.wait_ge(s_ld.sem, lay.get("c_ld", 0))
                    for i in range(NP):
                        p0, p1 = ps0[i % 2], ps1[i % 2]
                        dest = (QT[:, i * T:(i + 1) * T] if kind == "q"
                                else kts[i % 2][:, :])
                        vector.wait_ge(cB.sem, Bp1 + i + 1)
                        if kind == "k" and i >= 2:
                            vector.wait_ge(s_st.sem, Bkd + 16 * (i - 1))
                        if i >= 1:
                            vector.wait_ge(cG.sem, Bnm + i)
                        if rope:
                            h = cC.inc(vector.tensor_copy(qf32[i % 2][:, :],
                                                          p0[:, :]))
                            vector.wait_ge(cD.sem, Broll + i + 1)
                            vector.wait_ge(cC.sem, h)
                            h = cH.inc(vector.tensor_tensor(
                                scr[2][:, 0:1024], qf32[i % 2][:, :],
                                cosf[:, :], ALU.mult))
                            vector.wait_ge(cH.sem, h)
                            h = cH.inc(vector.tensor_tensor(
                                scr[2][:, 1024:2048], p1[:, :], sinf[:, :],
                                ALU.mult))
                            vector.wait_ge(cH.sem, h)
                            cE.inc(vector.tensor_tensor(
                                dest, scr[2][:, 0:1024], scr[2][:, 1024:2048],
                                ALU.add))
                        else:
                            if qz:
                                cE.inc(vector.tensor_tensor(
                                    dest, p0[:, :], qzb[:, :], ALU.mult))
                            else:
                                cE.inc(vector.tensor_copy(dest, p0[:, :]))
                        vector.wait_ge(cE.sem, Bdst + i + 1)
                        cF.inc(vector.tensor_tensor(sqb[:, :], dest, dest,
                                                    ALU.mult))
                        vector.wait_ge(cG.sem, Bnm + i + 1)
                        slot = (nrow[0:1, i:i + 1] if kind == "q"
                                else nrow[0:1, 32 + i:32 + i + 1])
                        cI.inc(vector.reduce_max(slot, p1[0:1, :], axis=AX))
                    if kind == "q":
                        HKV = 16 if l == 1 else 4
                        vector.wait_ge(s_ldp.sem, s_ldp.v)
                        vector.wait_ge(cI.sem, Bnr + NP)
                        if HKV == 16:
                            h = cH.inc(vector.tensor_tensor(
                                nrow[0:1, 64:80], nrow[0:1, 0:16],
                                nrow[0:1, 48:64], ALU.mult))
                        else:
                            h = cH.inc(vector.tensor_tensor(
                                nrow[0:1, 64:80].rearrange(
                                    "p (a b) -> p a b", b=4),
                                nrow[0:1, 0:16].rearrange(
                                    "p (a b) -> p a b", b=4),
                                nrow[0:1, 48:52].rearrange(
                                    "p (a b) -> p a b", b=1)
                                .broadcast_to([1, 4, 4]), ALU.mult))
                        vector.wait_ge(cH.sem, h)
                        vector.wait_ge(cA.sem, cA.v + 1)
                        cC.inc(vector.tensor_scalar(
                            nrow[0:1, 96:112], nrow[0:1, 80:96], -1.0, None,
                            op0=ALU.mult))
                        vector.wait_ge(cB.sem, cB.v)
                        cD.inc(vector.tensor_copy(negc[l][:, :], psA[:, 0:16]))

                @blk.scalar
                def _(scalar):
                    if kind == "q":
                        scalar.wait_ge(cH.sem, cH.v)
                        cA.inc(scalar.activation(nrow[0:1, 80:96],
                                                 nrow[0:1, 64:80], AF.Sqrt,
                                                 scale=1.0 / 128.0))

        def phase_v_cached(l, wv, HKV, knrow_hkv):
            FW = HKV * HD
            memdbg(f"v_l{l}")
            with ExitStack() as pctx, nc.Block(f"v_l{l}") as blk:
                P = pctx.enter_context
                wvbf = P(sb([128, ND * FW], BF16))
                vstg = [P(sb([128, FW], BF16)) for _ in range(2)]
                Bld = s_ldp.v
                Bmm = cB.v
                Bev = cC.v
                Bst = s_st.v

                @blk.gpsimd
                def _(gpsimd):
                    for d in range(ND):
                        s_ldp.inc(gpsimd.dma_start(
                            out=wvbf[:, d * FW:(d + 1) * FW],
                            in_=wv[d * 128:(d + 1) * 128, :]), 16)
                    gpsimd.wait_ge(s_st.sem, Bst + 16 * NT)
                    s_cc[3 * l + 0].inc(gpsimd.collective_compute(
                        "AllGather", ALU.bypass,
                        replica_groups=[list(range(NCORES))],
                        ins=[kin[l].ap().opt()], outs=[kg[l].ap().opt()]))
                    s_cc[3 * l + 1].inc(gpsimd.collective_compute(
                        "AllGather", ALU.bypass,
                        replica_groups=[list(range(NCORES))],
                        ins=[vin[l].ap().opt()], outs=[vg[l].ap().opt()]))
                    s_ldp.inc(gpsimd.dma_start(
                        out=kml[l].ap(), in_=nrow[0:1, 32:32 + knrow_hkv]), 16)
                    gpsimd.wait_ge(s_ldp.sem, s_ldp.v)
                    s_cc[3 * l + 2].inc(gpsimd.collective_compute(
                        "AllReduce", ALU.max,
                        replica_groups=[list(range(NCORES))],
                        ins=[kml[l].ap().opt()], outs=[kmg[l].ap().opt()]))

                @blk.tensor
                def _(tensor):
                    tensor.wait_ge(s_ldp.sem, Bld + 16 * ND)
                    for t in range(NT):
                        p0 = [psA, psB][t % 2]
                        if t >= 2:
                            tensor.wait_ge(cC.sem, Bev + t - 1)
                        for d in range(ND):
                            mm = tensor.matmul(
                                p0[:, 0:FW],
                                xnT[:, d * T + t * 128:d * T + (t + 1) * 128],
                                wvbf[:, d * FW:(d + 1) * FW],
                                start=(d == 0), stop=(d == ND - 1),
                                skip_group_check=True)
                        cB.inc(mm)

                @blk.vector
                def _(vector):
                    for t in range(NT):
                        p0 = [psA, psB][t % 2]
                        vector.wait_ge(cB.sem, Bmm + t + 1)
                        if t >= 2:
                            vector.wait_ge(s_st.sem, Bst + 16 * (t - 1))
                        cC.inc(vector.tensor_copy(vstg[t % 2][:, :],
                                                  p0[:, 0:FW]))

                @blk.sync
                def _(sync):
                    for t in range(NT):
                        sync.wait_ge(cC.sem, Bev + t + 1)
                        s_st.inc(sync.dma_start(
                            out=vin[l].ap()[t * 128:(t + 1) * 128, :],
                            in_=vstg[t % 2][:, :]), 16)

        def phase_v_fmajor(l, wv, knrow_hkv):
            memdbg(f"v_l{l}")
            with ExitStack() as pctx, nc.Block(f"v_l{l}") as blk:
                P = pctx.enter_context
                wbf = [P(sb([128, D], BF16)) for _ in range(3)]
                vstg = [P(sb([128, NT * 128], BF16)) for _ in range(2)]
                svl = P(sb([128, 16], F32))
                Bsv = cI.v
                Bld = s_ldp.v
                Bp1 = cB.v
                Bf = cC.v
                Btr = cD.v
                Bbe = cE.v
                Bst = s_st.v

                @blk.gpsimd
                def _(gpsimd):
                    for fc in range(16):
                        if fc >= 3:
                            gpsimd.wait_ge(cB.sem, Bp1 + fc - 2)
                        s_ldp.inc(gpsimd.dma_start(
                            out=wbf[fc % 3][:, :],
                            in_=wv.rearrange("(fc p) k -> fc p k", p=128)[fc]),
                            16)
                    gpsimd.wait_ge(s_st.sem, Bst + 16 * 16)
                    s_cc[3 * l + 0].inc(gpsimd.collective_compute(
                        "AllGather", ALU.bypass,
                        replica_groups=[list(range(NCORES))],
                        ins=[kin[l].ap().opt()], outs=[kg[l].ap().opt()]))
                    s_cc[3 * l + 1].inc(gpsimd.collective_compute(
                        "AllGather", ALU.bypass,
                        replica_groups=[list(range(NCORES))],
                        ins=[vin[l].ap().opt()], outs=[vg[l].ap().opt()]))
                    s_ldp.inc(gpsimd.dma_start(
                        out=kml[l].ap(), in_=nrow[0:1, 32:32 + knrow_hkv]), 16)
                    gpsimd.wait_ge(cI.sem, Bsv + 16)
                    s_ldp.inc(gpsimd.dma_start(out=svl_d.ap(),
                                               in_=svl[:, :]), 16)
                    gpsimd.wait_ge(s_ldp.sem, s_ldp.v)
                    s_cc[3 * l + 2].inc(gpsimd.collective_compute(
                        "AllReduce", ALU.max,
                        replica_groups=[list(range(NCORES))],
                        ins=[kml[l].ap().opt()], outs=[kmg[l].ap().opt()]))
                    s_cc[5].inc(gpsimd.collective_compute(
                        "AllReduce", ALU.add,
                        replica_groups=[list(range(NCORES))],
                        ins=[svl_d.ap().opt()], outs=[svg_d.ap().opt()]))

                @blk.tensor
                def _(tensor):
                    for fc in range(16):
                        p0 = [psA, psC][fc % 2]
                        p1 = [psB, psD][fc % 2]
                        tensor.wait_ge(s_ldp.sem, Bld + 16 * (fc + 1))
                        if fc >= 2:
                            tensor.wait_ge(cC.sem, Bf + fc - 1)
                        for d in range(ND):
                            for hf in range(2):
                                mm = tensor.matmul(
                                    p0[:, hf * 512:(hf + 1) * 512],
                                    wbf[fc % 3][:, d * 128:(d + 1) * 128],
                                    xnT[:, d * T + hf * 512:d * T + (hf + 1) * 512],
                                    start=(d == 0), stop=(d == ND - 1),
                                    skip_group_check=True)
                        cB.inc(mm)
                        tensor.wait_ge(cC.sem, Bf + fc + 1)
                        if fc >= 2:
                            tensor.wait_ge(cE.sem, Bbe + fc - 1)
                        for t in range(NT):
                            mm = tensor.transpose(
                                p1[:, t * 128:(t + 1) * 128],
                                scr[1 + fc % 2][:, t * 128:(t + 1) * 128],
                                ident[:, :])
                        cD.inc(mm)

                @blk.vector
                def _(vector):
                    for fc in range(16):
                        p0 = [psA, psC][fc % 2]
                        p1 = [psB, psD][fc % 2]
                        vector.wait_ge(cB.sem, Bp1 + fc + 1)
                        cI.inc(vector.reduce_sum(svl[:, fc:fc + 1], p0[:, :],
                                                 axis=AX))
                        if fc >= 2:
                            vector.wait_ge(cD.sem, Btr + fc - 1)
                        h = cC.inc(vector.tensor_copy(scr[1 + fc % 2][:, 0:1024],
                                                      p0[:, :]))
                        vector.wait_ge(cC.sem, h)
                        vector.wait_ge(cD.sem, Btr + fc + 1)
                        if fc >= 2:
                            vector.wait_ge(s_st.sem, Bst + 16 * (fc - 1))
                        cE.inc(vector.tensor_copy(vstg[fc % 2][:, :], p1[:, :]))

                @blk.sync
                def _(sync):
                    for fc in range(16):
                        sync.wait_ge(cE.sem, Bbe + fc + 1)
                        s_st.inc(sync.dma_start(
                            out=vin[l].ap().rearrange("(t p) f -> p t f", p=128)
                            [:, :, fc * 128:(fc + 1) * 128],
                            in_=vstg[fc % 2][:, :].rearrange(
                                "p (t f) -> p t f", f=128)), 16)

        def phase_flash(l, H, HKV, mask):
            C = NKC
            memdbg(f"flash_l{l}")
            with ExitStack() as pctx, nc.Block(f"flash_l{l}") as blk:
                P = pctx.enter_context
                NKV = 6
                NPB = 5 if mask else 6
                kbuf = [P(sb([128, 128], BF16)) for _ in range(NKV)]
                vbuf = [P(sb([128, 129], BF16)) for _ in range(NKV)]
                pbuf = [P(sb([128, T], BF16)) for _ in range(NPB)]
                rec8 = P(sb([128, 8], F32))
                dmy = nrow[0:1, 256:264]
                bm = scr[1][:, 0:1024]
                Bkv = s_kv.v
                Bss = s_ss.v
                Bse = s_se.v
                Bsm = s_sm.v
                Bsa = s_sa.v
                B1 = s_hd1.v
                B3 = s_hd3.v
                Bpe = s_hpe.v
                BJ = cJ.v
                kgl = kg[l].ap()
                vgl = vg[l].ap()
                psS = [psA, psB]

                def kv_idx(qh, c):
                    if mask:
                        return (kgl[c * 128:(c + 1) * 128,
                                    qh * 128:(qh + 1) * 128],
                                vgl[c * 128:(c + 1) * 128,
                                    qh * 128:(qh + 1) * 128])
                    kvh = qh // (H // HKV)
                    return (kgl[(c // 8) * 128:(c // 8 + 1) * 128,
                                kvh * T + (c % 8) * 128:
                                kvh * T + (c % 8 + 1) * 128],
                            vgl[c * 128:(c + 1) * 128,
                                kvh * 128:(kvh + 1) * 128])

                def psO(qc):
                    p = psC if qc < 4 else psD
                    return p[:, (qc % 4) * 256:(qc % 4) * 256 + 129]

                @blk.sync
                def _(sync):
                    sync.wait_ge(s_cc[3 * l + 0].sem, 1)
                    sync.wait_ge(s_cc[3 * l + 1].sem, 1)
                    for g in range(H * C):
                        qh, c = divmod(g, C)
                        if g >= NKV:
                            sync.wait_ge(s_sa.sem, Bsa + g - NKV + 1)
                        ksrc, vsrc = kv_idx(qh, c)
                        s_kv.inc(sync.dma_start(out=kbuf[g % NKV][:, :],
                                                in_=ksrc), 16)
                        s_kv.inc(sync.dma_start(out=vbuf[g % NKV][:, 0:128],
                                                in_=vsrc), 16)

                @blk.vector
                def _(vector):
                    if mask:
                        vector.wait_ge(s_ld.sem, lay.get("c_ld", 0))
                    for b in range(NKV):
                        inst = vector.memset(vbuf[b][:, 128:129], 1.0)
                    cJ.inc(inst)

                    def dve_headend(qh):
                        vector.wait_ge(s_sa.sem, Bsa + (qh + 1) * C)
                        h = cH.inc(vector.reciprocal(
                            rec8[:, 0:4].rearrange("p (a b) -> p a b", b=1),
                            psC[:, :].rearrange("p (a b) -> p a b", b=256)
                            [:, :, 128:129]))
                        h = cH.inc(vector.reciprocal(
                            rec8[:, 4:8].rearrange("p (a b) -> p a b", b=1),
                            psD[:, :].rearrange("p (a b) -> p a b", b=256)
                            [:, :, 128:129]))
                        vector.wait_ge(cH.sem, h)
                        for qc in range(8):
                            p = psC if qc < 4 else psD
                            inst2 = vector.tensor_scalar(
                                scr[0][:, qc * 128:(qc + 1) * 128],
                                p[:, (qc % 4) * 256:(qc % 4) * 256 + 128],
                                rec8[:, qc:qc + 1], None, op0=ALU.mult)
                        s_hd1.inc(inst2)
                        vector.wait_ge(s_hpe.sem, Bpe + qh + 1)
                        h = s_hd3.inc(vector.tensor_copy(
                            OTn[:, qh * T:(qh + 1) * T], psC[:, :]))
                        if mask:
                            oc = OTn[:, qh * T + 1023:qh * T + 1024]
                            vector.wait_ge(s_hd3.sem, h)
                            h2 = cH.inc(vector.tensor_scalar(
                                rec8[:, 0:1], oc, lay["zw"][:, 0:1], None,
                                op0=ALU.mult))
                            vector.wait_ge(cH.sem, h2)
                            h2 = cH.inc(vector.scalar_tensor_tensor(
                                oc, lay["svg"][:, qh:qh + 1],
                                lay["zw"][:, 1:2], rec8[:, 0:1],
                                op0=ALU.mult, op1=ALU.add))
                            vector.wait_ge(cH.sem, h2)

                    pend = None
                    for g in range(H * C):
                        qh, i = divmod(g, C)
                        if mask:
                            vector.wait_ge(s_se.sem, Bse + 2 * g + 3)
                            s_sm.inc(vector.scalar_tensor_tensor(
                                pbuf[g % NPB][:, 0:T],
                                lay["qloc"][:, 0:T],
                                lay["gkl"][:, i:i + 1],
                                pbuf[g % NPB][:, 0:T],
                                op0=ALU.is_lt, op1=ALU.mult))
                        if pend is not None:
                            dve_headend(pend)
                            pend = None
                        if i == C - 1:
                            pend = qh
                    if mask:
                        s_sm.inc(vector.scalar_tensor_tensor(
                            dmy[0:1, 1:2], lay["qloc"][0:1, 0:1],
                            lay["gkl"][0:1, 0:1], dmy[0:1, 1:2],
                            op0=ALU.is_lt, op1=ALU.mult))
                    dve_headend(pend)

                @blk.tensor
                def _(tensor):
                    def emit_S(g):
                        qh, c = divmod(g, C)
                        tensor.wait_ge(s_kv.sem, Bkv + 32 * (g + 1))
                        if g == 0:
                            tensor.wait_ge(cJ.sem, cJ.v)
                        for hf in range(2):
                            if g >= 2:
                                tensor.wait_ge(s_se.sem,
                                               Bse + 2 * (g - 2) + hf + 1)
                            mm = tensor.matmul(
                                psS[g % 2][:, hf * 512:(hf + 1) * 512],
                                kbuf[g % NKV][:, :],
                                QT[:, qh * T + hf * 512:qh * T + (hf + 1) * 512],
                                start=True, stop=True, skip_group_check=True)
                            s_ss.inc(mm)

                    def emit_AV(g):
                        qh, i = divmod(g, C)
                        if mask:
                            tensor.wait_ge(s_sm.sem, Bsm + g + 2)
                        else:
                            tensor.wait_ge(s_se.sem, Bse + 2 * g + 3)
                        if i == 0 and qh >= 1:
                            tensor.wait_ge(s_hd3.sem, B3 + qh)
                        for qc in range(8):
                            mm = tensor.matmul(
                                psO(qc),
                                pbuf[g % NPB][:, qc * 128:(qc + 1) * 128],
                                vbuf[g % NKV][:, :],
                                start=(i == 0 and qc % 2 == 0),
                                stop=(i == C - 1),
                                skip_group_check=True)
                        s_sa.inc(mm)

                    def emit_headend(p):
                        tensor.wait_ge(s_hd1.sem, B1 + p + 1)
                        for qc in range(8):
                            mm = tensor.transpose(
                                psC[:, qc * 128:(qc + 1) * 128],
                                scr[0][:, qc * 128:(qc + 1) * 128],
                                ident[:, :])
                        s_hpe.inc(mm)

                    pend_he = None
                    for g in range(H * C):
                        emit_S(g)
                        if pend_he is not None:
                            emit_headend(pend_he)
                            pend_he = None
                        if g >= 2:
                            emit_AV(g - 2)
                            if (g - 2) % C == C - 1:
                                pend_he = (g - 2) // C
                    emit_AV(H * C - 2)
                    emit_AV(H * C - 1)
                    emit_headend(H - 1)

                @blk.scalar
                def _(scalar):
                    for g in range(H * C):
                        qh, i = divmod(g, C)
                        bias = negc[l][:, qh:qh + 1]
                        for hf in range(2):
                            scalar.wait_ge(s_ss.sem, Bss + 2 * g + hf + 1)
                            if hf == 0 and g >= NPB:
                                scalar.wait_ge(s_sa.sem, Bsa + g - NPB + 1)
                            s_se.inc(scalar.activation(
                                pbuf[g % NPB][:, hf * 512:(hf + 1) * 512],
                                psS[g % 2][:, hf * 512:(hf + 1) * 512],
                                AF.Exp, bias=bias, scale=ISQ))
                    s_se.inc(scalar.activation(
                        dmy[0:1, 0:1], psS[0][0:1, 0:1], AF.Exp,
                        bias=negc[l][0:1, 0:1], scale=ISQ))

        def phase_wo(l, wo):
            memdbg(f"wo_l{l}")
            with ExitStack() as pctx, nc.Block(f"wo_l{l}") as blk:
                P = pctx.enter_context
                wobf = QT
                Bld = s_ldp.v
                Bmm = cB.v
                Bx = cC.v

                @blk.gpsimd
                def _(gpsimd):
                    for grp in range(2):
                        if grp == 1:
                            gpsimd.wait_ge(cB.sem, Bmm + 8)
                        for h8 in range(8):
                            h = grp * 8 + h8
                            s_ldp.inc(gpsimd.dma_start(
                                out=wobf[:, h8 * D:(h8 + 1) * D],
                                in_=wo[h * 128:(h + 1) * 128, :]), 16)

                @blk.tensor
                def _(tensor):
                    for grp in range(2):
                        tensor.wait_ge(s_ldp.sem, Bld + 16 * 8 * (grp + 1))
                        for t in range(NT):
                            pl, ph = [(psA, psB), (psC, psD)][t % 2]
                            if grp * NT + t >= 2:
                                tensor.wait_ge(cC.sem, Bx + grp * NT + t - 1)
                            for h8 in range(8):
                                h = grp * 8 + h8
                                for hf in range(4):
                                    p = pl if hf < 2 else ph
                                    mm = tensor.matmul(
                                        p[:, (hf % 2) * 512:(hf % 2 + 1) * 512],
                                        OTn[:, h * T + t * 128:h * T + (t + 1) * 128],
                                        wobf[:, h8 * D + hf * 512:
                                             h8 * D + (hf + 1) * 512],
                                        start=(h8 == 0), stop=(h8 == 7),
                                        skip_group_check=True)
                            cB.inc(mm)

                @blk.vector
                def _(vector):
                    for grp in range(2):
                        for t in range(NT):
                            pl, ph = [(psA, psB), (psC, psD)][t % 2]
                            vector.wait_ge(cB.sem, Bmm + grp * NT + t + 1)
                            h = cH.inc(vector.tensor_tensor(
                                X[:, t * D:t * D + 1024],
                                X[:, t * D:t * D + 1024], pl[:, :], ALU.add))
                            vector.wait_ge(cH.sem, h)
                            cC.inc(vector.tensor_tensor(
                                X[:, t * D + 1024:(t + 1) * D],
                                X[:, t * D + 1024:(t + 1) * D], ph[:, :],
                                ALU.add))

        def phase_ffn0():
            memdbg("ffn0")
            with ExitStack() as pctx, nc.Block("ffn0") as blk:
                P = pctx.enter_context
                dsb = P(sb([128, D], F32))
                gs_sb = P(sb([128, NIC], F32))
                us_sb = P(sb([128, NIC], F32))
                xadd = scr[2][:, 1024:2048]
                h_sb = P(sb([128, NIC * 512], BF16))
                gbf = [P(sb([128, D], BF16)) for _ in range(2)]
                ubf = [P(sb([128, D], BF16)) for _ in range(2)]
                dbf = [P(sb([128, 1024], BF16))
                       for _ in range(2)]
                silu = [P(sb([128, 512], BF16))
                        for _ in range(2)]
                usb = [P(sb([128, 512], BF16))
                       for _ in range(2)]
                Bl = s_ld.v
                Bc = cA.v
                Bgu = cB.v
                Bsil = cC.v
                Buev = cD.v
                Bh = cE.v
                Bxa = cF.v
                Bdn = cG.v
                Bdc = cI.v
                Bdl = s_ldp.v
                BJ = cJ.v

                @blk.sync
                def _(sync):
                    cJ.inc(sync.dma_start(out=dsb[:, :], in_=ds_b), 16)
                    cJ.inc(sync.dma_start(out=gs_sb[:, :], in_=gs_c), 16)
                    cJ.inc(sync.dma_start(out=us_sb[:, :], in_=us_c), 16)

                @blk.gpsimd
                def _(gpsimd):
                    for hh in range(2):
                        for ic in range(NIC):
                            i = hh * NIC + ic
                            if i >= 2:
                                gpsimd.wait_ge(cB.sem, Bgu + i - 1)
                            s_ld.inc(gpsimd.dma_start(
                                out=gbf[i % 2][:, :],
                                in_=gwT.rearrange("(ic p) k -> ic p k", p=128)
                                [ic]), 16)
                            s_ld.inc(gpsimd.dma_start(
                                out=ubf[i % 2][:, :],
                                in_=uwT.rearrange("(ic p) k -> ic p k", p=128)
                                [ic]), 16)
                        for dmh in range(2):
                            for ic in range(NIC):
                                j = (hh * 2 + dmh) * NIC + ic
                                if j >= 2:
                                    gpsimd.wait_ge(cG.sem, Bdn + j - 1)
                                s_ldp.inc(gpsimd.dma_start(
                                    out=dbf[j % 2][:, :],
                                    in_=dwT[ic * 128:(ic + 1) * 128,
                                            dmh * 1024:(dmh + 1) * 1024]), 16)

                @blk.scalar
                def _(scalar):
                    for hh in range(2):
                        for ic in range(NIC):
                            i = hh * NIC + ic
                            scalar.wait_ge(cB.sem, Bgu + i + 1)
                            if i == 0:
                                scalar.wait_ge(cJ.sem, BJ + 48)
                            psGU = [psA, psB][i % 2]
                            cC.inc(scalar.activation(
                                silu[i % 2][:, :], psGU[:, 0:512], AF.Silu,
                                scale=gs_sb[:, ic:ic + 1]))

                @blk.tensor
                def _(tensor):
                    for hh in range(2):
                        for ic in range(NIC):
                            i = hh * NIC + ic
                            psGU = [psA, psB][i % 2]
                            tensor.wait_ge(s_ld.sem, Bl + 32 * (i + 1))
                            if i >= 2:
                                tensor.wait_ge(cC.sem, Bsil + i - 1)
                                tensor.wait_ge(cD.sem, Buev + i - 1)
                            if ic == 0 and hh == 1:
                                tensor.wait_ge(cF.sem, Bxa + 8)
                            for d in range(ND):
                                rhs = xnT[:, d * T + hh * 512:
                                          d * T + (hh + 1) * 512]
                                tensor.matmul(
                                    psGU[:, 0:512],
                                    gbf[i % 2][:, d * 128:(d + 1) * 128], rhs,
                                    start=(d == 0), stop=(d == ND - 1),
                                    skip_group_check=True)
                                mm = tensor.matmul(
                                    psGU[:, 512:1024],
                                    ubf[i % 2][:, d * 128:(d + 1) * 128], rhs,
                                    start=(d == 0), stop=(d == ND - 1),
                                    skip_group_check=True)
                            cB.inc(mm)
                        tensor.wait_ge(cE.sem, Bh + (hh + 1) * NIC)
                        pstch = [psA, psB, psC, psD]
                        for dmh in range(2):
                            for ic in range(NIC):
                                j = (hh * 2 + dmh) * NIC + ic
                                tensor.wait_ge(s_ldp.sem, Bdl + 16 * (j + 1))
                                if dmh == 0 and ic == 0:
                                    tensor.wait_ge(cC.sem, Bsil + (hh + 1) * NIC)
                                    tensor.wait_ge(cD.sem, Buev + (hh + 1) * NIC)
                                if ic == 0 and (hh * 2 + dmh) >= 1:
                                    tensor.wait_ge(cF.sem, Bxa + (hh * 2 + dmh) * 4)
                                for tch in range(4):
                                    for hf in range(2):
                                        mm = tensor.matmul(
                                            pstch[tch][:, hf * 512:(hf + 1) * 512],
                                            h_sb[:, ic * 512 + tch * 128:
                                                 ic * 512 + (tch + 1) * 128],
                                            dbf[j % 2][:, hf * 512:(hf + 1) * 512],
                                            start=(ic == 0), stop=(ic == NIC - 1),
                                            skip_group_check=True)
                                cG.inc(mm)

                @blk.vector
                def _(vector):
                    vector.wait_ge(cJ.sem, BJ + 48)
                    for hh in range(2):
                        for ic in range(NIC):
                            i = hh * NIC + ic
                            psGU = [psA, psB][i % 2]
                            vector.wait_ge(cB.sem, Bgu + i + 1)
                            h = cD.inc(vector.tensor_scalar(
                                usb[i % 2][:, :], psGU[:, 512:1024],
                                us_sb[:, ic:ic + 1], None, op0=ALU.mult))
                            vector.wait_ge(cD.sem, h)
                            vector.wait_ge(cC.sem, Bsil + i + 1)
                            cE.inc(vector.tensor_tensor(
                                h_sb[:, ic * 512:(ic + 1) * 512],
                                silu[i % 2][:, :], usb[i % 2][:, :], ALU.mult))
                        pstch = [psA, psB, psC, psD]
                        for dmh in range(2):
                            for tch in range(4):
                                vector.wait_ge(cG.sem,
                                               Bdn + (hh * 2 + dmh + 1) * NIC)
                                tg = hh * 4 + tch
                                xsl = X[:, tg * D + dmh * 1024:
                                        tg * D + (dmh + 1) * 1024]
                                h = cH.inc(vector.tensor_tensor(
                                    xadd, pstch[tch][:, :],
                                    dsb[:, dmh * 1024:(dmh + 1) * 1024],
                                    ALU.mult))
                                vector.wait_ge(cH.sem, h)
                                cF.inc(vector.tensor_tensor(
                                    xsl, xsl, xadd, ALU.add))

        def phase_ffn1():
            memdbg("ffn1")
            with ExitStack() as pctx, nc.Block("ffn1") as blk:
                P = pctx.enter_context
                abf = P(sb([128, ND * 512], BF16))
                bbf = P(sb([128, 4 * D], BF16))
                wrT = P(sb([128, ND * 8], BF16))
                wrst = P(sb([8, D], F32))
                brsb = P(sb([1, 8], F32))
                p8 = P(sb([128, 8 * 8], F32))
                ow = P(sb([128, 8 * 8], F32))
                sst = P(sb([128, 4 * 8], F32))
                hmf = [P(sb([128, 512], F32)) for _ in range(2)]
                hTb = [P(sb([128, 512], BF16)) for _ in range(2)]
                Bl = s_ld.v
                Bldp = s_ldp.v
                Bc = cA.v
                Bex = cI.v
                Bpe = cB.v
                Btr = cG.v
                BwT = cD.v
                Bsx = cC.v
                Bmk = cF.v
                Bhev = cE.v
                BxH = cH.v

                @blk.gpsimd
                def _(gpsimd):
                    for d in range(ND):
                        s_ldp.inc(gpsimd.dma_start(
                            out=abf[:, d * 512:(d + 1) * 512],
                            in_=a_T[d * 128:(d + 1) * 128, :]), 16)

                @blk.sync
                def _(sync):
                    for b in range(4):
                        if b >= 2:
                            sync.wait_ge(cA.sem, Bc + b - 1)
                        s_ld.inc(sync.dma_start(
                            out=scr[b % 2][:, :],
                            in_=b_sT[b * 128:(b + 1) * 128, :]), 16)
                    s_ld.inc(sync.dma_start(out=wrst[:, :], in_=wr_n), 16)
                    s_ld.inc(sync.dma_start(out=brsb[:, :], in_=br_r), 16)

                @blk.scalar
                def _(scalar):
                    for b in range(4):
                        scalar.wait_ge(s_ld.sem, Bl + 16 * (b + 1))
                        cA.inc(scalar.mul(bbf[:, b * D:(b + 1) * D],
                                          scr[b % 2][:, :], 1.0 / RK))
                    for t in range(NT):
                        scalar.wait_ge(cC.sem, Bsx + 5 * t + 1)
                        pl_ = [psA, psB, psC, psD][t % 4]
                        cI.inc(scalar.activation(
                            p8[:, t * 8:(t + 1) * 8],
                            pl_[:, (t // 4) * 512:(t // 4) * 512 + 8], AF.Exp,
                            bias=sst[:, 4 * t:4 * t + 1]))

                @blk.tensor
                def _(tensor):
                    tensor.wait_ge(s_ld.sem, Bl + 16 * 6)
                    for d in range(ND):
                        mm = tensor.transpose(
                            psC[:, d * 8:(d + 1) * 8],
                            wrst[:, d * 128:(d + 1) * 128], ident[0:8, 0:8])
                    cB.inc(mm)
                    tensor.wait_ge(cD.sem, BwT + 1)
                    psall = [psA, psB, psC, psD]

                    def lgt(t):
                        return psall[t % 4][:, (t // 4) * 512:
                                            (t // 4) * 512 + 8]
                    for t in range(NT):
                        for d in range(ND):
                            tensor.matmul(
                                lgt(t),
                                xnT[:, d * T + t * 128:d * T + (t + 1) * 128],
                                wrT[:, d * 8:(d + 1) * 8],
                                start=(d == 0), stop=False,
                                skip_group_check=True)
                        mm = tensor.matmul(
                            lgt(t), ones_row_f[:, :],
                            brsb[:, :], start=False, stop=True,
                            skip_group_check=True)
                        cB.inc(mm)
                    tensor.wait_ge(s_ldp.sem, Bldp + 16 * ND)
                    tensor.wait_ge(cA.sem, Bc + 4)
                    tensor.wait_ge(cC.sem, Bsx + 5 * NT)
                    for t in range(NT):
                        if t >= 2:
                            tensor.wait_ge(cF.sem, Bmk + t - 1)
                        for d in range(ND):
                            mm = tensor.matmul(
                                psA[:, (t % 2) * 512:(t % 2 + 1) * 512],
                                xnT[:, d * T + t * 128:d * T + (t + 1) * 128],
                                abf[:, d * 512:(d + 1) * 512],
                                start=(d == 0), stop=(d == ND - 1),
                                skip_group_check=True)
                        cB.inc(mm)
                        tensor.wait_ge(cF.sem, Bmk + t + 1)
                        if t >= 1:
                            tensor.wait_ge(cE.sem, Bhev + t)
                        for b in range(4):
                            mm = tensor.transpose(
                                psB[:, 512 + b * 128:512 + (b + 1) * 128],
                                hmf[t % 2][:, b * 128:(b + 1) * 128],
                                ident[:, :])
                        cG.inc(mm)
                        tensor.wait_ge(cE.sem, Bhev + t + 1)
                        if t >= 1:
                            tensor.wait_ge(cH.sem, BxH + 2 * t)
                        for b in range(4):
                            for hf in range(4):
                                p = psC if hf < 2 else psD
                                mm = tensor.matmul(
                                    p[:, (hf % 2) * 512:(hf % 2 + 1) * 512],
                                    hTb[t % 2][:, b * 128:(b + 1) * 128],
                                    bbf[:, b * D + hf * 512:b * D + (hf + 1) * 512],
                                    start=(b == 0), stop=(b == 3),
                                    skip_group_check=True)
                        cB.inc(mm)

                @blk.vector
                def _(vector):
                    vector.wait_ge(cB.sem, Bpe + 1)
                    cD.inc(vector.tensor_copy(wrT[:, :], psC[:, 0:ND * 8]))
                    psall = [psA, psB, psC, psD]
                    for t in range(NT):
                        lg = psall[t % 4][:, (t // 4) * 512:(t // 4) * 512 + 8]
                        vector.wait_ge(cB.sem, Bpe + 1 + t + 1)
                        h = cC.inc(vector.reduce_max(
                            sst[:, 4 * t:4 * t + 1], lg, axis=AX, negate=True))
                        h = cC.inc(vector.reduce_max(
                            sst[:, 4 * t + 1:4 * t + 2], lg, axis=AX))
                        vector.wait_ge(cC.sem, h)
                        vector.wait_ge(cI.sem, Bex + t + 1)
                        h = cC.inc(vector.reduce_sum(
                            sst[:, 4 * t + 2:4 * t + 3],
                            p8[:, t * 8:(t + 1) * 8], axis=AX))
                        vector.wait_ge(cC.sem, h)
                        h = cC.inc(vector.reciprocal(
                            sst[:, 4 * t + 3:4 * t + 4],
                            sst[:, 4 * t + 2:4 * t + 3]))
                        vector.wait_ge(cC.sem, h)
                        h = cC.inc(vector.tensor_scalar(
                            ow[:, t * 8:(t + 1) * 8], lg,
                            sst[:, 4 * t + 1:4 * t + 2],
                            sst[:, 4 * t + 3:4 * t + 4],
                            op0=ALU.is_ge, op1=ALU.mult))
                        vector.wait_ge(cC.sem, h)
                    for t in range(NT):
                        vector.wait_ge(cB.sem, Bpe + 9 + 2 * t + 1)
                        for e in range(NE):
                            inst = vector.tensor_scalar(
                                hmf[t % 2][:, e * 64:(e + 1) * 64],
                                psA[:, (t % 2) * 512 + e * 64:
                                    (t % 2) * 512 + (e + 1) * 64],
                                ow[:, t * 8 + e:t * 8 + e + 1], None,
                                op0=ALU.mult)
                        cF.inc(inst)
                        vector.wait_ge(cG.sem, Btr + t + 1)
                        if t >= 2:
                            vector.wait_ge(cB.sem, Bpe + 9 + 2 * (t - 1) + 2)
                        cE.inc(vector.tensor_copy(hTb[t % 2][:, :],
                                                  psB[:, 512:1024]))
                        vector.wait_ge(cB.sem, Bpe + 9 + 2 * t + 2)
                        cH.inc(vector.tensor_tensor(
                            X[:, t * D:t * D + 1024],
                            X[:, t * D:t * D + 1024], psC[:, :], ALU.add))
                        cH.inc(vector.tensor_tensor(
                            X[:, t * D + 1024:(t + 1) * D],
                            X[:, t * D + 1024:(t + 1) * D], psD[:, :],
                            ALU.add))

        def dump_x(dest_ap):
            with nc.Block("dump") as blk:
                @blk.sync
                def _(sync):
                    s_st.inc(sync.dma_start(
                        out=dest_ap.rearrange("(t p) d -> p t d", p=128),
                        in_=X[:, :].rearrange("p (t d) -> p t d", d=D)), 16)
                    if DEV_DUMP_KV:
                        kgo = nc.dram_tensor("kg0_tap", [NCORES * 128, 4 * T],
                                             BF16, kind="ExternalOutput").ap()
                        vgo = nc.dram_tensor("vg0_tap", [SEQ, 4 * HD], BF16,
                                             kind="ExternalOutput").ap()
                        nco = nc.dram_tensor("negc_tap", [128, 16], F32,
                                             kind="ExternalOutput").ap()
                        s_st.inc(sync.dma_start(out=kgo, in_=kg[0].ap()), 16)
                        s_st.inc(sync.dma_start(out=vgo, in_=vg[0].ap()), 16)
                        s_st.inc(sync.dma_start(out=nco, in_=negc[0][:, :]), 16)
                    sync.wait_ge(s_st.sem, s_st.v)

        def attn_layer(l):
            global QT, OTn
            with ExitStack() as actx:
                A = actx.enter_context
                lay.clear()
                if l == 1:
                    lay["idxk"] = A(nc.sbuf_tensor("idxk_s", [128, NKC],
                                                   mybir.dt.uint32))
                    lay["qloc"] = A(nc.sbuf_tensor("qloc_s", [128, T], F32))
                    lay["gkl"] = A(nc.sbuf_tensor("gkl_s", [128, NKC], F32))
                    lay["zw"] = A(nc.sbuf_tensor("zw_s", [128, 2], F32))
                    lay["svg"] = A(nc.sbuf_tensor("svg_s", [128, 16], F32))
                    lay["killt"] = A(nc.sbuf_tensor("killt_s", [128, NKC],
                                                    F32))
                QT = A(nc.sbuf_tensor(f"QT{l}", [128, 16 * T], BF16))
                ictx = ExitStack()
                IA = ictx.enter_context
                if l == 0:
                    lay["cosf"] = IA(nc.sbuf_tensor("cosf_s", [128, T], F32))
                    lay["sinf"] = IA(nc.sbuf_tensor("sinf_s", [128, T], F32))
                    lay["shiftm"] = IA(nc.sbuf_tensor("shiftm_s",
                                                      [128, 128], F32))
                else:
                    lay["qzb"] = IA(nc.sbuf_tensor("qzb_s", [128, T], F32))
                if l == 0:
                    phase_proj(0, wk0, 4, "k")
                    phase_v_cached(0, wv0, 4, 4)
                    phase_proj(0, wq0, 16, "q")
                    ictx.close()
                    if DEV_DUMP_KV:
                        qtap = nc.dram_tensor("qt_tap", [128, 16 * T], BF16,
                                              kind="ExternalOutput").ap()
                        with nc.Block("qtdump") as qblk:
                            @qblk.sync
                            def _(sync):
                                s_st.inc(sync.dma_start(out=qtap,
                                                        in_=QT[:, :]), 16)
                                sync.wait_ge(s_st.sem, s_st.v)
                    OTn = A(nc.sbuf_tensor(f"OTn{l}", [128, 16 * T], BF16))
                    phase_flash(0, 16, 4, False)
                    if DEV_DUMP_KV:
                        otap = nc.dram_tensor("otn_tap", [128, 16 * T], BF16,
                                              kind="ExternalOutput").ap()
                        with nc.Block("otdump") as oblk:
                            @oblk.sync
                            def _(sync):
                                s_st.inc(sync.dma_start(out=otap,
                                                        in_=OTn[:, :]), 16)
                                sync.wait_ge(s_st.sem, s_st.v)
                    phase_wo(0, wo0)
                else:
                    phase_proj(1, wk1, 16, "k")
                    phase_v_fmajor(1, wv1, 16)
                    phase_proj(1, wq1, 16, "q")
                    ictx.close()
                    OTn = A(nc.sbuf_tensor(f"OTn{l}", [128, 16 * T], BF16))
                    phase_flash(1, 16, 16, True)
                    phase_wo(1, wo1)

        phase_ln("l0a")
        attn_layer(0)
        if "l0_attn" in taps:
            dump_x(taps["l0_attn"])
        if stop_after == "l0_attn":
            dump_x(y_out)
            return nc, din
        phase_ln("l0f")
        phase_ffn0()
        if "l0_ffn" in taps:
            dump_x(taps["l0_ffn"])
        if stop_after == "l0_ffn":
            dump_x(y_out)
            return nc, din
        phase_ln("l1a")
        attn_layer(1)
        if "l1_attn" in taps:
            dump_x(taps["l1_attn"])
        if stop_after == "l1_attn":
            dump_x(y_out)
            return nc, din
        phase_ln("l1f")
        phase_ffn1()
        dump_x(y_out)
    return nc, din



def _rope_tables(pos):
    inv = 1.0 / (10000.0 ** (np.arange(0, HD, 2, dtype=np.float32) / HD))
    fr = pos[:, None].astype(np.float32) * inv[None, :]
    emb = np.concatenate([fr, fr], axis=-1)
    return np.cos(emb).astype(np.float32), np.sin(emb).astype(np.float32)


def _shift_mat():
    m = np.zeros((128, 128), dtype=np.float32)
    for k in range(128):
        m[k, (k + 1) % 128] = 1.0
    return m


def _tile_w(w, nf):
    K, F = w.shape
    a = w.reshape(K // 128, 128, nf, F // nf)
    a = a.transpose(2, 1, 0, 3)
    return np.ascontiguousarray(a.reshape(F, K))


def _core_inputs(inputs, r):
    x = np.ascontiguousarray(
        inputs["L0_attn_x"][0, r * T:(r + 1) * T, :]).astype(np.float32, copy=False)
    pos = np.arange(r * T, (r + 1) * T)
    cos, sin = _rope_tables(pos)
    cosf = np.ascontiguousarray(cos.T)
    sinf = np.ascontiguousarray(sin.T)
    qz = np.ones((128, T), dtype=np.float32)
    if r == NCORES - 1:
        qz[:, T - 1] = 0.0
    ordr = (np.arange(NKC) + 8 * r) % NKC
    idxk = ((ordr[None, :] * 128 + np.arange(128)[:, None]) * 16
            ).astype(np.uint32)
    killt = np.zeros((128, NKC), dtype=np.float32)
    killt[:, ordr < 8 * r] = -30000.0
    qloc = np.tile(np.arange(T, dtype=np.float32)[None, :], (128, 1))
    if r == NCORES - 1:
        qloc[:, T - 1] = -1.0
    gkl = (np.arange(NKC, dtype=np.float32)[None, :] * 128
           + np.arange(128, dtype=np.float32)[:, None] - r * T)
    zw = np.zeros((128, 2), dtype=np.float32)
    zw[:, 0] = 1.0
    if r == NCORES - 1:
        zw[:, 0] = 0.0
        zw[:, 1] = 1.0 / SEQ
    gs = inputs["L0_ffn_gate_s"].astype(np.float32)
    us = inputs["L0_ffn_up_s"].astype(np.float32)
    ds = inputs["L0_ffn_down_s"].astype(np.float32)
    A = np.asarray(inputs["L1_ffn_A"], dtype=np.float32)
    B = np.asarray(inputs["L1_ffn_B"], dtype=np.float32)
    d = dict(
        x=x,
        wq0=_tile_w(np.asarray(inputs["L0_attn_Wq"], dtype=np.float32), 16),
        wk0=_tile_w(np.asarray(inputs["L0_attn_Wk"], dtype=np.float32), 4),
        wv0=np.asarray(inputs["L0_attn_Wv"], dtype=np.float32),
        wo0=np.asarray(inputs["L0_attn_Wo"], dtype=np.float32),
        gwT=_tile_w(np.asarray(inputs["L0_ffn_gate_w"], dtype=np.float32).T,
                    NIC).astype(ml_dtypes.bfloat16),
        uwT=_tile_w(np.asarray(inputs["L0_ffn_up_w"], dtype=np.float32).T,
                    NIC).astype(ml_dtypes.bfloat16),
        dwT=np.ascontiguousarray(np.asarray(inputs["L0_ffn_down_w"],
                                            dtype=np.float32).T
                                 .astype(ml_dtypes.bfloat16)),
        gs_c=np.ascontiguousarray(gs.reshape(NIC, 128).T),
        us_c=np.ascontiguousarray(us.reshape(NIC, 128).T),
        ds_b=np.ascontiguousarray(np.tile(ds[None, :], (128, 1))),
        wq1=_tile_w(np.asarray(inputs["L1_attn_Wq"], dtype=np.float32), 16),
        wk1=_tile_w(np.asarray(inputs["L1_attn_Wk"], dtype=np.float32), 16),
        wv1=_tile_w(np.asarray(inputs["L1_attn_Wv"], dtype=np.float32), 16),
        wo1=np.asarray(inputs["L1_attn_Wo"], dtype=np.float32),
        a_T=np.ascontiguousarray(A.reshape(NE * RK, D).T),
        b_sT=np.ascontiguousarray(B.transpose(0, 2, 1).reshape(NE * RK, D)),
        wr_n=np.asarray(inputs["L1_ffn_Wr"], dtype=np.float32),
        br_r=np.ascontiguousarray(
            np.asarray(inputs["L1_ffn_br"], dtype=np.float32)[None, :]),
        ident=np.eye(128, dtype=np.float32),
        shiftm=_shift_mat(),
        cosf=cosf, sinf=sinf, qzb=qz,
        idxk=np.ascontiguousarray(idxk),
        killt=np.ascontiguousarray(killt),
        qloc=np.ascontiguousarray(qloc),
        gkl=np.ascontiguousarray(gkl),
        zw=np.ascontiguousarray(zw),
    )
    return d


_CACHE = {}


def kernel(**inputs):
    key = "full"
    if key not in _CACHE:
        _CACHE[key] = build_program()
    nc, _ = _CACHE[key]
    in_maps = [_core_inputs(inputs, r) for r in range(NCORES)]
    res = run_bass_kernel_spmd(nc, in_maps, core_ids=list(range(NCORES)))
    out = np.concatenate([res.results[r]["y"] for r in range(NCORES)], axis=0)
    return out.reshape(1, SEQ, D).astype(np.float32)
